# version 40
# baseline (speedup 1.0000x reference)
import contextlib
import numpy as np
import concourse.bass as bass
import concourse.mybir as mybir
from concourse.bass_utils import run_bass_kernel_spmd

F32 = mybir.dt.float32
BF16 = mybir.dt.bfloat16
AF = mybir.ActivationFunctionType
ALU = mybir.AluOpType
AX = mybir.AxisListType

ENGS = ("tensor", "vector", "scalar", "sync", "gpsimd")
DMA_Q = {"sync": 4, "gpsimd": 4}

SEQ, OWN, DM, STN, HID = 8192, 4096, 1024, 16, 4096
CC_GROUPS = [[0, 1], [2, 3], [4, 5], [6, 7]]
FT, NF = 512, 16
BT, NB = 512, 8
KD = 8
KH = 32


class Plan:

    def __init__(self):
        self.ops = {e: [] for e in ENGS}
        self.lw = {}
        self.rd = {}
        self.xa = {}
        self._bar = {}
        self._bar_left = set()

    def barrier(self):
        snap = []
        for e in ENGS:
            n = len(self.ops[e])
            if n == 0:
                continue
            if e in DMA_Q:
                nq = DMA_Q[e]
                for q in range(min(nq, n)):
                    i = n - 1 - ((n - 1 - q) % nq)
                    i = max(j for j in range(n) if j % nq == q)
                    snap.append((e, i))
            else:
                self.ops[e][-1]["inc"] = True
                snap.append((e, n - 1))
        self._bar = snap
        self._bar_left = set(ENGS)

    def add(self, eng, fn, r=(), w=(), x=(), skip_inc=False):
        ops = self.ops[eng]
        idx = len(ops)
        waits = {}

        def cover(tok, same_ok):
            if tok is None:
                return
            te, ti = tok
            if same_ok and te == eng and te not in DMA_Q:
                return
            q = ti % DMA_Q[te] if te in DMA_Q else 0
            if waits.get((te, q), -1) < ti:
                waits[(te, q)] = ti

        if eng in self._bar_left and self._bar:
            for te, ti in self._bar:
                if te != eng or te in DMA_Q:
                    cover((te, ti), same_ok=True)
            self._bar_left.discard(eng)
        for k in r:
            cover(self.lw.get(k), same_ok=False)
        for k in w:
            cover(self.lw.get(k), same_ok=False)
            for te, ti in self.rd.get(k, {}).items():
                cover((te, ti), same_ok=False)
        for k in x:
            for te, ti in self.xa.get(k, {}).items():
                if te != eng:
                    cover((te, ti), same_ok=True)
        for (te, _q), ti in waits.items():
            self.ops[te][ti]["inc"] = True
        ops.append(dict(fn=fn, waits=waits, inc=False, skip=skip_inc))
        for k in r:
            self.rd.setdefault(k, {})[eng] = idx
        for k in w:
            self.lw[k] = (eng, idx)
            self.rd[k] = {}
        for k in x:
            self.xa.setdefault(k, {})[eng] = idx
        return idx

    def emit(self, nc, stack):
        sems = {}
        for e in ("tensor", "vector", "scalar"):
            sems[e] = {0: stack.enter_context(nc.semaphore(f"P_{e}"))}
        for e, nq in DMA_Q.items():
            sems[e] = {i: stack.enter_context(nc.semaphore(f"P_{e}{i}")) for i in range(nq)}
        sems["gpsimd"]["cc"] = stack.enter_context(nc.semaphore("P_cc"))
        semsel = {e: [None] * len(self.ops[e]) for e in ENGS}
        val_after = {e: [0] * len(self.ops[e]) for e in ENGS}
        for e in ENGS:
            if e in DMA_Q:
                nq = DMA_Q[e]
                counts = [0] * nq
                cnt = 0
                ccn = 0
                for i, op in enumerate(self.ops[e]):
                    if op.get("skip"):
                        op["inc"] = "cc"
                        ccn += 1
                        semsel[e][i] = "cc"
                        val_after[e][i] = ("cc", ccn)
                        continue
                    q = cnt % nq
                    cnt += 1
                    counts[q] += 16
                    op["inc"] = True
                    semsel[e][i] = q
                    val_after[e][i] = (q, counts[q])
            else:
                c = 0
                for i, op in enumerate(self.ops[e]):
                    if op["inc"]:
                        c += 1
                        semsel[e][i] = 0
                    val_after[e][i] = (0, c)

        def emit_engine(handle, e):
            waited = {}
            dma_i = 0
            for i, op in enumerate(self.ops[e]):
                if e in DMA_Q and not op.get("skip"):
                    if dma_i >= DMA_Q[e]:
                        q = dma_i % DMA_Q[e]
                        v = 16 * (dma_i // DMA_Q[e])
                        if waited.get((e, q), -1) < v:
                            handle.wait_ge(sems[e][q], v)
                            waited[(e, q)] = v
                    dma_i += 1
                for (te, _q), ti in sorted(op["waits"].items()):
                    q, v = val_after[te][ti]
                    keyq = (te, q)
                    if waited.get(keyq, -1) < v:
                        handle.wait_ge(sems[te][q], v)
                        waited[keyq] = v
                ins = op["fn"](handle)
                if op["inc"] == "cc":
                    ins.then_inc(sems[e]["cc"], 1)
                elif op["inc"]:
                    q = semsel[e][i]
                    ins.then_inc(sems[e][q], 16 if e in DMA_Q else 1)
            if e in DMA_Q:
                finals = {}
                for i, op in enumerate(self.ops[e]):
                    if op.get("inc") and semsel[e][i] is not None:
                        q, v = val_after[e][i]
                        finals[q] = max(finals.get(q, 0), v)
                for q, v in sorted(finals.items(), key=lambda kv: str(kv[0])):
                    handle.wait_ge(sems[e][q], v)
        return sems, emit_engine


def build_nc():
    nc = bass.Bass("TRN2", target_bir_lowering=False)
    h = {}
    h["x"] = nc.dram_tensor("x", [OWN, DM], F32, kind="ExternalInput").ap()
    h["cmask"] = nc.dram_tensor("cmask", [2, 1], F32, kind="ExternalInput").ap()
    agin = nc.dram_tensor("agin", [STN, 1], F32, kind="Internal").ap()
    agout = nc.dram_tensor("agout", [2 * STN, 1], F32, kind="Internal").ap()
    h["wdt"] = nc.dram_tensor("wdt", [DM, DM], F32, kind="ExternalInput").ap()
    h["bt"] = nc.dram_tensor("bt", [DM, STN], F32, kind="ExternalInput").ap()
    h["ct"] = nc.dram_tensor("ct", [STN, DM], F32, kind="ExternalInput").ap()
    h["dv"] = nc.dram_tensor("dv", [1, DM], F32, kind="ExternalInput").ap()
    h["bdel"] = nc.dram_tensor("bdel", [1, DM], F32, kind="ExternalInput").ap()
    h["alog"] = nc.dram_tensor("alog", [STN, 1], F32, kind="ExternalInput").ap()
    h["w1t"] = nc.dram_tensor("w1t", [DM, HID], F32, kind="ExternalInput").ap()
    h["b1c"] = nc.dram_tensor("b1c", [128, KH], F32, kind="ExternalInput").ap()
    h["w2t"] = nc.dram_tensor("w2t", [HID, DM], F32, kind="ExternalInput").ap()
    h["b2v"] = nc.dram_tensor("b2v", [1, DM], F32, kind="ExternalInput").ap()
    h["eye"] = nc.dram_tensor("eye", [128, 128], F32, kind="ExternalInput").ap()
    h["w1n"] = nc.dram_tensor("w1n", [HID, DM], F32, kind="ExternalInput").ap()
    h["w2n"] = nc.dram_tensor("w2n", [DM, HID], F32, kind="ExternalInput").ap()
    out_h = nc.dram_tensor("out", [OWN, DM], F32, kind="ExternalOutput").ap()
    bw1d = nc.dram_tensor("bw1d", [4, 128, KD, 8, 128], BF16, kind="Internal").ap()

    NFr = OWN // FT
    x_f = h["x"].rearrange("(i j p) d -> i p j d", j=4, p=128)
    out_t = out_h.rearrange("(u j p) d -> u p j d", j=4, p=128)

    st = contextlib.ExitStack()
    stW = contextlib.ExitStack()
    stS = contextlib.ExitStack()
    stF = contextlib.ExitStack()
    T = lambda *a: st.enter_context(nc.sbuf_tensor(*a))
    TW = lambda *a: stW.enter_context(nc.sbuf_tensor(*a))
    TS = lambda *a: stS.enter_context(nc.sbuf_tensor(*a))
    TF = lambda *a: stF.enter_context(nc.sbuf_tensor(*a))
    P = lambda *a: st.enter_context(nc.psum_tensor(*a))

    eye16 = T("eye16", [128, 128], BF16)
    ones16r = T("ones16r", [1, 128], BF16)
    onescol = T("onescol", [128, 1], BF16)
    epsv = T("epsv", [128, 1], F32)
    onev = T("onev", [128, 1], F32)
    a_sc = T("a_sc", [STN, 1], F32)
    d_bc = T("d_bc", [128, DM], F32)
    b2r16 = T("b2r16", [1, DM], BF16)
    b1c = T("b1c_s", [128, KH], F32)
    alpha1c = T("alpha1c", [128, KH], F32)
    ct16r = T("ct16r", [128, DM], BF16)
    w2t16 = T("w2t16", [128, KH, DM], BF16)
    hs16all = T("hs16all", [128, 1024], BF16)
    eye_f = TW("eye_f", [128, 128], F32)
    sqj = TW("sqj", [128, 1024], BF16)
    ones1f = TW("ones1f", [1, STN], F32)
    ones1f128 = TW("ones1f128", [1, 128], F32)
    bdel16 = TW("bdel16", [1, DM], BF16)
    bt16 = TW("bt16", [128, KD, STN], BF16)
    wdt16 = TW("wdt16", [128, KD, DM], BF16)
    hscr = TW("hscr", [STN, 2, FT], F32)
    cum16 = TW("cum16", [128, 1024], BF16)
    hinit = TW("hinit", [128, 1], F32)
    agsb = TW("agsb", [2, STN], F32)
    cmsb = TW("cmsb", [2, 1], F32)
    cumscr = TW("cumscr", [STN, 2, FT], F32)
    zer16 = TW("zer16", [STN, FT], F32)
    d_row = TW("d_row", [1, DM], F32)
    alog_s = TW("alog_s", [STN, 1], F32)
    w1q = [stS.enter_context(nc.sbuf_tensor(f"w1q{i}", [128, KD, 512], BF16, side="right")) for i in range(2)]
    ab16 = [stS.enter_context(nc.sbuf_tensor(f"ab16_{i}", [128, 512], BF16, side="right")) for i in range(2)]
    a1r = stS.enter_context(nc.sbuf_tensor("a1r", [1, 512], F32, side="right"))
    a2acc = stS.enter_context(nc.sbuf_tensor("a2acc", [1, DM], F32, side="right"))
    a2bc16 = stS.enter_context(nc.sbuf_tensor("a2bc16", [128, DM], BF16, side="right"))

    pA = P("pA", [128, 1024], F32)
    pB = P("pB", [128, 1024], F32)
    pT1 = P("pT1", [128, 1024], BF16)
    pT2 = P("pT2", [128, 1024], BF16)
    pS1 = P("pS1", [128, 512], F32)
    pS2 = P("pS2", [128, 512], F32)

    pl = Plan()
    A = pl.add

    A("sync", lambda e: e.dma_start(out=eye_f[:], in_=h["eye"][:]), w=["eye_f"])
    A("sync", lambda e: e.dma_start(out=d_row[:], in_=h["dv"][:]), w=["d_row"])
    A("sync", lambda e: e.dma_start(out=b1c[:], in_=h["b1c"][:]), w=["b1c"])
    A("sync", lambda e: e.dma_start(out=alog_s[:], in_=h["alog"][:]), w=["alog"])
    A("gpsimd", lambda e: e.dma_start(out=bdel16[:], in_=h["bdel"][:]), w=["bdel16"])
    A("gpsimd", lambda e: e.dma_start(out=b2r16[:], in_=h["b2v"][:]), w=["b2r16"])
    A("gpsimd", lambda e: e.dma_start(out=bt16[:], in_=h["bt"].rearrange("(k p) n -> p k n", p=128)), w=["bt16"])
    A("gpsimd", lambda e: e.dma_start(out=wdt16[:], in_=h["wdt"].rearrange("(k p) d -> p k d", p=128)), w=["wdt16"])
    A("gpsimd", lambda e: e.dma_start(out=ct16r[0:STN, :], in_=h["ct"][:]), w=["ct16"])
    for bp in (32, 64, 96):
        A("gpsimd", lambda e, bp=bp: e.dma_start(out=ct16r[bp:bp + STN, :], in_=ct16r[0:STN, :]),
          r=["ct16"], w=["ct16"])

    A("vector", lambda e: e.memset(ones16r[:], 1.0), w=["ones16r"])
    A("vector", lambda e: e.memset(ones1f[:], 1.0), w=["ones1f"])
    A("vector", lambda e: e.memset(onescol[:], 1.0), w=["onescol"])
    A("vector", lambda e: e.memset(epsv[:], 1e-5), w=["epsv"])
    A("vector", lambda e: e.memset(onev[:], 1.0), w=["onev"])
    A("vector", lambda e: e.tensor_copy(out=eye16[:], in_=eye_f[:]), r=["eye_f"], w=["eye16"])
    A("scalar", lambda e: e.activation(out=a_sc[:], in_=alog_s[:], func=AF.Exp), r=["alog"], w=["a_sc"])
    A("vector", lambda e: e.tensor_scalar_mul(out=a_sc[:], in0=a_sc[:], scalar1=-1.0 / DM), r=["a_sc"], w=["a_sc"])
    A("vector", lambda e: e.memset(ones1f128[:], 1.0), w=["ones1f128"])
    A("vector", lambda e: e.memset(zer16[:], 0.0), w=["zer16"])
    A("vector", lambda e: e.memset(hs16all[:], 0.0), w=["hsall_init"])
    A("vector", lambda e: e.memset(cum16[:], 0.0), w=["cum_init"])
    A("vector", lambda e: e.memset(hinit[:], 0.0), w=["hinit"])
    A("sync", lambda e: e.dma_start(out=cmsb[:], in_=h["cmask"][:]), w=["cmsb"])
    for half in range(2):
        sl = slice(half * 512, half * 512 + 512)
        A("tensor", lambda e, sl=sl: e.matmul(pA[:, sl], ones1f128[:], d_row[:, sl], start=True, stop=True),
          r=["ones1f128", "d_row"], x=["pA"])
    A("vector", lambda e: e.tensor_copy(out=d_bc[:], in_=pA[:]), w=["d_bc"], x=["pA"])

    setup_steps = []

    def _w2load():
        _w1load(0)
        A("gpsimd", lambda e: e.dma_start(out=w2t16[:], in_=h["w2t"].rearrange("(j p) d -> p j d", p=128)),
          w=["w2raw"])

    def _a2step(g):
        for jj in range(4):
            j = 4 * g + jj
            A("scalar", lambda e, j=j, jj=jj: e.activation(out=sqj[:], in_=w2t16[:, j, :],
                                                           func=AF.Abs),
              r=["w2raw"], w=["sqj"])
            for half in range(2):
                sl = slice(half * 512, half * 512 + 512)
                A("tensor", lambda e, jj=jj, sl=sl, s0=(jj == 0), s1=(jj == 3): e.matmul(
                    pA[0:1, sl], onescol[:], sqj[:, sl], start=s0, stop=s1),
                  r=["sqj", "onescol"], x=["pA"])
        if g == 0:
            A("vector", lambda e: e.tensor_copy(out=a2acc[:], in_=pA[0:1, 0:DM]),
              w=["a2acc"], x=["pA"])
        else:
            A("vector", lambda e: e.tensor_add(out=a2acc[:], in0=a2acc[:], in1=pA[0:1, 0:DM]),
              r=["a2acc"], w=["a2acc"], x=["pA"])

    def _w2fold(j):
        A("scalar", lambda e, j=j: e.activation(out=w2t16[:, j, :], in_=w2t16[:, j, :], func=AF.Sign),
          r=["w2raw"], w=["w2raw"])
        A("vector", lambda e, j=j: e.tensor_mul(out=w2t16[:, j, :], in0=w2t16[:, j, :], in1=a2bc16[:]),
          r=["w2raw", "a2bc16"], w=[f"bw2_{j}"])

    def _a2fin():
        for half in range(2):
            sl = slice(half * 512, half * 512 + 512)
            A("tensor", lambda e, sl=sl: e.matmul(pA[:, sl], ones1f128[:], a2acc[0:1, sl], start=True, stop=True),
              r=["ones1f128", "a2acc"], x=["pA"])
        A("vector", lambda e: e.tensor_scalar_mul(out=a2bc16[:], in0=pA[:], scalar1=1.0 / HID),
          w=["a2bc16"], x=["pA"])

    def _a1fin():
        A("vector", lambda e: e.tensor_scalar_mul(out=alpha1c[:], in0=alpha1c[:], scalar1=1.0 / DM),
          r=["alpha1c"], w=["alpha1c"])

    def _w1load(g):
        if g >= 8:
            return
        q, hf = divmod(g, 2)
        A("gpsimd", lambda e, g=g, q=q, hf=hf: e.dma_start(
            out=w1q[g % 2][:], in_=h["w1t"].rearrange("(k p) h -> p k h", p=128)[:, :, q * 1024 + hf * 512:q * 1024 + hf * 512 + 512]),
          w=[f"w1q{g % 2}"])

    def _w1chunk(q, hf):
        g = q * 2 + hf
        _w1load(g + 1)
        wq, wk = w1q[g % 2], f"w1q{g % 2}"
        for k in range(KD):
            A("scalar", lambda e, k=k, wq=wq: e.activation(out=ab16[k % 2][:], in_=wq[:, k, :], func=AF.Abs),
              r=[wk], w=[f"ab{k % 2}"])
            A("tensor", lambda e, k=k, s0=(k == 0), s1=(k == KD - 1): e.matmul(
                pS1[0:1, 0:512], onescol[:], ab16[k % 2][:], start=s0, stop=s1),
              r=[f"ab{k % 2}", "onescol"], x=["pS1"])
        A("vector", lambda e: e.tensor_copy(out=a1r[:], in_=pS1[0:1, 0:512]), w=["a1r"], x=["pS1"])
        base = q * 8 + hf * 4
        for jj in range(4):
            A("tensor", lambda e, jj=jj: e.matmul(
                pS2[:, jj:jj + 1], a1r[0:1, jj * 128:(jj + 1) * 128], onev[0:1, 0:1],
                start=True, stop=True),
              r=["a1r", "onev"], x=["pS2"])
        A("vector", lambda e, base=base: e.tensor_copy(out=alpha1c[:, base:base + 4], in_=pS2[:, 0:4]),
          w=["alpha1c"], x=["pS2"])
        for k in range(KD):
            A("scalar", lambda e, k=k, wq=wq: e.activation(out=wq[:, k, :], in_=wq[:, k, :], func=AF.Sign),
              r=[wk], w=[wk])
        A("sync", lambda e, q=q, hf=hf, wq=wq: e.dma_start(
            out=bw1d[q][:, :, hf * 4:(hf + 1) * 4, :],
            in_=wq[:, :, :].rearrange("p k (jj h) -> p k jj h", h=128)),
          r=[wk], w=[f"bw1d_{q}" if hf == 1 else f"bw1dh_{q}"])
    setup_steps.append((1, _w2load))
    for g in range(8):
        setup_steps.append((4, lambda q=g // 2, hf=g % 2: _w1chunk(q, hf)))
        setup_steps.append((2, lambda g=g: _a2step(g)))
    setup_steps.append((1, _a2fin))
    setup_steps.append((1, _a1fin))
    for j in range(KH):
        setup_steps.append((1, lambda j=j: _w2fold(j)))

    def pop_steps(budget):
        while setup_steps and budget > 0:
            c, fn = setup_steps.pop(0)
            fn()
            budget -= c

    xb32 = [TF(f"xb32_{i}", [128, 4, 1024], F32) for i in range(2)]
    xn1 = [TF(f"xn1_{i}", [128, 4, 1024], BF16) for i in range(2)]
    xn1T_t = TF("xn1T", [128, KD, FT], BF16)
    xn1T = [xn1T_t, xn1T_t]
    bst = [TF(f"bst_{i}", [128, 4, 12], F32) for i in range(2)]
    bag = [TF(f"bag_{i}", [128, 4, 2], F32) for i in range(2)]
    frv = [TF(f"frv_{i}", [128, 4], F32) for i in range(2)]
    cbuf = TF("cbuf", [128, 1024], F32)
    dcol = [TF(f"dcol_{i}", [128, 4], F32) for i in range(2)]
    drow_t = TF("drow", [1, FT], F32)
    drow = [drow_t, drow_t]
    dA_t = TF("dA", [STN, FT], F32)
    dA_sb = [dA_t, dA_t]
    dBx_t = TF("dBx", [STN, FT], F32)
    dBx = [dBx_t, dBx_t]
    dsb_t = TF("dsb", [STN, FT], F32)
    dsb = [dsb_t, dsb_t]

    def f_load(i):
        if i >= NFr:
            return
        pi = i % 2
        A("sync", lambda e, i=i, pi=pi: e.dma_start(out=xb32[pi][:], in_=x_f[i]), w=[f"xb{pi}"])

    def f_norm(i):
        pi = i % 2
        kx = f"xb{pi}"
        for j in range(4):
            A("vector", lambda e, pi=pi, j=j: e.bn_stats(out=bst[pi][:, j, 0:6], in_=xb32[pi][:, j, 0:512]),
              r=[kx], w=[f"bst{pi}"])
            A("vector", lambda e, pi=pi, j=j: e.bn_stats(out=bst[pi][:, j, 6:12], in_=xb32[pi][:, j, 512:1024]),
              r=[kx], w=[f"bst{pi}"])
            A("vector", lambda e, pi=pi, j=j: e.bn_aggr(out=bag[pi][:, j, :], in_=bst[pi][:, j, :]),
              r=[f"bst{pi}"], w=[f"bag{pi}"])
        A("scalar", lambda e, pi=pi: e.activation(out=frv[pi][:], in_=bag[pi][:, :, 1:2], func=AF.Ln,
                                                  bias=epsv[:, 0:1]),
          r=[f"bag{pi}", "epsv"], w=[f"frv{pi}"])
        A("scalar", lambda e, pi=pi: e.activation(out=frv[pi][:], in_=frv[pi][:], func=AF.Exp, scale=-0.5),
          r=[f"frv{pi}"], w=[f"frv{pi}"])
        for j in range(4):
            A("vector", lambda e, pi=pi, j=j: e.tensor_scalar(
                out=xn1[pi][:, j, :], in0=xb32[pi][:, j, :],
                scalar1=bag[pi][:, j, 0:1], scalar2=frv[pi][:, j:j + 1],
                op0=ALU.subtract, op1=ALU.mult),
              r=[kx, f"bag{pi}", f"frv{pi}"], w=[f"xn1{pi}_{j}"])

    def f_stage_t(i):
        pi = i % 2
        for k in range(KD):
            pt, ptk = (pT1, "pT1") if k % 2 == 0 else (pT2, "pT2")
            for j in range(4):
                A("tensor", lambda e, pi=pi, j=j, k=k, pt=pt: e.transpose(
                    pt[:, j * 128:(j + 1) * 128], xn1[pi][:, j, k * 128:(k + 1) * 128], eye16[:]),
                  r=[f"xn1{pi}_{j}", "eye16"], x=[ptk])
            A("vector", lambda e, pi=pi, k=k, pt=pt: e.tensor_copy(out=xn1T[pi][:, k, :], in_=pt[:, 0:512]),
              w=[f"xn1T_{k}"], x=[ptk])

    def f_stage_b(i):
        pi = i % 2
        for m in range(4):
            pz, pzk = (pA, "pA") if m % 2 == 0 else (pB, "pB")
            for k in range(KD):
                for half in range(2):
                    sl = slice(half * 512, half * 512 + 512)
                    A("tensor", lambda e, pi=pi, m=m, k=k, sl=sl, pz=pz, s0=(k == 0): e.matmul(
                        pz[:, sl], xn1T[pi][:, k, m * 128:(m + 1) * 128], wdt16[:, k, sl],
                        start=s0, stop=False),
                      r=[f"xn1T_{k}", "wdt16"], x=[pzk])
            for half in range(2):
                sl = slice(half * 512, half * 512 + 512)
                A("tensor", lambda e, sl=sl, pz=pz: e.matmul(
                    pz[:, sl], ones16r[:], bdel16[:, sl], start=False, stop=True),
                  r=["ones16r", "bdel16"], x=[pzk])
            A("scalar", lambda e, pz=pz: e.activation(out=cbuf[:], in_=pz[:], func=AF.Exp),
              r=[], w=["cbuf"], x=[pzk])
            A("scalar", lambda e, pi=pi, m=m: e.activation(out=sqj[:], in_=cbuf[:], func=AF.Ln,
                                                           bias=onev[:, 0:1],
                                                           accum_out=dcol[pi][:, m:m + 1]),
              r=["cbuf", "onev"], w=["sqj", f"dcol{pi}"])
        for k in range(KD):
            A("tensor", lambda e, pi=pi, k=k, s0=(k == 0): e.matmul(
                pS1[0:STN, :], bt16[:, k, :], xn1T[pi][:, k, :], start=s0, stop=(k == KD - 1)),
              r=["bt16", f"xn1T_{k}"], x=["pS1"])

    def f_chain(i):
        pi = i % 2
        for m in range(4):
            A("tensor", lambda e, pi=pi, m=m: e.transpose(
                pS2[0:1, m * 128:(m + 1) * 128], dcol[pi][:, m:m + 1], eye_f[:]),
              r=[f"dcol{pi}", "eye_f"], x=["pS2"])
        A("vector", lambda e, pi=pi: e.tensor_copy(out=drow[pi][:], in_=pS2[0:1, 0:FT]),
          w=["drow"], x=["pS2"])
        A("tensor", lambda e, pi=pi: e.matmul(pS2[0:STN, :], ones1f[:], drow[pi][:], start=True, stop=True),
          r=["ones1f", "drow"], x=["pS2"])
        A("scalar", lambda e, pi=pi: e.activation(out=dA_sb[pi][:], in_=pS2[0:STN, :], func=AF.Exp,
                                                  scale=a_sc[:, 0:1]),
          r=["a_sc"], w=["dAk"], x=["pS2"])
        A("vector", lambda e, pi=pi: e.tensor_scalar_mul(out=dsb[pi][:], in0=pS2[0:STN, :], scalar1=1.0 / DM),
          w=["dsb"], x=["pS2"])
        A("vector", lambda e, pi=pi: e.tensor_mul(out=dBx[pi][:], in0=dsb[pi][:], in1=pS1[0:STN, :]),
          r=["dsb"], w=["dBxk"], x=["pS1"])

        def pkey(t):
            return f"hscr{t % 2}"

        def scanfn(e, pi=pi, i=i):
            ini = 0.0 if i == 0 else hscr[:, (i - 1) % 2, FT - 1:FT]
            return e.tensor_tensor_scan(out=hscr[:, i % 2, :], data0=dA_sb[pi][:], data1=dBx[pi][:],
                                        initial=ini, op0=ALU.mult, op1=ALU.add)
        A("vector", scanfn, r=["dAk", "dBxk"] + ([pkey(i - 1)] if i > 0 else []), w=[pkey(i)])

        def cscanfn(e, pi=pi, i=i):
            ini = 1.0 if i == 0 else cumscr[:, (i - 1) % 2, FT - 1:FT]
            return e.tensor_tensor_scan(out=cumscr[:, i % 2, :], data0=dA_sb[pi][:], data1=zer16[:],
                                        initial=ini, op0=ALU.mult, op1=ALU.add)
        A("vector", cscanfn, r=["dAk", "zer16"] + ([f"cumscr{(i - 1) % 2}"] if i > 0 else []),
          w=[f"cumscr{i % 2}"])
        g = i
        A("gpsimd", lambda e, i=i, g=g: e.dma_start(
            out=hs16all[32 * (g % 4):32 * (g % 4) + STN, (g // 4) * FT:(g // 4 + 1) * FT],
            in_=hscr[:, i % 2, :]),
          r=[pkey(i), "hsall_init"], w=[f"hsb{g}"])
        A("gpsimd", lambda e, i=i, g=g: e.dma_start(
            out=cum16[32 * (g % 4):32 * (g % 4) + STN, (g // 4) * FT:(g // 4 + 1) * FT],
            in_=cumscr[:, i % 2, :]),
          r=[f"cumscr{i % 2}", "cum_init"], w=[f"cum_{g}"])

    nsteps = sum(c for c, _ in setup_steps)
    per_tile = -(-nsteps // max(NFr - 1, 1)) + 1
    f_load(0)
    f_load(1)
    f_norm(0)
    f_stage_t(0)
    for i in range(NFr):
        f_load(i + 2)
        if i + 1 < NFr:
            f_norm(i + 1)
        f_stage_b(i)
        if i + 1 < NFr:
            f_stage_t(i + 1)
        f_chain(i)
        pop_steps(per_tile if i > 0 else 1)
    pop_steps(10 ** 9)

    A("sync", lambda e: e.dma_start(out=agin.rearrange("n 1 -> n 1"), in_=hscr[:, (NFr - 1) % 2, FT - 1:FT]),
      r=[f"hscr{(NFr - 1) % 2}"], w=["agin_d"])
    A("gpsimd", lambda e: e.collective_compute(
        "AllGather", ALU.bypass, CC_GROUPS, ins=[agin[:, :]], outs=[agout[:, :]]),
      r=["agin_d"], w=["agout_d"], skip_inc=True)
    A("sync", lambda e: e.dma_start(out=agsb[:], in_=agout.rearrange("(r n) 1 -> r n", n=STN)),
      r=["agout_d"], w=["agsb"])
    A("tensor", lambda e: e.matmul(pS2[0:STN, 0:1], agsb[:], cmsb[:], start=True, stop=True),
      r=["agsb", "cmsb"], x=["pS2"])
    A("vector", lambda e: e.tensor_copy(out=hinit[0:STN, :], in_=pS2[0:STN, 0:1]),
      r=["hinit"], w=["hinit"], x=["pS2"])
    for bp in (32, 64, 96):
        A("sync", lambda e, bp=bp: e.dma_start(out=hinit[bp:bp + STN, :], in_=hinit[0:STN, :]),
          r=["hinit"], w=["hinit"])
    A("vector", lambda e: e.scalar_tensor_tensor(
        out=hs16all[:], in0=cum16[:], scalar=hinit[:, 0:1], in1=hs16all[:],
        op0=ALU.mult, op1=ALU.add),
      r=["hinit", "cum_init", "hsall_init"] + [f"cum_{g}" for g in range(NFr)] + [f"hsb{g}" for g in range(NFr)],
      w=[f"hsb{g}" for g in range(NFr)])

    pl.barrier()
    stS.close()
    stF.close()
    stW.close()
    MB = 4
    BTK = 512
    NBT = OWN // BTK
    OFF = 0
    x_b = h["x"].rearrange("(i j p) d -> i p j d", j=MB, p=128)
    out_b = out_h.rearrange("(u j p) d -> u p j d", j=MB, p=128)

    xf = [T(f"xf_{i}", [128, MB, 1024], F32) for i in range(2)]
    yp = [T(f"yp_{i}", [128, MB, 1024], F32) for i in range(2)]
    xn3m = T("xn3m", [128, 1024], BF16)
    xn3T = T("xn3T", [128, KD, BTK], BF16)
    h_sb = T("h_sb", [128, KH, BTK], BF16)
    scr = T("scr", [128, 512], F32)
    w1buf = [T(f"w1buf_{i}", [128, KD, 4, 128], BF16) for i in range(3)]
    bbst = T("bbst", [128, MB, 12], F32)
    bbag = T("bbag", [128, MB, 2], F32)
    bbst4 = T("bbst4", [128, MB, 12], F32)
    bag4 = T("bag4", [128, MB, 2], F32)
    brv = {n: T(f"brv{n}", [128, MB], F32) for n in (1, 2, 3, 4)}

    def b_load(c):
        if c < NBT:
            for m in range(MB):
                A("sync", lambda e, c=c, m=m: e.dma_start(out=xf[c % 2][:, m, :], in_=x_b[OFF + c][:, m, :]),
                  w=[f"xf{c % 2}m{m}"])

    def b_stream(u, s):
        g, hf = divmod(s, 2)
        A("sync", lambda e, s=s, g=g, hf=hf: e.dma_start(
            out=w1buf[s % 3][:], in_=bw1d[g][:, :, hf * 4:(hf + 1) * 4, :]),
          r=[f"bw1d_{g}", f"bw1dh_{g}"], w=[f"w1b{s % 3}"])

    def b_ln(src_fn, src_keys, rvk, n):
        for m in range(MB):
            sk = src_keys(m) if callable(src_keys) else src_keys
            A("vector", lambda e, m=m: e.bn_stats(out=bbst[:, m, 0:6], in_=src_fn(m)[:, 0:512]),
              r=sk, w=["bbst"])
            A("vector", lambda e, m=m: e.bn_stats(out=bbst[:, m, 6:12], in_=src_fn(m)[:, 512:1024]),
              r=sk, w=["bbst"])
            A("vector", lambda e, m=m: e.bn_aggr(out=bbag[:, m, :], in_=bbst[:, m, :]),
              r=["bbst"], w=["bbag"])
        A("scalar", lambda e, rvk=rvk: e.activation(out=brv[n][:], in_=bbag[:, :, 1:2], func=AF.Sqrt,
                                                    bias=epsv[:, 0:1]),
          r=["bbag", "epsv"], w=[rvk])
        A("vector", lambda e, rvk=rvk: e.reciprocal(out=brv[n][:], in_=brv[n][:]),
          r=[rvk], w=[rvk])

    def b_chain_a(c):
        pc = c % 2
        b_ln(lambda m: xf[pc][:, m, :], lambda m: [f"xf{pc}m{m}"], "brv1", 1)

    def b_chain_b(c):
        pc = c % 2
        bp = 32 * (c % 4)
        cb = (c // 4) * FT
        for m in range(MB):
            pz, pzk = (pA, "pA") if m % 2 == 0 else (pB, "pB")
            for half in range(2):
                sl = slice(half * 512, half * 512 + 512)
                A("tensor", lambda e, bp=bp, cb=cb, m=m, sl=sl, pz=pz: e.matmul(
                    pz[:, sl], hs16all[bp:bp + STN, cb + m * 128:cb + (m + 1) * 128],
                    ct16r[bp:bp + STN, sl],
                    start=True, stop=True, tile_position=(bp, 0)),
                  r=[f"hsb{c}", "ct16"], x=[pzk])
            for half in range(2):
                sl = slice(half * 512, half * 512 + 512)
                A("vector", lambda e, pc=pc, m=m, sl=sl: e.tensor_scalar(out=scr[:], in0=xf[pc][:, m, sl],
                                                                         scalar1=bbag[:, m, 0:1],
                                                                         scalar2=brv[1][:, m:m + 1],
                                                                         op0=ALU.subtract, op1=ALU.mult),
                  r=[f"xf{pc}m{m}", "bbag", "brv1"], w=["scr"])
                A("vector", lambda e, sl=sl: e.tensor_mul(out=scr[:], in0=scr[:], in1=d_bc[:, sl]),
                  r=["scr", "d_bc"], w=["scr"])
                A("vector", lambda e, pc=pc, m=m, sl=sl, pz=pz: e.tensor_add(out=yp[pc][:, m, sl], in0=scr[:], in1=pz[:, sl]),
                  r=["scr"], w=[f"yp{pc}"], x=[pzk])
        b_ln(lambda m: yp[pc][:, m, :], [f"yp{pc}"], "brv2", 2)

    def b_chain_c(c):
        pc = c % 2
        for m in range(MB):
            for half in range(2):
                sl = slice(half * 512, half * 512 + 512)
                A("vector", lambda e, pc=pc, m=m, sl=sl: e.tensor_scalar(out=scr[:], in0=yp[pc][:, m, sl],
                                                                         scalar1=bbag[:, m, 0:1],
                                                                         scalar2=brv[2][:, m:m + 1],
                                                                         op0=ALU.subtract, op1=ALU.mult),
                  r=[f"yp{pc}", "bbag", "brv2"], w=["scr"])
                A("vector", lambda e, pc=pc, m=m, sl=sl: e.tensor_add(out=xf[pc][:, m, sl], in0=scr[:], in1=xf[pc][:, m, sl]),
                  r=["scr", f"xf{pc}m{m}"], w=[f"xf{pc}m{m}"])
        b_ln(lambda m: xf[pc][:, m, :], lambda m: [f"xf{pc}m{m}"], "brv3", 3)

    def b_chain_d(c):
        pc = c % 2
        for m in range(MB):
            pt, ptk = (pT1, "pT1") if m % 2 == 0 else (pT2, "pT2")
            A("vector", lambda e, pc=pc, m=m: e.tensor_scalar(out=xn3m[:], in0=xf[pc][:, m, :],
                                                              scalar1=bbag[:, m, 0:1],
                                                              scalar2=brv[3][:, m:m + 1],
                                                              op0=ALU.subtract, op1=ALU.mult),
              r=[f"xf{pc}m{m}", "bbag", "brv3"], w=["xn3m"])
            for k in range(KD):
                A("tensor", lambda e, m=m, k=k, pt=pt: e.transpose(
                    pt[:, k * 128:(k + 1) * 128], xn3m[:, k * 128:(k + 1) * 128], eye16[:]),
                  r=["xn3m", "eye16"], x=[ptk])
            A("vector", lambda e, m=m, pt=pt: e.tensor_copy(
                out=xn3T[:, :, m * 128:(m + 1) * 128],
                in_=pt[:, :].rearrange("p (k t) -> p k t", t=128)),
              w=[f"xn3Tm_{m}"], x=[ptk])

    def b_mm1_group(u, grp):
        for j in range(8 * grp, 8 * grp + 8):
            s = j // 4
            if j % 4 == 0 and s + 2 < 8 and j > 0:
                b_stream(u, s + 2)
            ph, phk = (pS1, "pS1") if j % 2 == 0 else (pS2, "pS2")
            for k in range(KD):
                A("tensor", lambda e, s=s, j=j, k=k, ph=ph, s0=(k == 0): e.matmul(
                    ph[:, 0:BTK], w1buf[s % 3][:, k, j % 4, :], xn3T[:, k, :],
                    start=s0, stop=(k == KD - 1)),
                  r=[f"w1b{s % 3}"] + [f"xn3Tm_{m2}" for m2 in range(MB)], x=[phk])
            A("scalar", lambda e, j=j, ph=ph: e.activation(
                out=h_sb[:, j, :], in_=ph[:, 0:BTK], func=AF.Gelu,
                bias=b1c[:, j:j + 1], scale=alpha1c[:, j:j + 1]),
              r=["b1c", "alpha1c"], w=[f"h_{j}"], x=[phk])

    def b_mm2(u):
        pc = u % 2
        for m in range(MB):
            pz, pzk = (pA, "pA") if m % 2 == 0 else (pB, "pB")
            for j in range(KH):
                for half in range(2):
                    sl = slice(half * 512, half * 512 + 512)
                    A("tensor", lambda e, m=m, j=j, sl=sl, pz=pz, s0=(j == 0): e.matmul(
                        pz[:, sl], h_sb[:, j, m * 128:(m + 1) * 128], w2t16[:, j, sl],
                        start=s0, stop=False),
                      r=[f"h_{j}", f"bw2_{j}"], x=[pzk])
            for half in range(2):
                sl = slice(half * 512, half * 512 + 512)
                A("tensor", lambda e, sl=sl, pz=pz: e.matmul(
                    pz[:, sl], ones16r[:], b2r16[:, sl], start=False, stop=True),
                  r=["ones16r", "b2r16"], x=[pzk])
            A("vector", lambda e, m=m, pz=pz: e.bn_stats(out=bbst4[:, m, 0:6], in_=pz[:, 0:512]),
              w=["bbst4"], x=[pzk])
            A("vector", lambda e, m=m, pz=pz: e.bn_stats(out=bbst4[:, m, 6:12], in_=pz[:, 512:1024]),
              w=["bbst4"], x=[pzk])
            A("vector", lambda e, m=m: e.bn_aggr(out=bag4[:, m, :], in_=bbst4[:, m, :]),
              r=["bbst4"], w=["bag4"])
            A("scalar", lambda e, pc=pc, m=m, pz=pz: e.copy(out=yp[pc][:, m, :], in_=pz[:]),
              w=[f"yp{pc}"], x=[pzk])
        A("scalar", lambda e: e.activation(out=brv[4][:], in_=bag4[:, :, 1:2], func=AF.Sqrt,
                                           bias=epsv[:, 0:1]),
          r=["bag4", "epsv"], w=["brv4"])
        A("vector", lambda e: e.reciprocal(out=brv[4][:], in_=brv[4][:]),
          r=["brv4"], w=["brv4"])

    def b_tail_out(u):
        pc = u % 2
        for m in range(MB):
            for half in range(2):
                sl = slice(half * 512, half * 512 + 512)
                A("vector", lambda e, pc=pc, m=m, sl=sl: e.tensor_scalar(out=scr[:], in0=yp[pc][:, m, sl],
                                                                         scalar1=bag4[:, m, 0:1],
                                                                         scalar2=brv[4][:, m:m + 1],
                                                                         op0=ALU.subtract, op1=ALU.mult),
                  r=[f"yp{pc}", "bag4", "brv4"], w=["scr"])
                A("vector", lambda e, pc=pc, m=m, sl=sl: e.tensor_add(out=xf[pc][:, m, sl], in0=scr[:], in1=xf[pc][:, m, sl]),
                  r=["scr", f"xf{pc}m{m}"], w=[f"xf{pc}m{m}"])
            A("gpsimd", lambda e, u=u, pc=pc, m=m: e.dma_start(out=out_b[u][:, m, :], in_=xf[pc][:, m, :]),
              r=[f"xf{pc}m{m}"], w=[f"out_{u}_{m}"])

    b_load(0)
    b_stream(0, 0)
    b_stream(0, 1)
    b_stream(0, 2)
    b_chain_a(0)
    b_chain_b(0)
    b_chain_c(0)
    b_chain_d(0)
    for u in range(NBT):
        if u > 0:
            b_tail_out(u - 1)
        b_load(u + 1)
        b_mm1_group(u, 0)
        if u + 1 < NBT:
            b_chain_a(u + 1)
        b_mm1_group(u, 1)
        if u + 1 < NBT:
            b_chain_b(u + 1)
        b_mm1_group(u, 2)
        if u + 1 < NBT:
            b_chain_c(u + 1)
        b_mm1_group(u, 3)
        if u + 1 < NBT:
            b_chain_d(u + 1)
            b_stream(u + 1, 0)
            b_stream(u + 1, 1)
            b_stream(u + 1, 2)
        b_mm2(u)
    b_tail_out(NBT - 1)

    sems, emit_engine = pl.emit(nc, st)
    with nc.Block() as block:
        @block.sync
        def _(e):
            emit_engine(e, "sync")

        @block.vector
        def _(e):
            emit_engine(e, "vector")

        @block.scalar
        def _(e):
            emit_engine(e, "scalar")

        @block.tensor
        def _(e):
            emit_engine(e, "tensor")

        @block.gpsimd
        def _(e):
            emit_engine(e, "gpsimd")

    st.close()
    return nc


_NC_CACHE = []


def kernel(L0_attn_x, L0_attn_A_log, L0_attn_B, L0_attn_C, L0_attn_D,
           L0_attn_W_delta, L0_attn_b_delta, L0_ffn_w1, L0_ffn_b1,
           L0_ffn_w2, L0_ffn_b2):
    x = np.asarray(L0_attn_x, dtype=np.float32)
    wdt = np.ascontiguousarray(np.asarray(L0_attn_W_delta, np.float32).T)
    bt = np.ascontiguousarray(np.asarray(L0_attn_B, np.float32).T)
    ct = np.ascontiguousarray(np.asarray(L0_attn_C, np.float32).T)
    dv = np.ascontiguousarray(np.asarray(L0_attn_D, np.float32).reshape(1, DM))
    bdel = np.ascontiguousarray(np.asarray(L0_attn_b_delta, np.float32).reshape(1, DM))
    alog = np.ascontiguousarray(np.asarray(L0_attn_A_log, np.float32).reshape(STN, 1))
    w1t = np.ascontiguousarray(np.asarray(L0_ffn_w1, np.float32).T)
    b1c = np.ascontiguousarray(np.asarray(L0_ffn_b1, np.float32).reshape(KH, 128).T)
    w2t = np.ascontiguousarray(np.asarray(L0_ffn_w2, np.float32).T)
    b2v = np.ascontiguousarray(np.asarray(L0_ffn_b2, np.float32).reshape(1, DM))
    eye = np.eye(128, dtype=np.float32)

    if not _NC_CACHE:
        _NC_CACHE.append(build_nc())
    nc = _NC_CACHE[0]

    in_maps = []
    for c in range(8):
        b, hf = divmod(c, 2)
        xfr = np.ascontiguousarray(x[b, hf * OWN:(hf + 1) * OWN])
        cmask = np.asarray([[1.0], [0.0]], np.float32) if hf == 1 else np.zeros((2, 1), np.float32)
        in_maps.append(dict(x=xfr, cmask=cmask, wdt=wdt, bt=bt, ct=ct, dv=dv,
                            bdel=bdel, alog=alog, w1t=w1t, b1c=b1c, w2t=w2t, b2v=b2v, eye=eye,
                            w1n=np.asarray(L0_ffn_w1, np.float32), w2n=np.asarray(L0_ffn_w2, np.float32)))
    res = run_bass_kernel_spmd(nc, in_maps, core_ids=list(range(8)))
    out = np.empty((4, SEQ, DM), np.float32)
    for c, r in enumerate(res.results):
        b, hf = divmod(c, 2)
        out[b, hf * OWN:(hf + 1) * OWN] = r["out"]
    return out


# version 42
# speedup vs baseline: 1.0323x; 1.0323x over previous
import contextlib
import numpy as np
import concourse.bass as bass
import concourse.mybir as mybir
from concourse.bass_utils import run_bass_kernel_spmd

F32 = mybir.dt.float32
BF16 = mybir.dt.bfloat16
AF = mybir.ActivationFunctionType
ALU = mybir.AluOpType
AX = mybir.AxisListType

ENGS = ("tensor", "vector", "scalar", "sync", "gpsimd")
DMA_Q = {"sync": 4, "gpsimd": 4}

SEQ, OWN, DM, STN, HID = 8192, 4096, 1024, 16, 4096
CC_GROUPS = [[0, 1], [2, 3], [4, 5], [6, 7]]
FT, NF = 512, 16
BT, NB = 512, 8
KD = 8
KH = 32


class Plan:

    def __init__(self):
        self.ops = {e: [] for e in ENGS}
        self.lw = {}
        self.rd = {}
        self.xa = {}
        self._bar = {}
        self._bar_left = set()

    def barrier(self):
        snap = []
        for e in ENGS:
            n = len(self.ops[e])
            if n == 0:
                continue
            if e in DMA_Q:
                nq = DMA_Q[e]
                for q in range(min(nq, n)):
                    i = n - 1 - ((n - 1 - q) % nq)
                    i = max(j for j in range(n) if j % nq == q)
                    snap.append((e, i))
            else:
                self.ops[e][-1]["inc"] = True
                snap.append((e, n - 1))
        self._bar = snap
        self._bar_left = set(ENGS)

    def add(self, eng, fn, r=(), w=(), x=(), skip_inc=False):
        ops = self.ops[eng]
        idx = len(ops)
        waits = {}

        def cover(tok, same_ok):
            if tok is None:
                return
            te, ti = tok
            if same_ok and te == eng and te not in DMA_Q:
                return
            q = ti % DMA_Q[te] if te in DMA_Q else 0
            if waits.get((te, q), -1) < ti:
                waits[(te, q)] = ti

        if eng in self._bar_left and self._bar:
            for te, ti in self._bar:
                if te != eng or te in DMA_Q:
                    cover((te, ti), same_ok=True)
            self._bar_left.discard(eng)
        for k in r:
            cover(self.lw.get(k), same_ok=False)
        for k in w:
            cover(self.lw.get(k), same_ok=False)
            for te, ti in self.rd.get(k, {}).items():
                cover((te, ti), same_ok=False)
        for k in x:
            for te, ti in self.xa.get(k, {}).items():
                if te != eng:
                    cover((te, ti), same_ok=True)
        for (te, _q), ti in waits.items():
            self.ops[te][ti]["inc"] = True
        ops.append(dict(fn=fn, waits=waits, inc=False, skip=skip_inc))
        for k in r:
            self.rd.setdefault(k, {})[eng] = idx
        for k in w:
            self.lw[k] = (eng, idx)
            self.rd[k] = {}
        for k in x:
            self.xa.setdefault(k, {})[eng] = idx
        return idx

    def emit(self, nc, stack):
        sems = {}
        for e in ("tensor", "vector", "scalar"):
            sems[e] = {0: stack.enter_context(nc.semaphore(f"P_{e}"))}
        for e, nq in DMA_Q.items():
            sems[e] = {i: stack.enter_context(nc.semaphore(f"P_{e}{i}")) for i in range(nq)}
        sems["gpsimd"]["cc"] = stack.enter_context(nc.semaphore("P_cc"))
        semsel = {e: [None] * len(self.ops[e]) for e in ENGS}
        val_after = {e: [0] * len(self.ops[e]) for e in ENGS}
        for e in ENGS:
            if e in DMA_Q:
                nq = DMA_Q[e]
                counts = [0] * nq
                cnt = 0
                ccn = 0
                for i, op in enumerate(self.ops[e]):
                    if op.get("skip"):
                        op["inc"] = "cc"
                        ccn += 1
                        semsel[e][i] = "cc"
                        val_after[e][i] = ("cc", ccn)
                        continue
                    q = cnt % nq
                    cnt += 1
                    counts[q] += 16
                    op["inc"] = True
                    semsel[e][i] = q
                    val_after[e][i] = (q, counts[q])
            else:
                c = 0
                for i, op in enumerate(self.ops[e]):
                    if op["inc"]:
                        c += 1
                        semsel[e][i] = 0
                    val_after[e][i] = (0, c)

        def emit_engine(handle, e):
            waited = {}
            dma_i = 0
            for i, op in enumerate(self.ops[e]):
                if e in DMA_Q and not op.get("skip"):
                    if dma_i >= DMA_Q[e]:
                        q = dma_i % DMA_Q[e]
                        v = 16 * (dma_i // DMA_Q[e])
                        if waited.get((e, q), -1) < v:
                            handle.wait_ge(sems[e][q], v)
                            waited[(e, q)] = v
                    dma_i += 1
                for (te, _q), ti in sorted(op["waits"].items()):
                    q, v = val_after[te][ti]
                    keyq = (te, q)
                    if waited.get(keyq, -1) < v:
                        handle.wait_ge(sems[te][q], v)
                        waited[keyq] = v
                ins = op["fn"](handle)
                if op["inc"] == "cc":
                    ins.then_inc(sems[e]["cc"], 1)
                elif op["inc"]:
                    q = semsel[e][i]
                    ins.then_inc(sems[e][q], 16 if e in DMA_Q else 1)
            if e in DMA_Q:
                finals = {}
                for i, op in enumerate(self.ops[e]):
                    if op.get("inc") and semsel[e][i] is not None:
                        q, v = val_after[e][i]
                        finals[q] = max(finals.get(q, 0), v)
                for q, v in sorted(finals.items(), key=lambda kv: str(kv[0])):
                    handle.wait_ge(sems[e][q], v)
        return sems, emit_engine


def build_nc():
    nc = bass.Bass("TRN2", target_bir_lowering=False)
    h = {}
    h["x"] = nc.dram_tensor("x", [OWN, DM], F32, kind="ExternalInput").ap()
    h["cmask"] = nc.dram_tensor("cmask", [2, 1], F32, kind="ExternalInput").ap()
    agin = nc.dram_tensor("agin", [STN, 1], F32, kind="Internal").ap()
    agout = nc.dram_tensor("agout", [2 * STN, 1], F32, kind="Internal").ap()
    h["wdt"] = nc.dram_tensor("wdt", [DM, DM], F32, kind="ExternalInput").ap()
    h["bt"] = nc.dram_tensor("bt", [DM, STN], F32, kind="ExternalInput").ap()
    h["ct"] = nc.dram_tensor("ct", [STN, DM], F32, kind="ExternalInput").ap()
    h["dv"] = nc.dram_tensor("dv", [1, DM], F32, kind="ExternalInput").ap()
    h["bdel"] = nc.dram_tensor("bdel", [1, DM], F32, kind="ExternalInput").ap()
    h["alog"] = nc.dram_tensor("alog", [STN, 1], F32, kind="ExternalInput").ap()
    h["w1t"] = nc.dram_tensor("w1t", [DM, HID], F32, kind="ExternalInput").ap()
    h["b1c"] = nc.dram_tensor("b1c", [128, KH], F32, kind="ExternalInput").ap()
    h["w2t"] = nc.dram_tensor("w2t", [HID, DM], F32, kind="ExternalInput").ap()
    h["b2v"] = nc.dram_tensor("b2v", [1, DM], F32, kind="ExternalInput").ap()
    h["eye"] = nc.dram_tensor("eye", [128, 128], F32, kind="ExternalInput").ap()
    h["w1n"] = nc.dram_tensor("w1n", [HID, DM], F32, kind="ExternalInput").ap()
    h["w2n"] = nc.dram_tensor("w2n", [DM, HID], F32, kind="ExternalInput").ap()
    out_h = nc.dram_tensor("out", [OWN, DM], F32, kind="ExternalOutput").ap()
    bw1d = nc.dram_tensor("bw1d", [4, 128, KD, 8, 128], BF16, kind="Internal").ap()

    NFr = OWN // FT
    x_f = h["x"].rearrange("(i j p) d -> i p j d", j=4, p=128)
    out_t = out_h.rearrange("(u j p) d -> u p j d", j=4, p=128)

    st = contextlib.ExitStack()
    stW = contextlib.ExitStack()
    stS = contextlib.ExitStack()
    stF = contextlib.ExitStack()
    T = lambda *a: st.enter_context(nc.sbuf_tensor(*a))
    TW = lambda *a: stW.enter_context(nc.sbuf_tensor(*a))
    TS = lambda *a: stS.enter_context(nc.sbuf_tensor(*a))
    TF = lambda *a: stF.enter_context(nc.sbuf_tensor(*a))
    P = lambda *a: st.enter_context(nc.psum_tensor(*a))

    eye16 = T("eye16", [128, 128], BF16)
    ones16r = T("ones16r", [1, 128], BF16)
    onescol = T("onescol", [128, 1], BF16)
    epsv = T("epsv", [128, 1], F32)
    onev = T("onev", [128, 1], F32)
    a_sc = T("a_sc", [STN, 1], F32)
    d_bc = T("d_bc", [128, DM], F32)
    b2r16 = T("b2r16", [1, DM], BF16)
    b1c = T("b1c_s", [128, KH], F32)
    alpha1c = T("alpha1c", [128, KH], F32)
    ct16r = T("ct16r", [128, DM], BF16)
    w2t16 = T("w2t16", [128, KH, DM], BF16)
    hs16all = T("hs16all", [128, 1024], BF16)
    eye_f = TW("eye_f", [128, 128], F32)
    sqj = TW("sqj", [128, 1024], BF16)
    ones1f = TW("ones1f", [1, STN], F32)
    ones1f128 = TW("ones1f128", [1, 128], F32)
    bdel16 = TW("bdel16", [1, DM], BF16)
    bt16 = TW("bt16", [128, KD, STN], BF16)
    wdt16 = TW("wdt16", [128, KD, DM], BF16)
    hscr = TW("hscr", [STN, 2, FT], F32)
    cum16 = TW("cum16", [128, 1024], BF16)
    hinit = TW("hinit", [128, 1], F32)
    agsb = TW("agsb", [2, STN], F32)
    cmsb = TW("cmsb", [2, 1], F32)
    cumscr = TW("cumscr", [STN, 2, FT], F32)
    zer16 = TW("zer16", [STN, FT], F32)
    d_row = TW("d_row", [1, DM], F32)
    alog_s = TW("alog_s", [STN, 1], F32)
    w1q = [stS.enter_context(nc.sbuf_tensor(f"w1q{i}", [128, KD, 512], BF16, side="right")) for i in range(2)]
    ab16 = [stS.enter_context(nc.sbuf_tensor(f"ab16_{i}", [128, 512], BF16, side="right")) for i in range(2)]
    a1r = stS.enter_context(nc.sbuf_tensor("a1r", [1, 512], F32, side="right"))
    a2acc = stS.enter_context(nc.sbuf_tensor("a2acc", [1, DM], F32, side="right"))
    a2bc16 = stS.enter_context(nc.sbuf_tensor("a2bc16", [128, DM], BF16, side="right"))

    pA = P("pA", [128, 1024], F32)
    pB = P("pB", [128, 1024], F32)
    pT1 = P("pT1", [128, 1024], BF16)
    pT2 = P("pT2", [128, 1024], BF16)
    pS1 = P("pS1", [128, 512], F32)
    pS2 = P("pS2", [128, 512], F32)

    pl = Plan()
    A = pl.add

    A("sync", lambda e: e.dma_start(out=eye_f[:], in_=h["eye"][:]), w=["eye_f"])
    A("sync", lambda e: e.dma_start(out=d_row[:], in_=h["dv"][:]), w=["d_row"])
    A("sync", lambda e: e.dma_start(out=b1c[:], in_=h["b1c"][:]), w=["b1c"])
    A("sync", lambda e: e.dma_start(out=alog_s[:], in_=h["alog"][:]), w=["alog"])
    A("gpsimd", lambda e: e.dma_start(out=bdel16[:], in_=h["bdel"][:]), w=["bdel16"])
    A("gpsimd", lambda e: e.dma_start(out=b2r16[:], in_=h["b2v"][:]), w=["b2r16"])
    A("gpsimd", lambda e: e.dma_start(out=bt16[:], in_=h["bt"].rearrange("(k p) n -> p k n", p=128)), w=["bt16"])
    A("gpsimd", lambda e: e.dma_start(out=wdt16[:], in_=h["wdt"].rearrange("(k p) d -> p k d", p=128)), w=["wdt16"])
    A("gpsimd", lambda e: e.dma_start(out=ct16r[0:STN, :], in_=h["ct"][:]), w=["ct16"])
    for bp in (32, 64, 96):
        A("gpsimd", lambda e, bp=bp: e.dma_start(out=ct16r[bp:bp + STN, :], in_=ct16r[0:STN, :]),
          r=["ct16"], w=["ct16"])

    A("vector", lambda e: e.memset(ones16r[:], 1.0), w=["ones16r"])
    A("vector", lambda e: e.memset(ones1f[:], 1.0), w=["ones1f"])
    A("vector", lambda e: e.memset(onescol[:], 1.0), w=["onescol"])
    A("vector", lambda e: e.memset(epsv[:], 1e-5), w=["epsv"])
    A("vector", lambda e: e.memset(onev[:], 1.0), w=["onev"])
    A("vector", lambda e: e.tensor_copy(out=eye16[:], in_=eye_f[:]), r=["eye_f"], w=["eye16"])
    A("scalar", lambda e: e.activation(out=a_sc[:], in_=alog_s[:], func=AF.Exp), r=["alog"], w=["a_sc"])
    A("vector", lambda e: e.tensor_scalar_mul(out=a_sc[:], in0=a_sc[:], scalar1=-1.0 / DM), r=["a_sc"], w=["a_sc"])
    A("vector", lambda e: e.memset(ones1f128[:], 1.0), w=["ones1f128"])
    A("vector", lambda e: e.memset(zer16[:], 0.0), w=["zer16"])
    A("vector", lambda e: e.memset(hs16all[:], 0.0), w=["hsall_init"])
    A("vector", lambda e: e.memset(cum16[:], 0.0), w=["cum_init"])
    A("vector", lambda e: e.memset(hinit[:], 0.0), w=["hinit"])
    A("sync", lambda e: e.dma_start(out=cmsb[:], in_=h["cmask"][:]), w=["cmsb"])
    for half in range(2):
        sl = slice(half * 512, half * 512 + 512)
        A("tensor", lambda e, sl=sl: e.matmul(pA[:, sl], ones1f128[:], d_row[:, sl], start=True, stop=True),
          r=["ones1f128", "d_row"], x=["pA"])
    A("vector", lambda e: e.tensor_copy(out=d_bc[:], in_=pA[:]), w=["d_bc"], x=["pA"])

    setup_steps = []

    def _w2load():
        _w1load(0)
        A("gpsimd", lambda e: e.dma_start(out=w2t16[:], in_=h["w2t"].rearrange("(j p) d -> p j d", p=128)),
          w=["w2raw"])

    def _a2step(g):
        for jj in range(4):
            j = 4 * g + jj
            for half in range(2):
                sl = slice(half * 512, half * 512 + 512)
                A("vector", lambda e, j=j, sl=sl, half=half: e.scalar_tensor_tensor(
                    out=ab16[half][:], in0=w2t16[:, j, sl], scalar=-1.0, in1=w2t16[:, j, sl],
                    op0=ALU.mult, op1=ALU.max),
                  r=["w2raw"], w=[f"ab{half}"])
                A("tensor", lambda e, sl=sl, half=half, s0=(jj == 0), s1=(jj == 3): e.matmul(
                    pA[0:1, sl], onescol[:], ab16[half][:], start=s0, stop=s1),
                  r=[f"ab{half}", "onescol"], x=["pA"])
        if g == 0:
            A("vector", lambda e: e.tensor_copy(out=a2acc[:], in_=pA[0:1, 0:DM]),
              w=["a2acc"], x=["pA"])
        else:
            A("vector", lambda e: e.tensor_add(out=a2acc[:], in0=a2acc[:], in1=pA[0:1, 0:DM]),
              r=["a2acc"], w=["a2acc"], x=["pA"])

    def _w2fold(g):
        for j in range(4 * g, 4 * g + 4):
            A("scalar", lambda e, j=j: e.activation(out=w2t16[:, j, :], in_=w2t16[:, j, :], func=AF.Sign),
              r=["w2raw"], w=["w2raw", f"s2f_{j}"])
        for j in range(4 * g, 4 * g + 4):
            A("vector", lambda e, j=j: e.tensor_mul(out=w2t16[:, j, :], in0=w2t16[:, j, :], in1=a2bc16[:]),
              r=[f"s2f_{j}", "a2bc16"], w=[f"bw2_{j}"])

    def _a2fin():
        for half in range(2):
            sl = slice(half * 512, half * 512 + 512)
            A("tensor", lambda e, sl=sl: e.matmul(pA[:, sl], ones1f128[:], a2acc[0:1, sl], start=True, stop=True),
              r=["ones1f128", "a2acc"], x=["pA"])
        A("vector", lambda e: e.tensor_scalar_mul(out=a2bc16[:], in0=pA[:], scalar1=1.0 / HID),
          w=["a2bc16"], x=["pA"])

    def _a1fin():
        A("vector", lambda e: e.tensor_scalar_mul(out=alpha1c[:], in0=alpha1c[:], scalar1=1.0 / DM),
          r=["alpha1c"], w=["alpha1c"])

    def _w1load(g):
        if g >= 8:
            return
        q, hf = divmod(g, 2)
        A("gpsimd", lambda e, g=g, q=q, hf=hf: e.dma_start(
            out=w1q[g % 2][:], in_=h["w1t"].rearrange("(k p) h -> p k h", p=128)[:, :, q * 1024 + hf * 512:q * 1024 + hf * 512 + 512]),
          w=[f"w1q{g % 2}"])

    def _w1chunk(q, hf):
        g = q * 2 + hf
        _w1load(g + 1)
        wq, wk = w1q[g % 2], f"w1q{g % 2}"
        for k in range(KD):
            A("vector", lambda e, k=k, wq=wq: e.scalar_tensor_tensor(
                out=ab16[k % 2][:], in0=wq[:, k, :], scalar=-1.0, in1=wq[:, k, :],
                op0=ALU.mult, op1=ALU.max),
              r=[wk], w=[f"ab{k % 2}"])
            A("tensor", lambda e, k=k, s0=(k == 0), s1=(k == KD - 1): e.matmul(
                pS1[0:1, 0:512], onescol[:], ab16[k % 2][:], start=s0, stop=s1),
              r=[f"ab{k % 2}", "onescol"], x=["pS1"])
        A("vector", lambda e: e.tensor_copy(out=a1r[:], in_=pS1[0:1, 0:512]), w=["a1r"], x=["pS1"])
        base = q * 8 + hf * 4
        for jj in range(4):
            A("tensor", lambda e, jj=jj: e.matmul(
                pS2[:, jj:jj + 1], a1r[0:1, jj * 128:(jj + 1) * 128], onev[0:1, 0:1],
                start=True, stop=True),
              r=["a1r", "onev"], x=["pS2"])
        A("vector", lambda e, base=base: e.tensor_copy(out=alpha1c[:, base:base + 4], in_=pS2[:, 0:4]),
          w=["alpha1c"], x=["pS2"])
        for k in range(KD):
            A("scalar", lambda e, k=k, wq=wq: e.activation(out=wq[:, k, :], in_=wq[:, k, :], func=AF.Sign),
              r=[wk], w=[wk])
        A("sync", lambda e, q=q, hf=hf, wq=wq: e.dma_start(
            out=bw1d[q][:, :, hf * 4:(hf + 1) * 4, :],
            in_=wq[:, :, :].rearrange("p k (jj h) -> p k jj h", h=128)),
          r=[wk], w=[f"bw1d_{q}" if hf == 1 else f"bw1dh_{q}"])
    setup_steps.append((1, _w2load))
    for g in range(8):
        setup_steps.append((4, lambda q=g // 2, hf=g % 2: _w1chunk(q, hf)))
        setup_steps.append((2, lambda g=g: _a2step(g)))
    setup_steps.append((1, _a2fin))
    setup_steps.append((1, _a1fin))
    for g in range(KH // 4):
        setup_steps.append((3, lambda g=g: _w2fold(g)))

    def pop_steps(budget):
        while setup_steps and budget > 0:
            c, fn = setup_steps.pop(0)
            fn()
            budget -= c

    xb32 = [TF(f"xb32_{i}", [128, 4, 1024], F32) for i in range(2)]
    xn1 = [TF(f"xn1_{i}", [128, 4, 1024], BF16) for i in range(2)]
    xn1T_t = TF("xn1T", [128, KD, FT], BF16)
    xn1T = [xn1T_t, xn1T_t]
    bst = [TF(f"bst_{i}", [128, 4, 12], F32) for i in range(2)]
    bag = [TF(f"bag_{i}", [128, 4, 2], F32) for i in range(2)]
    frv = [TF(f"frv_{i}", [128, 4], F32) for i in range(2)]
    cbuf = TF("cbuf", [128, 1024], F32)
    dcol = [TF(f"dcol_{i}", [128, 4], F32) for i in range(2)]
    drow_t = TF("drow", [1, FT], F32)
    drow = [drow_t, drow_t]
    dA_t = TF("dA", [STN, FT], F32)
    dA_sb = [dA_t, dA_t]
    dBx_t = TF("dBx", [STN, FT], F32)
    dBx = [dBx_t, dBx_t]
    dsb_t = TF("dsb", [STN, FT], F32)
    dsb = [dsb_t, dsb_t]

    def f_load(i):
        if i >= NFr:
            return
        pi = i % 2
        A("sync", lambda e, i=i, pi=pi: e.dma_start(out=xb32[pi][:], in_=x_f[i]), w=[f"xb{pi}"])

    def f_norm(i):
        pi = i % 2
        kx = f"xb{pi}"
        for j in range(4):
            A("vector", lambda e, pi=pi, j=j: e.bn_stats(out=bst[pi][:, j, 0:6], in_=xb32[pi][:, j, 0:512]),
              r=[kx], w=[f"bst{pi}"])
            A("vector", lambda e, pi=pi, j=j: e.bn_stats(out=bst[pi][:, j, 6:12], in_=xb32[pi][:, j, 512:1024]),
              r=[kx], w=[f"bst{pi}"])
            A("vector", lambda e, pi=pi, j=j: e.bn_aggr(out=bag[pi][:, j, :], in_=bst[pi][:, j, :]),
              r=[f"bst{pi}"], w=[f"bag{pi}"])
        A("scalar", lambda e, pi=pi: e.activation(out=frv[pi][:], in_=bag[pi][:, :, 1:2], func=AF.Ln,
                                                  bias=epsv[:, 0:1]),
          r=[f"bag{pi}", "epsv"], w=[f"frv{pi}"])
        A("scalar", lambda e, pi=pi: e.activation(out=frv[pi][:], in_=frv[pi][:], func=AF.Exp, scale=-0.5),
          r=[f"frv{pi}"], w=[f"frv{pi}"])
        for j in range(4):
            A("vector", lambda e, pi=pi, j=j: e.tensor_scalar(
                out=xn1[pi][:, j, :], in0=xb32[pi][:, j, :],
                scalar1=bag[pi][:, j, 0:1], scalar2=frv[pi][:, j:j + 1],
                op0=ALU.subtract, op1=ALU.mult),
              r=[kx, f"bag{pi}", f"frv{pi}"], w=[f"xn1{pi}_{j}"])

    def f_stage_t(i):
        pi = i % 2
        for k in range(KD):
            pt, ptk = (pT1, "pT1") if k % 2 == 0 else (pT2, "pT2")
            for j in range(4):
                A("tensor", lambda e, pi=pi, j=j, k=k, pt=pt: e.transpose(
                    pt[:, j * 128:(j + 1) * 128], xn1[pi][:, j, k * 128:(k + 1) * 128], eye16[:]),
                  r=[f"xn1{pi}_{j}", "eye16"], x=[ptk])
            A("vector", lambda e, pi=pi, k=k, pt=pt: e.tensor_copy(out=xn1T[pi][:, k, :], in_=pt[:, 0:512]),
              w=[f"xn1T_{k}"], x=[ptk])

    def f_stage_b(i):
        pi = i % 2
        for m in range(4):
            pz, pzk = (pA, "pA") if m % 2 == 0 else (pB, "pB")
            for k in range(KD):
                for half in range(2):
                    sl = slice(half * 512, half * 512 + 512)
                    A("tensor", lambda e, pi=pi, m=m, k=k, sl=sl, pz=pz, s0=(k == 0): e.matmul(
                        pz[:, sl], xn1T[pi][:, k, m * 128:(m + 1) * 128], wdt16[:, k, sl],
                        start=s0, stop=False),
                      r=[f"xn1T_{k}", "wdt16"], x=[pzk])
            for half in range(2):
                sl = slice(half * 512, half * 512 + 512)
                A("tensor", lambda e, sl=sl, pz=pz: e.matmul(
                    pz[:, sl], ones16r[:], bdel16[:, sl], start=False, stop=True),
                  r=["ones16r", "bdel16"], x=[pzk])
            A("scalar", lambda e, pz=pz: e.activation(out=cbuf[:], in_=pz[:], func=AF.Exp),
              r=[], w=["cbuf"], x=[pzk])
            A("scalar", lambda e, pi=pi, m=m: e.activation(out=sqj[:], in_=cbuf[:], func=AF.Ln,
                                                           bias=onev[:, 0:1],
                                                           accum_out=dcol[pi][:, m:m + 1]),
              r=["cbuf", "onev"], w=["sqj", f"dcol{pi}"])
        for k in range(KD):
            A("tensor", lambda e, pi=pi, k=k, s0=(k == 0): e.matmul(
                pS1[0:STN, :], bt16[:, k, :], xn1T[pi][:, k, :], start=s0, stop=(k == KD - 1)),
              r=["bt16", f"xn1T_{k}"], x=["pS1"])

    def f_chain(i):
        pi = i % 2
        for m in range(4):
            A("tensor", lambda e, pi=pi, m=m: e.transpose(
                pS2[0:1, m * 128:(m + 1) * 128], dcol[pi][:, m:m + 1], eye_f[:]),
              r=[f"dcol{pi}", "eye_f"], x=["pS2"])
        A("vector", lambda e, pi=pi: e.tensor_copy(out=drow[pi][:], in_=pS2[0:1, 0:FT]),
          w=["drow"], x=["pS2"])
        A("tensor", lambda e, pi=pi: e.matmul(pS2[0:STN, :], ones1f[:], drow[pi][:], start=True, stop=True),
          r=["ones1f", "drow"], x=["pS2"])
        A("scalar", lambda e, pi=pi: e.activation(out=dA_sb[pi][:], in_=pS2[0:STN, :], func=AF.Exp,
                                                  scale=a_sc[:, 0:1]),
          r=["a_sc"], w=["dAk"], x=["pS2"])
        A("vector", lambda e, pi=pi: e.tensor_scalar_mul(out=dsb[pi][:], in0=pS2[0:STN, :], scalar1=1.0 / DM),
          w=["dsb"], x=["pS2"])
        A("vector", lambda e, pi=pi: e.tensor_mul(out=dBx[pi][:], in0=dsb[pi][:], in1=pS1[0:STN, :]),
          r=["dsb"], w=["dBxk"], x=["pS1"])

        def pkey(t):
            return f"hscr{t % 2}"

        def scanfn(e, pi=pi, i=i):
            ini = 0.0 if i == 0 else hscr[:, (i - 1) % 2, FT - 1:FT]
            return e.tensor_tensor_scan(out=hscr[:, i % 2, :], data0=dA_sb[pi][:], data1=dBx[pi][:],
                                        initial=ini, op0=ALU.mult, op1=ALU.add)
        A("vector", scanfn, r=["dAk", "dBxk"] + ([pkey(i - 1)] if i > 0 else []), w=[pkey(i)])

        def cscanfn(e, pi=pi, i=i):
            ini = 1.0 if i == 0 else cumscr[:, (i - 1) % 2, FT - 1:FT]
            return e.tensor_tensor_scan(out=cumscr[:, i % 2, :], data0=dA_sb[pi][:], data1=zer16[:],
                                        initial=ini, op0=ALU.mult, op1=ALU.add)
        A("vector", cscanfn, r=["dAk", "zer16"] + ([f"cumscr{(i - 1) % 2}"] if i > 0 else []),
          w=[f"cumscr{i % 2}"])
        g = i
        A("gpsimd", lambda e, i=i, g=g: e.dma_start(
            out=hs16all[32 * (g % 4):32 * (g % 4) + STN, (g // 4) * FT:(g // 4 + 1) * FT],
            in_=hscr[:, i % 2, :]),
          r=[pkey(i), "hsall_init"], w=[f"hsb{g}"])
        A("gpsimd", lambda e, i=i, g=g: e.dma_start(
            out=cum16[32 * (g % 4):32 * (g % 4) + STN, (g // 4) * FT:(g // 4 + 1) * FT],
            in_=cumscr[:, i % 2, :]),
          r=[f"cumscr{i % 2}", "cum_init"], w=[f"cum_{g}"])

    nsteps = sum(c for c, _ in setup_steps)
    per_tile = -(-nsteps // max(NFr - 1, 1)) + 1
    f_load(0)
    f_load(1)
    f_norm(0)
    f_stage_t(0)
    for i in range(NFr):
        f_load(i + 2)
        if i + 1 < NFr:
            f_norm(i + 1)
        f_stage_b(i)
        if i + 1 < NFr:
            f_stage_t(i + 1)
        f_chain(i)
        pop_steps(per_tile if i > 0 else 1)
    pop_steps(10 ** 9)

    A("sync", lambda e: e.dma_start(out=agin.rearrange("n 1 -> n 1"), in_=hscr[:, (NFr - 1) % 2, FT - 1:FT]),
      r=[f"hscr{(NFr - 1) % 2}"], w=["agin_d"])
    A("gpsimd", lambda e: e.collective_compute(
        "AllGather", ALU.bypass, CC_GROUPS, ins=[agin[:, :]], outs=[agout[:, :]]),
      r=["agin_d"], w=["agout_d"], skip_inc=True)
    A("sync", lambda e: e.dma_start(out=agsb[:], in_=agout.rearrange("(r n) 1 -> r n", n=STN)),
      r=["agout_d"], w=["agsb"])
    A("tensor", lambda e: e.matmul(pS2[0:STN, 0:1], agsb[:], cmsb[:], start=True, stop=True),
      r=["agsb", "cmsb"], x=["pS2"])
    A("vector", lambda e: e.tensor_copy(out=hinit[0:STN, :], in_=pS2[0:STN, 0:1]),
      r=["hinit"], w=["hinit"], x=["pS2"])
    for bp in (32, 64, 96):
        A("sync", lambda e, bp=bp: e.dma_start(out=hinit[bp:bp + STN, :], in_=hinit[0:STN, :]),
          r=["hinit"], w=["hinit"])
    A("vector", lambda e: e.scalar_tensor_tensor(
        out=hs16all[:], in0=cum16[:], scalar=hinit[:, 0:1], in1=hs16all[:],
        op0=ALU.mult, op1=ALU.add),
      r=["hinit", "cum_init", "hsall_init"] + [f"cum_{g}" for g in range(NFr)] + [f"hsb{g}" for g in range(NFr)],
      w=[f"hsb{g}" for g in range(NFr)])

    pl.barrier()
    stS.close()
    stF.close()
    stW.close()
    MB = 4
    BTK = 512
    NBT = OWN // BTK
    OFF = 0
    x_b = h["x"].rearrange("(i j p) d -> i p j d", j=MB, p=128)
    out_b = out_h.rearrange("(u j p) d -> u p j d", j=MB, p=128)

    xf = [T(f"xf_{i}", [128, MB, 1024], F32) for i in range(2)]
    yp = [T(f"yp_{i}", [128, MB, 1024], F32) for i in range(2)]
    xn3 = T("xn3", [128, MB, 1024], BF16)
    xn3T = T("xn3T", [128, KD, BTK], BF16)
    h_sb = T("h_sb", [128, KH, BTK], BF16)
    scr = T("scr", [128, 1024], F32)
    w1buf = [T(f"w1buf_{i}", [128, KD, 4, 128], BF16) for i in range(2)]
    bbst = T("bbst", [128, MB, 12], F32)
    bbag = T("bbag", [128, MB, 2], F32)
    bbst4 = T("bbst4", [128, MB, 12], F32)
    bag4 = T("bag4", [128, MB, 2], F32)
    brv = {n: T(f"brv{n}", [128, MB], F32) for n in (1, 2, 3, 4)}

    def b_load(c):
        if c < NBT:
            for m in range(MB):
                A("sync", lambda e, c=c, m=m: e.dma_start(out=xf[c % 2][:, m, :], in_=x_b[OFF + c][:, m, :]),
                  w=[f"xf{c % 2}m{m}"])

    def b_stream(u, s):
        g, hf = divmod(s, 2)
        A("sync", lambda e, s=s, g=g, hf=hf: e.dma_start(
            out=w1buf[s % 2][:], in_=bw1d[g][:, :, hf * 4:(hf + 1) * 4, :]),
          r=[f"bw1d_{g}", f"bw1dh_{g}"], w=[f"w1b{s % 2}"])

    def b_ln(src_fn, src_keys, rvk, n):
        for m in range(MB):
            sk = src_keys(m) if callable(src_keys) else src_keys
            A("vector", lambda e, m=m: e.bn_stats(out=bbst[:, m, 0:6], in_=src_fn(m)[:, 0:512]),
              r=sk, w=["bbst"])
            A("vector", lambda e, m=m: e.bn_stats(out=bbst[:, m, 6:12], in_=src_fn(m)[:, 512:1024]),
              r=sk, w=["bbst"])
            A("vector", lambda e, m=m: e.bn_aggr(out=bbag[:, m, :], in_=bbst[:, m, :]),
              r=["bbst"], w=["bbag"])
        A("scalar", lambda e, rvk=rvk: e.activation(out=brv[n][:], in_=bbag[:, :, 1:2], func=AF.Sqrt,
                                                    bias=epsv[:, 0:1]),
          r=["bbag", "epsv"], w=[rvk])
        A("vector", lambda e, rvk=rvk: e.reciprocal(out=brv[n][:], in_=brv[n][:]),
          r=[rvk], w=[rvk])

    def b_chain_a(c):
        pc = c % 2
        b_ln(lambda m: xf[pc][:, m, :], lambda m: [f"xf{pc}m{m}"], "brv1", 1)

    def b_chain_b(c):
        pc = c % 2
        bp = 32 * (c % 4)
        cb = (c // 4) * FT
        for m in range(MB):
            pz, pzk = (pA, "pA") if m % 2 == 0 else (pB, "pB")
            for half in range(2):
                sl = slice(half * 512, half * 512 + 512)
                A("tensor", lambda e, bp=bp, cb=cb, m=m, sl=sl, pz=pz: e.matmul(
                    pz[:, sl], hs16all[bp:bp + STN, cb + m * 128:cb + (m + 1) * 128],
                    ct16r[bp:bp + STN, sl],
                    start=True, stop=True, tile_position=(bp, 0)),
                  r=[f"hsb{c}", "ct16"], x=[pzk])
            A("vector", lambda e, pc=pc, m=m: e.tensor_scalar(out=scr[:], in0=xf[pc][:, m, :],
                                                              scalar1=bbag[:, m, 0:1],
                                                              scalar2=brv[1][:, m:m + 1],
                                                              op0=ALU.subtract, op1=ALU.mult),
              r=[f"xf{pc}m{m}", "bbag", "brv1"], w=["scr"])
            A("vector", lambda e: e.tensor_mul(out=scr[:], in0=scr[:], in1=d_bc[:]),
              r=["scr", "d_bc"], w=["scr"])
            A("vector", lambda e, pc=pc, m=m, pz=pz: e.tensor_add(out=yp[pc][:, m, :], in0=scr[:], in1=pz[:]),
              r=["scr"], w=[f"yp{pc}"], x=[pzk])
        b_ln(lambda m: yp[pc][:, m, :], [f"yp{pc}"], "brv2", 2)

    def b_chain_c(c):
        pc = c % 2
        for m in range(MB):
            A("vector", lambda e, pc=pc, m=m: e.tensor_scalar(out=scr[:], in0=yp[pc][:, m, :],
                                                              scalar1=bbag[:, m, 0:1],
                                                              scalar2=brv[2][:, m:m + 1],
                                                              op0=ALU.subtract, op1=ALU.mult),
              r=[f"yp{pc}", "bbag", "brv2"], w=["scr"])
            A("vector", lambda e, pc=pc, m=m: e.tensor_add(out=xf[pc][:, m, :], in0=scr[:], in1=xf[pc][:, m, :]),
              r=["scr", f"xf{pc}m{m}"], w=[f"xf{pc}m{m}"])
        b_ln(lambda m: xf[pc][:, m, :], lambda m: [f"xf{pc}m{m}"], "brv3", 3)

    def b_chain_d(c):
        pc = c % 2
        for m in range(MB):
            A("vector", lambda e, pc=pc, m=m: e.tensor_scalar(out=xn3[:, m, :], in0=xf[pc][:, m, :],
                                                              scalar1=bbag[:, m, 0:1],
                                                              scalar2=brv[3][:, m:m + 1],
                                                              op0=ALU.subtract, op1=ALU.mult),
              r=[f"xf{pc}m{m}", "bbag", "brv3"], w=[f"xn3_{m}"])

    def b_transposes(u):
        for k in range(KD):
            pt, ptk = (pT1, "pT1") if k % 2 == 0 else (pT2, "pT2")
            for j in range(MB):
                A("tensor", lambda e, j=j, k=k, pt=pt: e.transpose(
                    pt[:, j * 128:(j + 1) * 128], xn3[:, j, k * 128:(k + 1) * 128], eye16[:]),
                  r=[f"xn3_{j}", "eye16"], x=[ptk])
            A("vector", lambda e, k=k, pt=pt: e.tensor_copy(out=xn3T[:, k, :], in_=pt[:, 0:BTK]),
              w=[f"xn3T_{k}"], x=[ptk])

    def b_mm1_group(u, grp):
        for j in range(8 * grp, 8 * grp + 8):
            s = j // 4
            if j % 4 == 0 and s + 1 < 8 and j > 0:
                b_stream(u, s + 1)
            ph, phk = (pS1, "pS1") if j % 2 == 0 else (pS2, "pS2")
            for k in range(KD):
                A("tensor", lambda e, s=s, j=j, k=k, ph=ph, s0=(k == 0): e.matmul(
                    ph[:, 0:BTK], w1buf[s % 2][:, k, j % 4, :], xn3T[:, k, :],
                    start=s0, stop=(k == KD - 1)),
                  r=[f"w1b{s % 2}", f"xn3T_{k}"], x=[phk])
            A("scalar", lambda e, j=j, ph=ph: e.activation(
                out=h_sb[:, j, :], in_=ph[:, 0:BTK], func=AF.Gelu,
                bias=b1c[:, j:j + 1], scale=alpha1c[:, j:j + 1]),
              r=["b1c", "alpha1c"], w=[f"h_{j}"], x=[phk])

    def b_mm2(u):
        pc = u % 2
        for m in range(MB):
            pz, pzk = (pA, "pA") if m % 2 == 0 else (pB, "pB")
            for j in range(KH):
                for half in range(2):
                    sl = slice(half * 512, half * 512 + 512)
                    A("tensor", lambda e, m=m, j=j, sl=sl, pz=pz, s0=(j == 0): e.matmul(
                        pz[:, sl], h_sb[:, j, m * 128:(m + 1) * 128], w2t16[:, j, sl],
                        start=s0, stop=False),
                      r=[f"h_{j}", f"bw2_{j}"], x=[pzk])
            for half in range(2):
                sl = slice(half * 512, half * 512 + 512)
                A("tensor", lambda e, sl=sl, pz=pz: e.matmul(
                    pz[:, sl], ones16r[:], b2r16[:, sl], start=False, stop=True),
                  r=["ones16r", "b2r16"], x=[pzk])
            A("vector", lambda e, m=m, pz=pz: e.bn_stats(out=bbst4[:, m, 0:6], in_=pz[:, 0:512]),
              w=["bbst4"], x=[pzk])
            A("vector", lambda e, m=m, pz=pz: e.bn_stats(out=bbst4[:, m, 6:12], in_=pz[:, 512:1024]),
              w=["bbst4"], x=[pzk])
            A("vector", lambda e, m=m: e.bn_aggr(out=bag4[:, m, :], in_=bbst4[:, m, :]),
              r=["bbst4"], w=["bag4"])
            A("scalar", lambda e, pc=pc, m=m, pz=pz: e.copy(out=yp[pc][:, m, :], in_=pz[:]),
              w=[f"yp{pc}"], x=[pzk])
        A("scalar", lambda e: e.activation(out=brv[4][:], in_=bag4[:, :, 1:2], func=AF.Sqrt,
                                           bias=epsv[:, 0:1]),
          r=["bag4", "epsv"], w=["brv4"])
        A("vector", lambda e: e.reciprocal(out=brv[4][:], in_=brv[4][:]),
          r=["brv4"], w=["brv4"])

    def b_tail_out(u):
        pc = u % 2
        for m in range(MB):
            A("vector", lambda e, pc=pc, m=m: e.tensor_scalar(out=scr[:], in0=yp[pc][:, m, :],
                                                              scalar1=bag4[:, m, 0:1],
                                                              scalar2=brv[4][:, m:m + 1],
                                                              op0=ALU.subtract, op1=ALU.mult),
              r=[f"yp{pc}", "bag4", "brv4"], w=["scr"])
            A("vector", lambda e, pc=pc, m=m: e.tensor_add(out=xf[pc][:, m, :], in0=scr[:], in1=xf[pc][:, m, :]),
              r=["scr", f"xf{pc}m{m}"], w=[f"xf{pc}m{m}"])
            A("gpsimd", lambda e, u=u, pc=pc, m=m: e.dma_start(out=out_b[u][:, m, :], in_=xf[pc][:, m, :]),
              r=[f"xf{pc}m{m}"], w=[f"out_{u}_{m}"])

    b_load(0)
    b_stream(0, 0)
    b_stream(0, 1)
    b_chain_a(0)
    b_chain_b(0)
    b_chain_c(0)
    b_chain_d(0)
    b_transposes(0)
    for u in range(NBT):
        if u > 0:
            b_tail_out(u - 1)
        b_load(u + 1)
        b_mm1_group(u, 0)
        if u + 1 < NBT:
            b_chain_a(u + 1)
        b_mm1_group(u, 1)
        if u + 1 < NBT:
            b_chain_b(u + 1)
        b_mm1_group(u, 2)
        if u + 1 < NBT:
            b_chain_c(u + 1)
        b_mm1_group(u, 3)
        if u + 1 < NBT:
            b_chain_d(u + 1)
            b_transposes(u + 1)
            b_stream(u + 1, 0)
            b_stream(u + 1, 1)
        b_mm2(u)
    b_tail_out(NBT - 1)

    sems, emit_engine = pl.emit(nc, st)
    with nc.Block() as block:
        @block.sync
        def _(e):
            emit_engine(e, "sync")

        @block.vector
        def _(e):
            emit_engine(e, "vector")

        @block.scalar
        def _(e):
            emit_engine(e, "scalar")

        @block.tensor
        def _(e):
            emit_engine(e, "tensor")

        @block.gpsimd
        def _(e):
            emit_engine(e, "gpsimd")

    st.close()
    return nc


_NC_CACHE = []


def kernel(L0_attn_x, L0_attn_A_log, L0_attn_B, L0_attn_C, L0_attn_D,
           L0_attn_W_delta, L0_attn_b_delta, L0_ffn_w1, L0_ffn_b1,
           L0_ffn_w2, L0_ffn_b2):
    x = np.asarray(L0_attn_x, dtype=np.float32)
    wdt = np.ascontiguousarray(np.asarray(L0_attn_W_delta, np.float32).T)
    bt = np.ascontiguousarray(np.asarray(L0_attn_B, np.float32).T)
    ct = np.ascontiguousarray(np.asarray(L0_attn_C, np.float32).T)
    dv = np.ascontiguousarray(np.asarray(L0_attn_D, np.float32).reshape(1, DM))
    bdel = np.ascontiguousarray(np.asarray(L0_attn_b_delta, np.float32).reshape(1, DM))
    alog = np.ascontiguousarray(np.asarray(L0_attn_A_log, np.float32).reshape(STN, 1))
    w1t = np.ascontiguousarray(np.asarray(L0_ffn_w1, np.float32).T)
    b1c = np.ascontiguousarray(np.asarray(L0_ffn_b1, np.float32).reshape(KH, 128).T)
    w2t = np.ascontiguousarray(np.asarray(L0_ffn_w2, np.float32).T)
    b2v = np.ascontiguousarray(np.asarray(L0_ffn_b2, np.float32).reshape(1, DM))
    eye = np.eye(128, dtype=np.float32)

    if not _NC_CACHE:
        _NC_CACHE.append(build_nc())
    nc = _NC_CACHE[0]

    in_maps = []
    for c in range(8):
        b, hf = divmod(c, 2)
        xfr = np.ascontiguousarray(x[b, hf * OWN:(hf + 1) * OWN])
        cmask = np.asarray([[1.0], [0.0]], np.float32) if hf == 1 else np.zeros((2, 1), np.float32)
        in_maps.append(dict(x=xfr, cmask=cmask, wdt=wdt, bt=bt, ct=ct, dv=dv,
                            bdel=bdel, alog=alog, w1t=w1t, b1c=b1c, w2t=w2t, b2v=b2v, eye=eye,
                            w1n=np.asarray(L0_ffn_w1, np.float32), w2n=np.asarray(L0_ffn_w2, np.float32)))
    res = run_bass_kernel_spmd(nc, in_maps, core_ids=list(range(8)))
    out = np.empty((4, SEQ, DM), np.float32)
    for c, r in enumerate(res.results):
        b, hf = divmod(c, 2)
        out[b, hf * OWN:(hf + 1) * OWN] = r["out"]
    return out


# version 43
# speedup vs baseline: 1.0792x; 1.0455x over previous
import contextlib
import numpy as np
import concourse.bass as bass
import concourse.mybir as mybir
from concourse.bass_utils import run_bass_kernel_spmd

F32 = mybir.dt.float32
BF16 = mybir.dt.bfloat16
AF = mybir.ActivationFunctionType
ALU = mybir.AluOpType
AX = mybir.AxisListType

ENGS = ("tensor", "vector", "scalar", "sync", "gpsimd")
DMA_Q = {"sync": 4, "gpsimd": 4}

SEQ, OWN, DM, STN, HID = 8192, 4096, 1024, 16, 4096
CC_GROUPS = [[0, 1], [2, 3], [4, 5], [6, 7]]
FT, NF = 512, 16
BT, NB = 512, 8
KD = 8
KH = 32


class Plan:

    def __init__(self):
        self.ops = {e: [] for e in ENGS}
        self.lw = {}
        self.rd = {}
        self.xa = {}
        self._bar = {}
        self._bar_left = set()

    def barrier(self):
        snap = []
        for e in ENGS:
            n = len(self.ops[e])
            if n == 0:
                continue
            if e in DMA_Q:
                nq = DMA_Q[e]
                for q in range(min(nq, n)):
                    i = n - 1 - ((n - 1 - q) % nq)
                    i = max(j for j in range(n) if j % nq == q)
                    snap.append((e, i))
            else:
                self.ops[e][-1]["inc"] = True
                snap.append((e, n - 1))
        self._bar = snap
        self._bar_left = set(ENGS)

    def add(self, eng, fn, r=(), w=(), x=(), skip_inc=False):
        ops = self.ops[eng]
        idx = len(ops)
        waits = {}

        def cover(tok, same_ok):
            if tok is None:
                return
            te, ti = tok
            if same_ok and te == eng and te not in DMA_Q:
                return
            q = ti % DMA_Q[te] if te in DMA_Q else 0
            if waits.get((te, q), -1) < ti:
                waits[(te, q)] = ti

        if eng in self._bar_left and self._bar:
            for te, ti in self._bar:
                if te != eng or te in DMA_Q:
                    cover((te, ti), same_ok=True)
            self._bar_left.discard(eng)
        for k in r:
            cover(self.lw.get(k), same_ok=False)
        for k in w:
            cover(self.lw.get(k), same_ok=False)
            for te, ti in self.rd.get(k, {}).items():
                cover((te, ti), same_ok=False)
        for k in x:
            for te, ti in self.xa.get(k, {}).items():
                if te != eng:
                    cover((te, ti), same_ok=True)
        for (te, _q), ti in waits.items():
            self.ops[te][ti]["inc"] = True
        ops.append(dict(fn=fn, waits=waits, inc=False, skip=skip_inc))
        for k in r:
            self.rd.setdefault(k, {})[eng] = idx
        for k in w:
            self.lw[k] = (eng, idx)
            self.rd[k] = {}
        for k in x:
            self.xa.setdefault(k, {})[eng] = idx
        return idx

    def emit(self, nc, stack):
        sems = {}
        for e in ("tensor", "vector", "scalar"):
            sems[e] = {0: stack.enter_context(nc.semaphore(f"P_{e}"))}
        for e, nq in DMA_Q.items():
            sems[e] = {i: stack.enter_context(nc.semaphore(f"P_{e}{i}")) for i in range(nq)}
        sems["gpsimd"]["cc"] = stack.enter_context(nc.semaphore("P_cc"))
        semsel = {e: [None] * len(self.ops[e]) for e in ENGS}
        val_after = {e: [0] * len(self.ops[e]) for e in ENGS}
        for e in ENGS:
            if e in DMA_Q:
                nq = DMA_Q[e]
                counts = [0] * nq
                cnt = 0
                ccn = 0
                for i, op in enumerate(self.ops[e]):
                    if op.get("skip"):
                        op["inc"] = "cc"
                        ccn += 1
                        semsel[e][i] = "cc"
                        val_after[e][i] = ("cc", ccn)
                        continue
                    q = cnt % nq
                    cnt += 1
                    counts[q] += 16
                    op["inc"] = True
                    semsel[e][i] = q
                    val_after[e][i] = (q, counts[q])
            else:
                c = 0
                for i, op in enumerate(self.ops[e]):
                    if op["inc"]:
                        c += 1
                        semsel[e][i] = 0
                    val_after[e][i] = (0, c)

        def emit_engine(handle, e):
            waited = {}
            dma_i = 0
            for i, op in enumerate(self.ops[e]):
                if e in DMA_Q and not op.get("skip"):
                    if dma_i >= DMA_Q[e]:
                        q = dma_i % DMA_Q[e]
                        v = 16 * (dma_i // DMA_Q[e])
                        if waited.get((e, q), -1) < v:
                            handle.wait_ge(sems[e][q], v)
                            waited[(e, q)] = v
                    dma_i += 1
                for (te, _q), ti in sorted(op["waits"].items()):
                    q, v = val_after[te][ti]
                    keyq = (te, q)
                    if waited.get(keyq, -1) < v:
                        handle.wait_ge(sems[te][q], v)
                        waited[keyq] = v
                ins = op["fn"](handle)
                if op["inc"] == "cc":
                    ins.then_inc(sems[e]["cc"], 1)
                elif op["inc"]:
                    q = semsel[e][i]
                    ins.then_inc(sems[e][q], 16 if e in DMA_Q else 1)
            if e in DMA_Q:
                finals = {}
                for i, op in enumerate(self.ops[e]):
                    if op.get("inc") and semsel[e][i] is not None:
                        q, v = val_after[e][i]
                        finals[q] = max(finals.get(q, 0), v)
                for q, v in sorted(finals.items(), key=lambda kv: str(kv[0])):
                    handle.wait_ge(sems[e][q], v)
        return sems, emit_engine


def build_nc():
    nc = bass.Bass("TRN2", target_bir_lowering=False)
    h = {}
    h["x"] = nc.dram_tensor("x", [OWN, DM], F32, kind="ExternalInput").ap()
    h["cmask"] = nc.dram_tensor("cmask", [2, 1], F32, kind="ExternalInput").ap()
    agin = nc.dram_tensor("agin", [STN, 1], F32, kind="Internal").ap()
    agout = nc.dram_tensor("agout", [2 * STN, 1], F32, kind="Internal").ap()
    h["wdt"] = nc.dram_tensor("wdt", [DM, DM], F32, kind="ExternalInput").ap()
    h["bt"] = nc.dram_tensor("bt", [DM, STN], F32, kind="ExternalInput").ap()
    h["ct"] = nc.dram_tensor("ct", [STN, DM], F32, kind="ExternalInput").ap()
    h["dv"] = nc.dram_tensor("dv", [1, DM], F32, kind="ExternalInput").ap()
    h["bdel"] = nc.dram_tensor("bdel", [1, DM], F32, kind="ExternalInput").ap()
    h["alog"] = nc.dram_tensor("alog", [STN, 1], F32, kind="ExternalInput").ap()
    h["w1t"] = nc.dram_tensor("w1t", [DM, HID], F32, kind="ExternalInput").ap()
    h["b1c"] = nc.dram_tensor("b1c", [128, KH], F32, kind="ExternalInput").ap()
    h["w2t"] = nc.dram_tensor("w2t", [HID, DM], F32, kind="ExternalInput").ap()
    h["b2v"] = nc.dram_tensor("b2v", [1, DM], F32, kind="ExternalInput").ap()
    h["eye"] = nc.dram_tensor("eye", [128, 128], F32, kind="ExternalInput").ap()
    h["w1n"] = nc.dram_tensor("w1n", [HID, DM], F32, kind="ExternalInput").ap()
    h["w2n"] = nc.dram_tensor("w2n", [DM, HID], F32, kind="ExternalInput").ap()
    out_h = nc.dram_tensor("out", [OWN, DM], F32, kind="ExternalOutput").ap()
    bw1d = nc.dram_tensor("bw1d", [4, 128, KD, 8, 128], BF16, kind="Internal").ap()

    NFr = OWN // FT
    x_f = h["x"].rearrange("(i j p) d -> i p j d", j=4, p=128)
    out_t = out_h.rearrange("(u j p) d -> u p j d", j=4, p=128)

    st = contextlib.ExitStack()
    stW = contextlib.ExitStack()
    stS = contextlib.ExitStack()
    stF = contextlib.ExitStack()
    T = lambda *a: st.enter_context(nc.sbuf_tensor(*a))
    TW = lambda *a: stW.enter_context(nc.sbuf_tensor(*a))
    TS = lambda *a: stS.enter_context(nc.sbuf_tensor(*a))
    TF = lambda *a: stF.enter_context(nc.sbuf_tensor(*a))
    P = lambda *a: st.enter_context(nc.psum_tensor(*a))

    eye16 = T("eye16", [128, 128], BF16)
    ones16r = T("ones16r", [1, 128], BF16)
    onescol = T("onescol", [128, 1], BF16)
    epsv = T("epsv", [128, 1], F32)
    onev = T("onev", [128, 1], F32)
    a_sc = T("a_sc", [STN, 1], F32)
    d_bc = T("d_bc", [128, DM], F32)
    b2r16 = T("b2r16", [1, DM], BF16)
    b1c = T("b1c_s", [128, KH], F32)
    alpha1c = T("alpha1c", [128, KH], F32)
    ct16r = T("ct16r", [128, DM], BF16)
    w2t16 = T("w2t16", [128, KH, DM], BF16)
    hs16all = T("hs16all", [128, 1024], BF16)
    eye_f = TW("eye_f", [128, 128], F32)
    sqj = TW("sqj", [128, 1024], BF16)
    ones1f = TW("ones1f", [1, STN], F32)
    ones1f128 = TW("ones1f128", [1, 128], F32)
    bdel16 = TW("bdel16", [1, DM], BF16)
    bt16 = TW("bt16", [128, KD, STN], BF16)
    wdt16 = TW("wdt16", [128, KD, DM], BF16)
    hscr = TW("hscr", [STN, 2, FT], F32)
    cum16 = TW("cum16", [128, 1024], BF16)
    hinit = TW("hinit", [128, 1], F32)
    agsb = TW("agsb", [2, STN], F32)
    cmsb = TW("cmsb", [2, 1], F32)
    cumscr = TW("cumscr", [STN, 2, FT], F32)
    zer16 = TW("zer16", [STN, FT], F32)
    d_row = TW("d_row", [1, DM], F32)
    alog_s = TW("alog_s", [STN, 1], F32)
    w1q = [stS.enter_context(nc.sbuf_tensor(f"w1q{i}", [128, KD, 512], BF16, side="right")) for i in range(2)]
    ab16 = [stS.enter_context(nc.sbuf_tensor(f"ab16_{i}", [128, 512], BF16, side="right")) for i in range(2)]
    a1r = stS.enter_context(nc.sbuf_tensor("a1r", [1, 512], F32, side="right"))
    a2acc = stS.enter_context(nc.sbuf_tensor("a2acc", [1, DM], F32, side="right"))
    a2bc16 = stS.enter_context(nc.sbuf_tensor("a2bc16", [128, DM], BF16, side="right"))

    pA = P("pA", [128, 1024], F32)
    pB = P("pB", [128, 1024], F32)
    pT1 = P("pT1", [128, 1024], BF16)
    pT2 = P("pT2", [128, 1024], BF16)
    pS1 = P("pS1", [128, 512], F32)
    pS2 = P("pS2", [128, 512], F32)

    pl = Plan()
    A = pl.add

    A("sync", lambda e: e.dma_start(out=eye_f[:], in_=h["eye"][:]), w=["eye_f"])
    A("sync", lambda e: e.dma_start(out=d_row[:], in_=h["dv"][:]), w=["d_row"])
    A("sync", lambda e: e.dma_start(out=b1c[:], in_=h["b1c"][:]), w=["b1c"])
    A("sync", lambda e: e.dma_start(out=alog_s[:], in_=h["alog"][:]), w=["alog"])
    A("gpsimd", lambda e: e.dma_start(out=bdel16[:], in_=h["bdel"][:]), w=["bdel16"])
    A("gpsimd", lambda e: e.dma_start(out=b2r16[:], in_=h["b2v"][:]), w=["b2r16"])
    A("gpsimd", lambda e: e.dma_start(out=bt16[:], in_=h["bt"].rearrange("(k p) n -> p k n", p=128)), w=["bt16"])
    A("gpsimd", lambda e: e.dma_start(out=wdt16[:], in_=h["wdt"].rearrange("(k p) d -> p k d", p=128)), w=["wdt16"])
    A("gpsimd", lambda e: e.dma_start(out=ct16r[0:STN, :], in_=h["ct"][:]), w=["ct16"])
    for bp in (32, 64, 96):
        A("gpsimd", lambda e, bp=bp: e.dma_start(out=ct16r[bp:bp + STN, :], in_=ct16r[0:STN, :]),
          r=["ct16"], w=["ct16"])

    A("vector", lambda e: e.memset(ones16r[:], 1.0), w=["ones16r"])
    A("vector", lambda e: e.memset(ones1f[:], 1.0), w=["ones1f"])
    A("vector", lambda e: e.memset(onescol[:], 1.0), w=["onescol"])
    A("vector", lambda e: e.memset(epsv[:], 1e-5), w=["epsv"])
    A("vector", lambda e: e.memset(onev[:], 1.0), w=["onev"])
    A("vector", lambda e: e.tensor_copy(out=eye16[:], in_=eye_f[:]), r=["eye_f"], w=["eye16"])
    A("scalar", lambda e: e.activation(out=a_sc[:], in_=alog_s[:], func=AF.Exp), r=["alog"], w=["a_sc"])
    A("vector", lambda e: e.tensor_scalar_mul(out=a_sc[:], in0=a_sc[:], scalar1=-1.0 / DM), r=["a_sc"], w=["a_sc"])
    A("vector", lambda e: e.memset(ones1f128[:], 1.0), w=["ones1f128"])
    A("vector", lambda e: e.memset(zer16[:], 0.0), w=["zer16"])
    A("vector", lambda e: e.memset(hs16all[:], 0.0), w=["hsall_init"])
    A("vector", lambda e: e.memset(cum16[:], 0.0), w=["cum_init"])
    A("vector", lambda e: e.memset(hinit[:], 0.0), w=["hinit"])
    A("sync", lambda e: e.dma_start(out=cmsb[:], in_=h["cmask"][:]), w=["cmsb"])
    for half in range(2):
        sl = slice(half * 512, half * 512 + 512)
        A("tensor", lambda e, sl=sl: e.matmul(pA[:, sl], ones1f128[:], d_row[:, sl], start=True, stop=True),
          r=["ones1f128", "d_row"], x=["pA"])
    A("vector", lambda e: e.tensor_copy(out=d_bc[:], in_=pA[:]), w=["d_bc"], x=["pA"])

    setup_steps = []

    def _w2load():
        _w1load(0)
        A("gpsimd", lambda e: e.dma_start(out=w2t16[:], in_=h["w2t"].rearrange("(j p) d -> p j d", p=128)),
          w=["w2raw"])

    def _a2step(g):
        for jj in range(4):
            j = 4 * g + jj
            for half in range(2):
                sl = slice(half * 512, half * 512 + 512)
                A("vector", lambda e, j=j, sl=sl, half=half: e.scalar_tensor_tensor(
                    out=ab16[half][:], in0=w2t16[:, j, sl], scalar=-1.0, in1=w2t16[:, j, sl],
                    op0=ALU.mult, op1=ALU.max),
                  r=["w2raw"], w=[f"ab{half}"])
                A("tensor", lambda e, sl=sl, half=half, s0=(jj == 0), s1=(jj == 3): e.matmul(
                    pA[0:1, sl], onescol[:], ab16[half][:], start=s0, stop=s1),
                  r=[f"ab{half}", "onescol"], x=["pA"])
        if g == 0:
            A("vector", lambda e: e.tensor_copy(out=a2acc[:], in_=pA[0:1, 0:DM]),
              w=["a2acc"], x=["pA"])
        else:
            A("vector", lambda e: e.tensor_add(out=a2acc[:], in0=a2acc[:], in1=pA[0:1, 0:DM]),
              r=["a2acc"], w=["a2acc"], x=["pA"])

    def _w2fold(g):
        for j in range(4 * g, 4 * g + 4):
            A("scalar", lambda e, j=j: e.activation(out=w2t16[:, j, :], in_=w2t16[:, j, :], func=AF.Sign),
              r=["w2raw"], w=["w2raw", f"s2f_{j}"])
        for j in range(4 * g, 4 * g + 4):
            A("vector", lambda e, j=j: e.tensor_mul(out=w2t16[:, j, :], in0=w2t16[:, j, :], in1=a2bc16[:]),
              r=[f"s2f_{j}", "a2bc16"], w=[f"bw2_{j}"])

    def _a2fin():
        for half in range(2):
            sl = slice(half * 512, half * 512 + 512)
            A("tensor", lambda e, sl=sl: e.matmul(pA[:, sl], ones1f128[:], a2acc[0:1, sl], start=True, stop=True),
              r=["ones1f128", "a2acc"], x=["pA"])
        A("vector", lambda e: e.tensor_scalar_mul(out=a2bc16[:], in0=pA[:], scalar1=1.0 / HID),
          w=["a2bc16"], x=["pA"])

    def _a1fin():
        A("vector", lambda e: e.tensor_scalar_mul(out=alpha1c[:], in0=alpha1c[:], scalar1=1.0 / DM),
          r=["alpha1c"], w=["alpha1c"])

    def _w1load(g):
        if g >= 8:
            return
        q, hf = divmod(g, 2)
        A("gpsimd", lambda e, g=g, q=q, hf=hf: e.dma_start(
            out=w1q[g % 2][:], in_=h["w1t"].rearrange("(k p) h -> p k h", p=128)[:, :, q * 1024 + hf * 512:q * 1024 + hf * 512 + 512]),
          w=[f"w1q{g % 2}"])

    def _w1chunk(q, hf):
        g = q * 2 + hf
        _w1load(g + 1)
        wq, wk = w1q[g % 2], f"w1q{g % 2}"
        for k in range(KD):
            A("vector", lambda e, k=k, wq=wq: e.scalar_tensor_tensor(
                out=ab16[k % 2][:], in0=wq[:, k, :], scalar=-1.0, in1=wq[:, k, :],
                op0=ALU.mult, op1=ALU.max),
              r=[wk], w=[f"ab{k % 2}"])
            A("tensor", lambda e, k=k, s0=(k == 0), s1=(k == KD - 1): e.matmul(
                pS1[0:1, 0:512], onescol[:], ab16[k % 2][:], start=s0, stop=s1),
              r=[f"ab{k % 2}", "onescol"], x=["pS1"])
        A("vector", lambda e: e.tensor_copy(out=a1r[:], in_=pS1[0:1, 0:512]), w=["a1r"], x=["pS1"])
        base = q * 8 + hf * 4
        for jj in range(4):
            A("tensor", lambda e, jj=jj: e.matmul(
                pS2[:, jj:jj + 1], a1r[0:1, jj * 128:(jj + 1) * 128], onev[0:1, 0:1],
                start=True, stop=True),
              r=["a1r", "onev"], x=["pS2"])
        A("vector", lambda e, base=base: e.tensor_copy(out=alpha1c[:, base:base + 4], in_=pS2[:, 0:4]),
          w=["alpha1c"], x=["pS2"])
        for k in range(KD):
            A("scalar", lambda e, k=k, wq=wq: e.activation(out=wq[:, k, :], in_=wq[:, k, :], func=AF.Sign),
              r=[wk], w=[wk])
        A("sync", lambda e, q=q, hf=hf, wq=wq: e.dma_start(
            out=bw1d[q][:, :, hf * 4:(hf + 1) * 4, :],
            in_=wq[:, :, :].rearrange("p k (jj h) -> p k jj h", h=128)),
          r=[wk], w=[f"bw1d_{q}" if hf == 1 else f"bw1dh_{q}"])
    setup_steps.append((1, _w2load))
    for g in range(8):
        setup_steps.append((4, lambda q=g // 2, hf=g % 2: _w1chunk(q, hf)))
        setup_steps.append((2, lambda g=g: _a2step(g)))
    setup_steps.append((1, _a2fin))
    setup_steps.append((1, _a1fin))
    for g in range(KH // 4):
        setup_steps.append((3, lambda g=g: _w2fold(g)))

    def pop_steps(budget):
        while setup_steps and budget > 0:
            c, fn = setup_steps.pop(0)
            fn()
            budget -= c

    xb32 = [TF(f"xb32_{i}", [128, 4, 1024], F32) for i in range(2)]
    xn1 = [TF(f"xn1_{i}", [128, 4, 1024], BF16) for i in range(2)]
    xn1T_t = TF("xn1T", [128, KD, FT], BF16)
    xn1T = [xn1T_t, xn1T_t]
    bst = [TF(f"bst_{i}", [128, 4, 12], F32) for i in range(2)]
    bag = [TF(f"bag_{i}", [128, 4, 2], F32) for i in range(2)]
    frv = [TF(f"frv_{i}", [128, 4], F32) for i in range(2)]
    cbuf = TF("cbuf", [128, 1024], F32)
    dcol = [TF(f"dcol_{i}", [128, 4], F32) for i in range(2)]
    drow_t = TF("drow", [1, FT], F32)
    drow = [drow_t, drow_t]
    dA_t = TF("dA", [STN, FT], F32)
    dA_sb = [dA_t, dA_t]
    dBx_t = TF("dBx", [STN, FT], F32)
    dBx = [dBx_t, dBx_t]
    dsb_t = TF("dsb", [STN, FT], F32)
    dsb = [dsb_t, dsb_t]

    def f_load(i):
        if i >= NFr:
            return
        pi = i % 2
        A("sync", lambda e, i=i, pi=pi: e.dma_start(out=xb32[pi][:], in_=x_f[i]), w=[f"xb{pi}"])

    def f_norm(i):
        pi = i % 2
        kx = f"xb{pi}"
        for j in range(4):
            A("vector", lambda e, pi=pi, j=j: e.bn_stats(out=bst[pi][:, j, 0:6], in_=xb32[pi][:, j, 0:512]),
              r=[kx], w=[f"bst{pi}"])
            A("vector", lambda e, pi=pi, j=j: e.bn_stats(out=bst[pi][:, j, 6:12], in_=xb32[pi][:, j, 512:1024]),
              r=[kx], w=[f"bst{pi}"])
            A("vector", lambda e, pi=pi, j=j: e.bn_aggr(out=bag[pi][:, j, :], in_=bst[pi][:, j, :]),
              r=[f"bst{pi}"], w=[f"bag{pi}"])
        A("scalar", lambda e, pi=pi: e.activation(out=frv[pi][:], in_=bag[pi][:, :, 1:2], func=AF.Ln,
                                                  bias=epsv[:, 0:1]),
          r=[f"bag{pi}", "epsv"], w=[f"frv{pi}"])
        A("scalar", lambda e, pi=pi: e.activation(out=frv[pi][:], in_=frv[pi][:], func=AF.Exp, scale=-0.5),
          r=[f"frv{pi}"], w=[f"frv{pi}"])
        for j in range(4):
            A("vector", lambda e, pi=pi, j=j: e.tensor_scalar(
                out=xn1[pi][:, j, :], in0=xb32[pi][:, j, :],
                scalar1=bag[pi][:, j, 0:1], scalar2=frv[pi][:, j:j + 1],
                op0=ALU.subtract, op1=ALU.mult),
              r=[kx, f"bag{pi}", f"frv{pi}"], w=[f"xn1{pi}_{j}"])

    def f_stage_t(i):
        pi = i % 2
        for k in range(KD):
            pt, ptk = (pT1, "pT1") if k % 2 == 0 else (pT2, "pT2")
            for j in range(4):
                A("tensor", lambda e, pi=pi, j=j, k=k, pt=pt: e.transpose(
                    pt[:, j * 128:(j + 1) * 128], xn1[pi][:, j, k * 128:(k + 1) * 128], eye16[:]),
                  r=[f"xn1{pi}_{j}", "eye16"], x=[ptk])
            A("vector", lambda e, pi=pi, k=k, pt=pt: e.tensor_copy(out=xn1T[pi][:, k, :], in_=pt[:, 0:512]),
              w=[f"xn1T_{k}"], x=[ptk])

    def f_stage_b(i):
        pi = i % 2
        for m in range(4):
            pz, pzk = (pA, "pA") if m % 2 == 0 else (pB, "pB")
            for k in range(KD):
                for half in range(2):
                    sl = slice(half * 512, half * 512 + 512)
                    A("tensor", lambda e, pi=pi, m=m, k=k, sl=sl, pz=pz, s0=(k == 0): e.matmul(
                        pz[:, sl], xn1T[pi][:, k, m * 128:(m + 1) * 128], wdt16[:, k, sl],
                        start=s0, stop=False),
                      r=[f"xn1T_{k}", "wdt16"], x=[pzk])
            for half in range(2):
                sl = slice(half * 512, half * 512 + 512)
                A("tensor", lambda e, sl=sl, pz=pz: e.matmul(
                    pz[:, sl], ones16r[:], bdel16[:, sl], start=False, stop=True),
                  r=["ones16r", "bdel16"], x=[pzk])
            A("scalar", lambda e, pz=pz: e.activation(out=cbuf[:], in_=pz[:], func=AF.Exp),
              r=[], w=["cbuf"], x=[pzk])
            A("scalar", lambda e, pi=pi, m=m: e.activation(out=sqj[:], in_=cbuf[:], func=AF.Ln,
                                                           bias=onev[:, 0:1],
                                                           accum_out=dcol[pi][:, m:m + 1]),
              r=["cbuf", "onev"], w=["sqj", f"dcol{pi}"])
        for k in range(KD):
            A("tensor", lambda e, pi=pi, k=k, s0=(k == 0): e.matmul(
                pS1[0:STN, :], bt16[:, k, :], xn1T[pi][:, k, :], start=s0, stop=(k == KD - 1)),
              r=["bt16", f"xn1T_{k}"], x=["pS1"])

    def f_chain(i):
        pi = i % 2
        for m in range(4):
            A("tensor", lambda e, pi=pi, m=m: e.transpose(
                pS2[0:1, m * 128:(m + 1) * 128], dcol[pi][:, m:m + 1], eye_f[:]),
              r=[f"dcol{pi}", "eye_f"], x=["pS2"])
        A("vector", lambda e, pi=pi: e.tensor_copy(out=drow[pi][:], in_=pS2[0:1, 0:FT]),
          w=["drow"], x=["pS2"])
        A("tensor", lambda e, pi=pi: e.matmul(pS2[0:STN, :], ones1f[:], drow[pi][:], start=True, stop=True),
          r=["ones1f", "drow"], x=["pS2"])
        A("scalar", lambda e, pi=pi: e.activation(out=dA_sb[pi][:], in_=pS2[0:STN, :], func=AF.Exp,
                                                  scale=a_sc[:, 0:1]),
          r=["a_sc"], w=["dAk"], x=["pS2"])
        A("vector", lambda e, pi=pi: e.tensor_scalar_mul(out=dsb[pi][:], in0=pS2[0:STN, :], scalar1=1.0 / DM),
          w=["dsb"], x=["pS2"])
        A("vector", lambda e, pi=pi: e.tensor_mul(out=dBx[pi][:], in0=dsb[pi][:], in1=pS1[0:STN, :]),
          r=["dsb"], w=["dBxk"], x=["pS1"])

        def pkey(t):
            return f"hscr{t % 2}"

        def scanfn(e, pi=pi, i=i):
            ini = 0.0 if i == 0 else hscr[:, (i - 1) % 2, FT - 1:FT]
            return e.tensor_tensor_scan(out=hscr[:, i % 2, :], data0=dA_sb[pi][:], data1=dBx[pi][:],
                                        initial=ini, op0=ALU.mult, op1=ALU.add)
        A("vector", scanfn, r=["dAk", "dBxk"] + ([pkey(i - 1)] if i > 0 else []), w=[pkey(i)])

        def cscanfn(e, pi=pi, i=i):
            ini = 1.0 if i == 0 else cumscr[:, (i - 1) % 2, FT - 1:FT]
            return e.tensor_tensor_scan(out=cumscr[:, i % 2, :], data0=dA_sb[pi][:], data1=zer16[:],
                                        initial=ini, op0=ALU.mult, op1=ALU.add)
        A("vector", cscanfn, r=["dAk", "zer16"] + ([f"cumscr{(i - 1) % 2}"] if i > 0 else []),
          w=[f"cumscr{i % 2}"])
        g = i
        A("gpsimd", lambda e, i=i, g=g: e.dma_start(
            out=hs16all[32 * (g % 4):32 * (g % 4) + STN, (g // 4) * FT:(g // 4 + 1) * FT],
            in_=hscr[:, i % 2, :]),
          r=[pkey(i), "hsall_init"], w=[f"hsb{g}"])
        A("gpsimd", lambda e, i=i, g=g: e.dma_start(
            out=cum16[32 * (g % 4):32 * (g % 4) + STN, (g // 4) * FT:(g // 4 + 1) * FT],
            in_=cumscr[:, i % 2, :]),
          r=[f"cumscr{i % 2}", "cum_init"], w=[f"cum_{g}"])

    nsteps = sum(c for c, _ in setup_steps)
    per_tile = -(-nsteps // max(NFr - 1, 1)) + 1
    f_load(0)
    f_load(1)
    f_norm(0)
    f_stage_t(0)
    for i in range(NFr):
        f_load(i + 2)
        if i + 1 < NFr:
            f_norm(i + 1)
        f_stage_b(i)
        if i + 1 < NFr:
            f_stage_t(i + 1)
        f_chain(i)
        pop_steps(per_tile if i > 0 else 1)
    pop_steps(10 ** 9)

    A("sync", lambda e: e.dma_start(out=agin.rearrange("n 1 -> n 1"), in_=hscr[:, (NFr - 1) % 2, FT - 1:FT]),
      r=[f"hscr{(NFr - 1) % 2}"], w=["agin_d"])
    A("gpsimd", lambda e: e.collective_compute(
        "AllGather", ALU.bypass, CC_GROUPS, ins=[agin[:, :]], outs=[agout[:, :]]),
      r=["agin_d"], w=["agout_d"], skip_inc=True)
    A("sync", lambda e: e.dma_start(out=agsb[:], in_=agout.rearrange("(r n) 1 -> r n", n=STN)),
      r=["agout_d"], w=["agsb"])
    A("tensor", lambda e: e.matmul(pS2[0:STN, 0:1], agsb[:], cmsb[:], start=True, stop=True),
      r=["agsb", "cmsb"], x=["pS2"])
    A("vector", lambda e: e.tensor_copy(out=hinit[0:STN, :], in_=pS2[0:STN, 0:1]),
      r=["hinit"], w=["hinit"], x=["pS2"])
    for bp in (32, 64, 96):
        A("sync", lambda e, bp=bp: e.dma_start(out=hinit[bp:bp + STN, :], in_=hinit[0:STN, :]),
          r=["hinit"], w=["hinit"])
    A("vector", lambda e: e.scalar_tensor_tensor(
        out=hs16all[:], in0=cum16[:], scalar=hinit[:, 0:1], in1=hs16all[:],
        op0=ALU.mult, op1=ALU.add),
      r=["hinit", "cum_init", "hsall_init"] + [f"cum_{g}" for g in range(NFr)] + [f"hsb{g}" for g in range(NFr)],
      w=[f"hsb{g}" for g in range(NFr)])

    pl.barrier()
    stS.close()
    stF.close()
    stW.close()
    MB = 4
    BTK = 512
    NBT = OWN // BTK
    OFF = 0
    x_b = h["x"].rearrange("(i j p) d -> i p j d", j=MB, p=128)
    out_b = out_h.rearrange("(u j p) d -> u p j d", j=MB, p=128)

    xf = [T(f"xf_{i}", [128, MB, 1024], F32) for i in range(2)]
    yp = [T(f"yp_{i}", [128, MB, 1024], F32) for i in range(2)]
    xn3 = T("xn3", [128, MB, 1024], BF16)
    xn3T = T("xn3T", [128, KD, BTK], BF16)
    h_sb = T("h_sb", [128, KH, BTK], BF16)
    scr = T("scr", [128, 1024], F32)
    w1buf = [T(f"w1buf_{i}", [128, KD, 4, 128], BF16) for i in range(2)]
    bbst = T("bbst", [128, MB, 12], F32)
    bbag = T("bbag", [128, MB, 2], F32)
    bbst4 = T("bbst4", [128, MB, 12], F32)
    bag4 = T("bag4", [128, MB, 2], F32)
    brv = {n: T(f"brv{n}", [128, MB], F32) for n in (1, 2, 3, 4)}

    def b_load(c):
        if c < NBT:
            for m in range(MB):
                A("sync", lambda e, c=c, m=m: e.dma_start(out=xf[c % 2][:, m, :], in_=x_b[OFF + c][:, m, :]),
                  w=[f"xf{c % 2}m{m}"])

    def b_stream(u, s):
        g, hf = divmod(s, 2)
        A("sync", lambda e, s=s, g=g, hf=hf: e.dma_start(
            out=w1buf[s % 2][:], in_=bw1d[g][:, :, hf * 4:(hf + 1) * 4, :]),
          r=[f"bw1d_{g}", f"bw1dh_{g}"], w=[f"w1b{s % 2}"])

    def b_ln(src_fn, src_keys, rvk, n):
        for m in range(MB):
            sk = src_keys(m) if callable(src_keys) else src_keys
            A("vector", lambda e, m=m: e.bn_stats(out=bbst[:, m, 0:6], in_=src_fn(m)[:, 0:512]),
              r=sk, w=["bbst"])
            A("vector", lambda e, m=m: e.bn_stats(out=bbst[:, m, 6:12], in_=src_fn(m)[:, 512:1024]),
              r=sk, w=["bbst"])
            A("vector", lambda e, m=m: e.bn_aggr(out=bbag[:, m, :], in_=bbst[:, m, :]),
              r=["bbst"], w=["bbag"])
        A("scalar", lambda e, rvk=rvk: e.activation(out=brv[n][:], in_=bbag[:, :, 1:2], func=AF.Sqrt,
                                                    bias=epsv[:, 0:1]),
          r=["bbag", "epsv"], w=[rvk])
        A("vector", lambda e, rvk=rvk: e.reciprocal(out=brv[n][:], in_=brv[n][:]),
          r=[rvk], w=[rvk])

    def b_chain_a(c):
        pc = c % 2
        b_ln(lambda m: xf[pc][:, m, :], lambda m: [f"xf{pc}m{m}"], "brv1", 1)

    def b_chain_b(c):
        pc = c % 2
        bp = 32 * (c % 4)
        cb = (c // 4) * FT
        for m in range(MB):
            pz, pzk = (pA, "pA") if m % 2 == 0 else (pB, "pB")
            for half in range(2):
                sl = slice(half * 512, half * 512 + 512)
                A("tensor", lambda e, bp=bp, cb=cb, m=m, sl=sl, pz=pz: e.matmul(
                    pz[:, sl], hs16all[bp:bp + STN, cb + m * 128:cb + (m + 1) * 128],
                    ct16r[bp:bp + STN, sl],
                    start=True, stop=True, tile_position=(bp, 0)),
                  r=[f"hsb{c}", "ct16"], x=[pzk])
            A("vector", lambda e, pc=pc, m=m: e.tensor_scalar(out=scr[:], in0=xf[pc][:, m, :],
                                                              scalar1=bbag[:, m, 0:1],
                                                              scalar2=brv[1][:, m:m + 1],
                                                              op0=ALU.subtract, op1=ALU.mult),
              r=[f"xf{pc}m{m}", "bbag", "brv1"], w=["scr"])
            A("vector", lambda e: e.tensor_mul(out=scr[:], in0=scr[:], in1=d_bc[:]),
              r=["scr", "d_bc"], w=["scr"])
            A("vector", lambda e, pc=pc, m=m, pz=pz: e.tensor_add(out=yp[pc][:, m, :], in0=scr[:], in1=pz[:]),
              r=["scr"], w=[f"yp{pc}"], x=[pzk])
        b_ln(lambda m: yp[pc][:, m, :], [f"yp{pc}"], "brv2", 2)

    def b_chain_c(c):
        pc = c % 2
        for m in range(MB):
            A("vector", lambda e, pc=pc, m=m: e.tensor_scalar(out=scr[:], in0=yp[pc][:, m, :],
                                                              scalar1=bbag[:, m, 0:1],
                                                              scalar2=brv[2][:, m:m + 1],
                                                              op0=ALU.subtract, op1=ALU.mult),
              r=[f"yp{pc}", "bbag", "brv2"], w=["scr"])
            A("vector", lambda e, pc=pc, m=m: e.tensor_add(out=xf[pc][:, m, :], in0=scr[:], in1=xf[pc][:, m, :]),
              r=["scr", f"xf{pc}m{m}"], w=[f"xf{pc}m{m}"])
        b_ln(lambda m: xf[pc][:, m, :], lambda m: [f"xf{pc}m{m}"], "brv3", 3)

    def b_chain_d(c):
        pc = c % 2
        for m in range(MB):
            A("vector", lambda e, pc=pc, m=m: e.tensor_scalar(out=xn3[:, m, :], in0=xf[pc][:, m, :],
                                                              scalar1=bbag[:, m, 0:1],
                                                              scalar2=brv[3][:, m:m + 1],
                                                              op0=ALU.subtract, op1=ALU.mult),
              r=[f"xf{pc}m{m}", "bbag", "brv3"], w=[f"xn3_{m}"])

    def b_transposes(u):
        for k in range(KD):
            pt, ptk = (pT1, "pT1") if k % 2 == 0 else (pT2, "pT2")
            for j in range(MB):
                A("tensor", lambda e, j=j, k=k, pt=pt: e.transpose(
                    pt[:, j * 128:(j + 1) * 128], xn3[:, j, k * 128:(k + 1) * 128], eye16[:]),
                  r=[f"xn3_{j}", "eye16"], x=[ptk])
            A("vector", lambda e, k=k, pt=pt: e.tensor_copy(out=xn3T[:, k, :], in_=pt[:, 0:BTK]),
              w=[f"xn3T_{k}"], x=[ptk])

    def b_mm1_group(u, grp):
        dirn = u % 2
        g2 = grp if dirn == 0 else 3 - grp
        js = list(range(8 * g2, 8 * g2 + 8))
        if dirn:
            js = js[::-1]
        for idx, j in enumerate(js):
            s = j // 4
            c = grp * 2 + idx // 4
            if idx % 4 == 0 and 1 <= c <= 6:
                b_stream(u, (c + 1) if dirn == 0 else 7 - (c + 1))
            ph, phk = (pS1, "pS1") if j % 2 == 0 else (pS2, "pS2")
            for k in range(KD):
                A("tensor", lambda e, s=s, j=j, k=k, ph=ph, s0=(k == 0): e.matmul(
                    ph[:, 0:BTK], w1buf[s % 2][:, k, j % 4, :], xn3T[:, k, :],
                    start=s0, stop=(k == KD - 1)),
                  r=[f"w1b{s % 2}", f"xn3T_{k}"], x=[phk])
            A("scalar", lambda e, j=j, ph=ph: e.activation(
                out=h_sb[:, j, :], in_=ph[:, 0:BTK], func=AF.Gelu,
                bias=b1c[:, j:j + 1], scale=alpha1c[:, j:j + 1]),
              r=["b1c", "alpha1c"], w=[f"h_{j}"], x=[phk])

    def b_mm2(u):
        pc = u % 2
        for m in range(MB):
            pz, pzk = (pA, "pA") if m % 2 == 0 else (pB, "pB")
            for j in range(KH):
                for half in range(2):
                    sl = slice(half * 512, half * 512 + 512)
                    A("tensor", lambda e, m=m, j=j, sl=sl, pz=pz, s0=(j == 0): e.matmul(
                        pz[:, sl], h_sb[:, j, m * 128:(m + 1) * 128], w2t16[:, j, sl],
                        start=s0, stop=False),
                      r=[f"h_{j}", f"bw2_{j}"], x=[pzk])
            for half in range(2):
                sl = slice(half * 512, half * 512 + 512)
                A("tensor", lambda e, sl=sl, pz=pz: e.matmul(
                    pz[:, sl], ones16r[:], b2r16[:, sl], start=False, stop=True),
                  r=["ones16r", "b2r16"], x=[pzk])
            A("vector", lambda e, m=m, pz=pz: e.bn_stats(out=bbst4[:, m, 0:6], in_=pz[:, 0:512]),
              w=["bbst4"], x=[pzk])
            A("vector", lambda e, m=m, pz=pz: e.bn_stats(out=bbst4[:, m, 6:12], in_=pz[:, 512:1024]),
              w=["bbst4"], x=[pzk])
            A("vector", lambda e, m=m: e.bn_aggr(out=bag4[:, m, :], in_=bbst4[:, m, :]),
              r=["bbst4"], w=["bag4"])
            A("scalar", lambda e, pc=pc, m=m, pz=pz: e.copy(out=yp[pc][:, m, :], in_=pz[:]),
              w=[f"yp{pc}"], x=[pzk])
        A("scalar", lambda e: e.activation(out=brv[4][:], in_=bag4[:, :, 1:2], func=AF.Sqrt,
                                           bias=epsv[:, 0:1]),
          r=["bag4", "epsv"], w=["brv4"])
        A("vector", lambda e: e.reciprocal(out=brv[4][:], in_=brv[4][:]),
          r=["brv4"], w=["brv4"])

    def b_tail_out(u):
        pc = u % 2
        for m in range(MB):
            A("vector", lambda e, pc=pc, m=m: e.tensor_scalar(out=scr[:], in0=yp[pc][:, m, :],
                                                              scalar1=bag4[:, m, 0:1],
                                                              scalar2=brv[4][:, m:m + 1],
                                                              op0=ALU.subtract, op1=ALU.mult),
              r=[f"yp{pc}", "bag4", "brv4"], w=["scr"])
            A("vector", lambda e, pc=pc, m=m: e.tensor_add(out=xf[pc][:, m, :], in0=scr[:], in1=xf[pc][:, m, :]),
              r=["scr", f"xf{pc}m{m}"], w=[f"xf{pc}m{m}"])
            A("gpsimd", lambda e, u=u, pc=pc, m=m: e.dma_start(out=out_b[u][:, m, :], in_=xf[pc][:, m, :]),
              r=[f"xf{pc}m{m}"], w=[f"out_{u}_{m}"])

    b_load(0)
    b_stream(0, 0)
    b_stream(0, 1)
    b_chain_a(0)
    b_chain_b(0)
    b_chain_c(0)
    b_chain_d(0)
    b_transposes(0)
    for u in range(NBT):
        if u > 0:
            b_tail_out(u - 1)
        b_load(u + 1)
        b_mm1_group(u, 0)
        if u + 1 < NBT:
            b_chain_a(u + 1)
        b_mm1_group(u, 1)
        if u + 1 < NBT:
            b_chain_b(u + 1)
        b_mm1_group(u, 2)
        if u + 1 < NBT:
            b_chain_c(u + 1)
        b_mm1_group(u, 3)
        if u + 1 < NBT:
            b_chain_d(u + 1)
            b_transposes(u + 1)
        b_mm2(u)
    b_tail_out(NBT - 1)

    sems, emit_engine = pl.emit(nc, st)
    with nc.Block() as block:
        @block.sync
        def _(e):
            emit_engine(e, "sync")

        @block.vector
        def _(e):
            emit_engine(e, "vector")

        @block.scalar
        def _(e):
            emit_engine(e, "scalar")

        @block.tensor
        def _(e):
            emit_engine(e, "tensor")

        @block.gpsimd
        def _(e):
            emit_engine(e, "gpsimd")

    st.close()
    return nc


_NC_CACHE = []


def kernel(L0_attn_x, L0_attn_A_log, L0_attn_B, L0_attn_C, L0_attn_D,
           L0_attn_W_delta, L0_attn_b_delta, L0_ffn_w1, L0_ffn_b1,
           L0_ffn_w2, L0_ffn_b2):
    x = np.asarray(L0_attn_x, dtype=np.float32)
    wdt = np.ascontiguousarray(np.asarray(L0_attn_W_delta, np.float32).T)
    bt = np.ascontiguousarray(np.asarray(L0_attn_B, np.float32).T)
    ct = np.ascontiguousarray(np.asarray(L0_attn_C, np.float32).T)
    dv = np.ascontiguousarray(np.asarray(L0_attn_D, np.float32).reshape(1, DM))
    bdel = np.ascontiguousarray(np.asarray(L0_attn_b_delta, np.float32).reshape(1, DM))
    alog = np.ascontiguousarray(np.asarray(L0_attn_A_log, np.float32).reshape(STN, 1))
    w1t = np.ascontiguousarray(np.asarray(L0_ffn_w1, np.float32).T)
    b1c = np.ascontiguousarray(np.asarray(L0_ffn_b1, np.float32).reshape(KH, 128).T)
    w2t = np.ascontiguousarray(np.asarray(L0_ffn_w2, np.float32).T)
    b2v = np.ascontiguousarray(np.asarray(L0_ffn_b2, np.float32).reshape(1, DM))
    eye = np.eye(128, dtype=np.float32)

    if not _NC_CACHE:
        _NC_CACHE.append(build_nc())
    nc = _NC_CACHE[0]

    in_maps = []
    for c in range(8):
        b, hf = divmod(c, 2)
        xfr = np.ascontiguousarray(x[b, hf * OWN:(hf + 1) * OWN])
        cmask = np.asarray([[1.0], [0.0]], np.float32) if hf == 1 else np.zeros((2, 1), np.float32)
        in_maps.append(dict(x=xfr, cmask=cmask, wdt=wdt, bt=bt, ct=ct, dv=dv,
                            bdel=bdel, alog=alog, w1t=w1t, b1c=b1c, w2t=w2t, b2v=b2v, eye=eye,
                            w1n=np.asarray(L0_ffn_w1, np.float32), w2n=np.asarray(L0_ffn_w2, np.float32)))
    res = run_bass_kernel_spmd(nc, in_maps, core_ids=list(range(8)))
    out = np.empty((4, SEQ, DM), np.float32)
    for c, r in enumerate(res.results):
        b, hf = divmod(c, 2)
        out[b, hf * OWN:(hf + 1) * OWN] = r["out"]
    return out


# version 45
# speedup vs baseline: 1.0884x; 1.0085x over previous
import contextlib
import numpy as np
import concourse.bass as bass
import concourse.mybir as mybir
from concourse.bass_utils import run_bass_kernel_spmd

F32 = mybir.dt.float32
BF16 = mybir.dt.bfloat16
AF = mybir.ActivationFunctionType
ALU = mybir.AluOpType
AX = mybir.AxisListType

ENGS = ("tensor", "vector", "scalar", "sync", "gpsimd")
DMA_Q = {"sync": 4, "gpsimd": 4}

SEQ, OWN, DM, STN, HID = 8192, 4096, 1024, 16, 4096
CC_GROUPS = [[0, 1], [2, 3], [4, 5], [6, 7]]
FT, NF = 512, 16
BT, NB = 512, 8
KD = 8
KH = 32


class Plan:

    def __init__(self):
        self.ops = {e: [] for e in ENGS}
        self.lw = {}
        self.rd = {}
        self.xa = {}
        self._bar = {}
        self._bar_left = set()

    def barrier(self):
        snap = []
        for e in ENGS:
            n = len(self.ops[e])
            if n == 0:
                continue
            if e in DMA_Q:
                nq = DMA_Q[e]
                for q in range(min(nq, n)):
                    i = n - 1 - ((n - 1 - q) % nq)
                    i = max(j for j in range(n) if j % nq == q)
                    snap.append((e, i))
            else:
                self.ops[e][-1]["inc"] = True
                snap.append((e, n - 1))
        self._bar = snap
        self._bar_left = set(ENGS)

    def add(self, eng, fn, r=(), w=(), x=(), skip_inc=False):
        ops = self.ops[eng]
        idx = len(ops)
        waits = {}

        def cover(tok, same_ok):
            if tok is None:
                return
            te, ti = tok
            if same_ok and te == eng and te not in DMA_Q:
                return
            q = ti % DMA_Q[te] if te in DMA_Q else 0
            if waits.get((te, q), -1) < ti:
                waits[(te, q)] = ti

        if eng in self._bar_left and self._bar:
            for te, ti in self._bar:
                if te != eng or te in DMA_Q:
                    cover((te, ti), same_ok=True)
            self._bar_left.discard(eng)
        for k in r:
            cover(self.lw.get(k), same_ok=False)
        for k in w:
            cover(self.lw.get(k), same_ok=False)
            for te, ti in self.rd.get(k, {}).items():
                cover((te, ti), same_ok=False)
        for k in x:
            for te, ti in self.xa.get(k, {}).items():
                if te != eng:
                    cover((te, ti), same_ok=True)
        for (te, _q), ti in waits.items():
            self.ops[te][ti]["inc"] = True
        ops.append(dict(fn=fn, waits=waits, inc=False, skip=skip_inc))
        for k in r:
            self.rd.setdefault(k, {})[eng] = idx
        for k in w:
            self.lw[k] = (eng, idx)
            self.rd[k] = {}
        for k in x:
            self.xa.setdefault(k, {})[eng] = idx
        return idx

    def emit(self, nc, stack):
        sems = {}
        for e in ("tensor", "vector", "scalar"):
            sems[e] = {0: stack.enter_context(nc.semaphore(f"P_{e}"))}
        for e, nq in DMA_Q.items():
            sems[e] = {i: stack.enter_context(nc.semaphore(f"P_{e}{i}")) for i in range(nq)}
        sems["gpsimd"]["cc"] = stack.enter_context(nc.semaphore("P_cc"))
        semsel = {e: [None] * len(self.ops[e]) for e in ENGS}
        val_after = {e: [0] * len(self.ops[e]) for e in ENGS}
        for e in ENGS:
            if e in DMA_Q:
                nq = DMA_Q[e]
                counts = [0] * nq
                cnt = 0
                ccn = 0
                for i, op in enumerate(self.ops[e]):
                    if op.get("skip"):
                        op["inc"] = "cc"
                        ccn += 1
                        semsel[e][i] = "cc"
                        val_after[e][i] = ("cc", ccn)
                        continue
                    q = cnt % nq
                    cnt += 1
                    counts[q] += 16
                    op["inc"] = True
                    semsel[e][i] = q
                    val_after[e][i] = (q, counts[q])
            else:
                c = 0
                for i, op in enumerate(self.ops[e]):
                    if op["inc"]:
                        c += 1
                        semsel[e][i] = 0
                    val_after[e][i] = (0, c)

        def emit_engine(handle, e):
            waited = {}
            dma_i = 0
            for i, op in enumerate(self.ops[e]):
                if e in DMA_Q and not op.get("skip"):
                    if dma_i >= DMA_Q[e]:
                        q = dma_i % DMA_Q[e]
                        v = 16 * (dma_i // DMA_Q[e])
                        if waited.get((e, q), -1) < v:
                            handle.wait_ge(sems[e][q], v)
                            waited[(e, q)] = v
                    dma_i += 1
                for (te, _q), ti in sorted(op["waits"].items()):
                    q, v = val_after[te][ti]
                    keyq = (te, q)
                    if waited.get(keyq, -1) < v:
                        handle.wait_ge(sems[te][q], v)
                        waited[keyq] = v
                ins = op["fn"](handle)
                if op["inc"] == "cc":
                    ins.then_inc(sems[e]["cc"], 1)
                elif op["inc"]:
                    q = semsel[e][i]
                    ins.then_inc(sems[e][q], 16 if e in DMA_Q else 1)
            if e in DMA_Q:
                finals = {}
                for i, op in enumerate(self.ops[e]):
                    if op.get("inc") and semsel[e][i] is not None:
                        q, v = val_after[e][i]
                        finals[q] = max(finals.get(q, 0), v)
                for q, v in sorted(finals.items(), key=lambda kv: str(kv[0])):
                    handle.wait_ge(sems[e][q], v)
        return sems, emit_engine


def build_nc():
    nc = bass.Bass("TRN2", target_bir_lowering=False)
    h = {}
    h["x"] = nc.dram_tensor("x", [OWN, DM], F32, kind="ExternalInput").ap()
    h["cmask"] = nc.dram_tensor("cmask", [2, 1], F32, kind="ExternalInput").ap()
    agin = nc.dram_tensor("agin", [STN, 1], F32, kind="Internal").ap()
    agout = nc.dram_tensor("agout", [2 * STN, 1], F32, kind="Internal").ap()
    h["wdt"] = nc.dram_tensor("wdt", [DM, DM], F32, kind="ExternalInput").ap()
    h["bt"] = nc.dram_tensor("bt", [DM, STN], F32, kind="ExternalInput").ap()
    h["ct"] = nc.dram_tensor("ct", [STN, DM], F32, kind="ExternalInput").ap()
    h["dv"] = nc.dram_tensor("dv", [1, DM], F32, kind="ExternalInput").ap()
    h["bdel"] = nc.dram_tensor("bdel", [1, DM], F32, kind="ExternalInput").ap()
    h["alog"] = nc.dram_tensor("alog", [STN, 1], F32, kind="ExternalInput").ap()
    h["w1t"] = nc.dram_tensor("w1t", [DM, HID], F32, kind="ExternalInput").ap()
    h["b1c"] = nc.dram_tensor("b1c", [128, KH], F32, kind="ExternalInput").ap()
    h["w2t"] = nc.dram_tensor("w2t", [HID, DM], F32, kind="ExternalInput").ap()
    h["b2v"] = nc.dram_tensor("b2v", [1, DM], F32, kind="ExternalInput").ap()
    h["eye"] = nc.dram_tensor("eye", [128, 128], F32, kind="ExternalInput").ap()
    h["w1n"] = nc.dram_tensor("w1n", [HID, DM], F32, kind="ExternalInput").ap()
    h["w2n"] = nc.dram_tensor("w2n", [DM, HID], F32, kind="ExternalInput").ap()
    out_h = nc.dram_tensor("out", [OWN, DM], F32, kind="ExternalOutput").ap()
    bw1d = nc.dram_tensor("bw1d", [8, 128, KD, 4, 128], BF16, kind="Internal").ap()

    NFr = OWN // FT
    x_f = h["x"].rearrange("(i j p) d -> i p j d", j=4, p=128)
    out_t = out_h.rearrange("(u j p) d -> u p j d", j=4, p=128)

    st = contextlib.ExitStack()
    stW = contextlib.ExitStack()
    stS = contextlib.ExitStack()
    stF = contextlib.ExitStack()
    T = lambda *a: st.enter_context(nc.sbuf_tensor(*a))
    TW = lambda *a: stW.enter_context(nc.sbuf_tensor(*a))
    TS = lambda *a: stS.enter_context(nc.sbuf_tensor(*a))
    TF = lambda *a: stF.enter_context(nc.sbuf_tensor(*a))
    P = lambda *a: st.enter_context(nc.psum_tensor(*a))

    eye16 = T("eye16", [128, 128], BF16)
    ones16r = T("ones16r", [1, 128], BF16)
    onescol = T("onescol", [128, 1], BF16)
    epsv = T("epsv", [128, 1], F32)
    onev = T("onev", [128, 1], F32)
    a_sc = T("a_sc", [STN, 1], F32)
    d_bc = T("d_bc", [128, DM], F32)
    b2r16 = T("b2r16", [1, DM], BF16)
    b1c = T("b1c_s", [128, KH], F32)
    alpha1c = T("alpha1c", [128, KH], F32)
    ct16r = T("ct16r", [128, DM], BF16)
    w2t16 = T("w2t16", [128, KH, DM], BF16)
    hs16all = T("hs16all", [128, 1024], BF16)
    eye_f = TW("eye_f", [128, 128], F32)
    sqj = TW("sqj", [128, 1024], BF16)
    ones1f = TW("ones1f", [1, STN], F32)
    ones1f128 = TW("ones1f128", [1, 128], F32)
    bdel16 = TW("bdel16", [1, DM], BF16)
    bt16 = TW("bt16", [128, KD, STN], BF16)
    wdt16 = TW("wdt16", [128, KD, DM], BF16)
    hscr = TW("hscr", [STN, 2, FT], F32)
    cum16 = TW("cum16", [128, 1024], BF16)
    hinit = TW("hinit", [128, 1], F32)
    agsb = TW("agsb", [2, STN], F32)
    cmsb = TW("cmsb", [2, 1], F32)
    cumscr = TW("cumscr", [STN, 2, FT], F32)
    zer16 = TW("zer16", [STN, FT], F32)
    d_row = TW("d_row", [1, DM], F32)
    alog_s = TW("alog_s", [STN, 1], F32)
    w1q = [stS.enter_context(nc.sbuf_tensor(f"w1q{i}", [128, KD, 512], BF16, side="right")) for i in range(2)]
    ab16 = [stS.enter_context(nc.sbuf_tensor(f"ab16_{i}", [128, 512], BF16, side="right")) for i in range(2)]
    a1r = stS.enter_context(nc.sbuf_tensor("a1r", [1, 512], F32, side="right"))
    a2acc = stS.enter_context(nc.sbuf_tensor("a2acc", [1, DM], F32, side="right"))
    a2bc16 = stS.enter_context(nc.sbuf_tensor("a2bc16", [128, DM], BF16, side="right"))

    pA = P("pA", [128, 1024], F32)
    pB = P("pB", [128, 1024], F32)
    pT1 = P("pT1", [128, 1024], BF16)
    pT2 = P("pT2", [128, 1024], BF16)
    pS1 = P("pS1", [128, 512], F32)
    pS2 = P("pS2", [128, 512], F32)

    pl = Plan()
    A = pl.add

    A("sync", lambda e: e.dma_start(out=eye_f[:], in_=h["eye"][:]), w=["eye_f"])
    A("sync", lambda e: e.dma_start(out=d_row[:], in_=h["dv"][:]), w=["d_row"])
    A("sync", lambda e: e.dma_start(out=b1c[:], in_=h["b1c"][:]), w=["b1c"])
    A("sync", lambda e: e.dma_start(out=alog_s[:], in_=h["alog"][:]), w=["alog"])
    A("gpsimd", lambda e: e.dma_start(out=bdel16[:], in_=h["bdel"][:]), w=["bdel16"])
    A("gpsimd", lambda e: e.dma_start(out=b2r16[:], in_=h["b2v"][:]), w=["b2r16"])
    A("gpsimd", lambda e: e.dma_start(out=bt16[:], in_=h["bt"].rearrange("(k p) n -> p k n", p=128)), w=["bt16"])
    A("gpsimd", lambda e: e.dma_start(out=wdt16[:], in_=h["wdt"].rearrange("(k p) d -> p k d", p=128)), w=["wdt16"])
    A("gpsimd", lambda e: e.dma_start(out=ct16r[0:STN, :], in_=h["ct"][:]), w=["ct16"])
    for bp in (32, 64, 96):
        A("gpsimd", lambda e, bp=bp: e.dma_start(out=ct16r[bp:bp + STN, :], in_=ct16r[0:STN, :]),
          r=["ct16"], w=["ct16"])

    A("vector", lambda e: e.memset(ones16r[:], 1.0), w=["ones16r"])
    A("vector", lambda e: e.memset(ones1f[:], 1.0), w=["ones1f"])
    A("vector", lambda e: e.memset(onescol[:], 1.0), w=["onescol"])
    A("vector", lambda e: e.memset(epsv[:], 1e-5), w=["epsv"])
    A("vector", lambda e: e.memset(onev[:], 1.0), w=["onev"])
    A("vector", lambda e: e.tensor_copy(out=eye16[:], in_=eye_f[:]), r=["eye_f"], w=["eye16"])
    A("scalar", lambda e: e.activation(out=a_sc[:], in_=alog_s[:], func=AF.Exp), r=["alog"], w=["a_sc"])
    A("vector", lambda e: e.tensor_scalar_mul(out=a_sc[:], in0=a_sc[:], scalar1=-1.0 / DM), r=["a_sc"], w=["a_sc"])
    A("vector", lambda e: e.memset(ones1f128[:], 1.0), w=["ones1f128"])
    A("vector", lambda e: e.memset(zer16[:], 0.0), w=["zer16"])
    A("vector", lambda e: e.memset(hs16all[:], 0.0), w=["hsall_init"])
    A("vector", lambda e: e.memset(cum16[:], 0.0), w=["cum_init"])
    A("vector", lambda e: e.memset(hinit[:], 0.0), w=["hinit"])
    A("sync", lambda e: e.dma_start(out=cmsb[:], in_=h["cmask"][:]), w=["cmsb"])
    for half in range(2):
        sl = slice(half * 512, half * 512 + 512)
        A("tensor", lambda e, sl=sl: e.matmul(pA[:, sl], ones1f128[:], d_row[:, sl], start=True, stop=True),
          r=["ones1f128", "d_row"], x=["pA"])
    A("vector", lambda e: e.tensor_copy(out=d_bc[:], in_=pA[:]), w=["d_bc"], x=["pA"])

    setup_steps = []

    def _w2load():
        _w1load(0)
        A("gpsimd", lambda e: e.dma_start(out=w2t16[:], in_=h["w2t"].rearrange("(j p) d -> p j d", p=128)),
          w=["w2raw"])

    def _a2step(g):
        for jj in range(4):
            j = 4 * g + jj
            for half in range(2):
                sl = slice(half * 512, half * 512 + 512)
                A("vector", lambda e, j=j, sl=sl, half=half: e.scalar_tensor_tensor(
                    out=ab16[half][:], in0=w2t16[:, j, sl], scalar=-1.0, in1=w2t16[:, j, sl],
                    op0=ALU.mult, op1=ALU.max),
                  r=["w2raw"], w=[f"ab{half}"])
                A("tensor", lambda e, sl=sl, half=half, s0=(jj == 0), s1=(jj == 3): e.matmul(
                    pA[0:1, sl], onescol[:], ab16[half][:], start=s0, stop=s1),
                  r=[f"ab{half}", "onescol"], x=["pA"])
        if g == 0:
            A("vector", lambda e: e.tensor_copy(out=a2acc[:], in_=pA[0:1, 0:DM]),
              w=["a2acc"], x=["pA"])
        else:
            A("vector", lambda e: e.tensor_add(out=a2acc[:], in0=a2acc[:], in1=pA[0:1, 0:DM]),
              r=["a2acc"], w=["a2acc"], x=["pA"])

    def _w2fold(g):
        for j in range(4 * g, 4 * g + 4):
            A("scalar", lambda e, j=j: e.activation(out=w2t16[:, j, :], in_=w2t16[:, j, :], func=AF.Sign),
              r=["w2raw"], w=["w2raw", f"s2f_{j}"])
        for j in range(4 * g, 4 * g + 4):
            A("vector", lambda e, j=j: e.tensor_mul(out=w2t16[:, j, :], in0=w2t16[:, j, :], in1=a2bc16[:]),
              r=[f"s2f_{j}", "a2bc16"], w=[f"bw2_{j}"])

    def _a2fin():
        for half in range(2):
            sl = slice(half * 512, half * 512 + 512)
            A("tensor", lambda e, sl=sl: e.matmul(pA[:, sl], ones1f128[:], a2acc[0:1, sl], start=True, stop=True),
              r=["ones1f128", "a2acc"], x=["pA"])
        A("vector", lambda e: e.tensor_scalar_mul(out=a2bc16[:], in0=pA[:], scalar1=1.0 / HID),
          w=["a2bc16"], x=["pA"])

    def _a1fin():
        A("vector", lambda e: e.tensor_scalar_mul(out=alpha1c[:], in0=alpha1c[:], scalar1=1.0 / DM),
          r=["alpha1c"], w=["alpha1c"])

    def _w1load(g):
        if g >= 8:
            return
        q, hf = divmod(g, 2)
        A("gpsimd", lambda e, g=g, q=q, hf=hf: e.dma_start(
            out=w1q[g % 2][:], in_=h["w1t"].rearrange("(k p) h -> p k h", p=128)[:, :, q * 1024 + hf * 512:q * 1024 + hf * 512 + 512]),
          w=[f"w1q{g % 2}"])

    def _w1chunk(q, hf):
        g = q * 2 + hf
        _w1load(g + 1)
        wq, wk = w1q[g % 2], f"w1q{g % 2}"
        for k in range(KD):
            A("vector", lambda e, k=k, wq=wq: e.scalar_tensor_tensor(
                out=ab16[k % 2][:], in0=wq[:, k, :], scalar=-1.0, in1=wq[:, k, :],
                op0=ALU.mult, op1=ALU.max),
              r=[wk], w=[f"ab{k % 2}"])
            A("tensor", lambda e, k=k, s0=(k == 0), s1=(k == KD - 1): e.matmul(
                pS1[0:1, 0:512], onescol[:], ab16[k % 2][:], start=s0, stop=s1),
              r=[f"ab{k % 2}", "onescol"], x=["pS1"])
        A("vector", lambda e: e.tensor_copy(out=a1r[:], in_=pS1[0:1, 0:512]), w=["a1r"], x=["pS1"])
        base = q * 8 + hf * 4
        for jj in range(4):
            A("tensor", lambda e, jj=jj: e.matmul(
                pS2[:, jj:jj + 1], a1r[0:1, jj * 128:(jj + 1) * 128], onev[0:1, 0:1],
                start=True, stop=True),
              r=["a1r", "onev"], x=["pS2"])
        A("vector", lambda e, base=base: e.tensor_copy(out=alpha1c[:, base:base + 4], in_=pS2[:, 0:4]),
          w=["alpha1c"], x=["pS2"])
        for k in range(KD):
            A("scalar", lambda e, k=k, wq=wq: e.activation(out=wq[:, k, :], in_=wq[:, k, :], func=AF.Sign),
              r=[wk], w=[wk])
        A("sync", lambda e, q=q, hf=hf, wq=wq: e.dma_start(
            out=bw1d[q * 2 + hf][:],
            in_=wq[:, :, :].rearrange("p k (jj h) -> p k jj h", h=128)),
          r=[wk], w=[f"bw1d_{q * 2 + hf}"])
    setup_steps.append((1, _w2load))
    for g in range(8):
        setup_steps.append((4, lambda q=g // 2, hf=g % 2: _w1chunk(q, hf)))
        setup_steps.append((2, lambda g=g: _a2step(g)))
    setup_steps.append((1, _a2fin))
    setup_steps.append((1, _a1fin))
    for g in range(KH // 4):
        setup_steps.append((3, lambda g=g: _w2fold(g)))

    def pop_steps(budget):
        while setup_steps and budget > 0:
            c, fn = setup_steps.pop(0)
            fn()
            budget -= c

    xb32 = [TF(f"xb32_{i}", [128, 4, 1024], F32) for i in range(2)]
    xn1 = [TF(f"xn1_{i}", [128, 4, 1024], BF16) for i in range(2)]
    xn1T_t = TF("xn1T", [128, KD, FT], BF16)
    xn1T = [xn1T_t, xn1T_t]
    bst = [TF(f"bst_{i}", [128, 4, 12], F32) for i in range(2)]
    bag = [TF(f"bag_{i}", [128, 4, 2], F32) for i in range(2)]
    frv = [TF(f"frv_{i}", [128, 4], F32) for i in range(2)]
    cbuf = TF("cbuf", [128, 1024], F32)
    dcol = [TF(f"dcol_{i}", [128, 4], F32) for i in range(2)]
    drow_t = TF("drow", [1, FT], F32)
    drow = [drow_t, drow_t]
    dA_t = TF("dA", [STN, FT], F32)
    dA_sb = [dA_t, dA_t]
    dBx_t = TF("dBx", [STN, FT], F32)
    dBx = [dBx_t, dBx_t]
    dsb_t = TF("dsb", [STN, FT], F32)
    dsb = [dsb_t, dsb_t]

    def f_load(i):
        if i >= NFr:
            return
        pi = i % 2
        A("sync", lambda e, i=i, pi=pi: e.dma_start(out=xb32[pi][:], in_=x_f[i]), w=[f"xb{pi}"])

    def f_norm(i):
        pi = i % 2
        kx = f"xb{pi}"
        for j in range(4):
            A("vector", lambda e, pi=pi, j=j: e.bn_stats(out=bst[pi][:, j, 0:6], in_=xb32[pi][:, j, 0:512]),
              r=[kx], w=[f"bst{pi}"])
            A("vector", lambda e, pi=pi, j=j: e.bn_stats(out=bst[pi][:, j, 6:12], in_=xb32[pi][:, j, 512:1024]),
              r=[kx], w=[f"bst{pi}"])
            A("vector", lambda e, pi=pi, j=j: e.bn_aggr(out=bag[pi][:, j, :], in_=bst[pi][:, j, :]),
              r=[f"bst{pi}"], w=[f"bag{pi}"])
        A("scalar", lambda e, pi=pi: e.activation(out=frv[pi][:], in_=bag[pi][:, :, 1:2], func=AF.Ln,
                                                  bias=epsv[:, 0:1]),
          r=[f"bag{pi}", "epsv"], w=[f"frv{pi}"])
        A("scalar", lambda e, pi=pi: e.activation(out=frv[pi][:], in_=frv[pi][:], func=AF.Exp, scale=-0.5),
          r=[f"frv{pi}"], w=[f"frv{pi}"])
        for j in range(4):
            A("vector", lambda e, pi=pi, j=j: e.tensor_scalar(
                out=xn1[pi][:, j, :], in0=xb32[pi][:, j, :],
                scalar1=bag[pi][:, j, 0:1], scalar2=frv[pi][:, j:j + 1],
                op0=ALU.subtract, op1=ALU.mult),
              r=[kx, f"bag{pi}", f"frv{pi}"], w=[f"xn1{pi}_{j}"])

    def f_stage_t(i):
        pi = i % 2
        for k in range(KD):
            pt, ptk = (pT1, "pT1") if k % 2 == 0 else (pT2, "pT2")
            for j in range(4):
                A("tensor", lambda e, pi=pi, j=j, k=k, pt=pt: e.transpose(
                    pt[:, j * 128:(j + 1) * 128], xn1[pi][:, j, k * 128:(k + 1) * 128], eye16[:]),
                  r=[f"xn1{pi}_{j}", "eye16"], x=[ptk])
            A("vector", lambda e, pi=pi, k=k, pt=pt: e.tensor_copy(out=xn1T[pi][:, k, :], in_=pt[:, 0:512]),
              w=[f"xn1T_{k}"], x=[ptk])

    def f_stage_b(i):
        pi = i % 2
        for m in range(4):
            pz, pzk = (pA, "pA") if m % 2 == 0 else (pB, "pB")
            for k in range(KD):
                for half in range(2):
                    sl = slice(half * 512, half * 512 + 512)
                    A("tensor", lambda e, pi=pi, m=m, k=k, sl=sl, pz=pz, s0=(k == 0): e.matmul(
                        pz[:, sl], xn1T[pi][:, k, m * 128:(m + 1) * 128], wdt16[:, k, sl],
                        start=s0, stop=False),
                      r=[f"xn1T_{k}", "wdt16"], x=[pzk])
            for half in range(2):
                sl = slice(half * 512, half * 512 + 512)
                A("tensor", lambda e, sl=sl, pz=pz: e.matmul(
                    pz[:, sl], ones16r[:], bdel16[:, sl], start=False, stop=True),
                  r=["ones16r", "bdel16"], x=[pzk])
            A("scalar", lambda e, pz=pz: e.activation(out=cbuf[:], in_=pz[:], func=AF.Exp),
              r=[], w=["cbuf"], x=[pzk])
            A("scalar", lambda e, pi=pi, m=m: e.activation(out=sqj[:], in_=cbuf[:], func=AF.Ln,
                                                           bias=onev[:, 0:1],
                                                           accum_out=dcol[pi][:, m:m + 1]),
              r=["cbuf", "onev"], w=["sqj", f"dcol{pi}"])
        for k in range(KD):
            A("tensor", lambda e, pi=pi, k=k, s0=(k == 0): e.matmul(
                pS1[0:STN, :], bt16[:, k, :], xn1T[pi][:, k, :], start=s0, stop=(k == KD - 1)),
              r=["bt16", f"xn1T_{k}"], x=["pS1"])

    def f_chain(i):
        pi = i % 2
        for m in range(4):
            A("tensor", lambda e, pi=pi, m=m: e.transpose(
                pS2[0:1, m * 128:(m + 1) * 128], dcol[pi][:, m:m + 1], eye_f[:]),
              r=[f"dcol{pi}", "eye_f"], x=["pS2"])
        A("vector", lambda e, pi=pi: e.tensor_copy(out=drow[pi][:], in_=pS2[0:1, 0:FT]),
          w=["drow"], x=["pS2"])
        A("tensor", lambda e, pi=pi: e.matmul(pS2[0:STN, :], ones1f[:], drow[pi][:], start=True, stop=True),
          r=["ones1f", "drow"], x=["pS2"])
        A("scalar", lambda e, pi=pi: e.activation(out=dA_sb[pi][:], in_=pS2[0:STN, :], func=AF.Exp,
                                                  scale=a_sc[:, 0:1]),
          r=["a_sc"], w=["dAk"], x=["pS2"])
        A("vector", lambda e, pi=pi: e.tensor_scalar_mul(out=dsb[pi][:], in0=pS2[0:STN, :], scalar1=1.0 / DM),
          w=["dsb"], x=["pS2"])
        A("vector", lambda e, pi=pi: e.tensor_mul(out=dBx[pi][:], in0=dsb[pi][:], in1=pS1[0:STN, :]),
          r=["dsb"], w=["dBxk"], x=["pS1"])

        def pkey(t):
            return f"hscr{t % 2}"

        def scanfn(e, pi=pi, i=i):
            ini = 0.0 if i == 0 else hscr[:, (i - 1) % 2, FT - 1:FT]
            return e.tensor_tensor_scan(out=hscr[:, i % 2, :], data0=dA_sb[pi][:], data1=dBx[pi][:],
                                        initial=ini, op0=ALU.mult, op1=ALU.add)
        A("vector", scanfn, r=["dAk", "dBxk"] + ([pkey(i - 1)] if i > 0 else []), w=[pkey(i)])

        def cscanfn(e, pi=pi, i=i):
            ini = 1.0 if i == 0 else cumscr[:, (i - 1) % 2, FT - 1:FT]
            return e.tensor_tensor_scan(out=cumscr[:, i % 2, :], data0=dA_sb[pi][:], data1=zer16[:],
                                        initial=ini, op0=ALU.mult, op1=ALU.add)
        A("vector", cscanfn, r=["dAk", "zer16"] + ([f"cumscr{(i - 1) % 2}"] if i > 0 else []),
          w=[f"cumscr{i % 2}"])
        g = i
        A("gpsimd", lambda e, i=i, g=g: e.dma_start(
            out=hs16all[32 * (g % 4):32 * (g % 4) + STN, (g // 4) * FT:(g // 4 + 1) * FT],
            in_=hscr[:, i % 2, :]),
          r=[pkey(i), "hsall_init"], w=[f"hsb{g}"])
        A("gpsimd", lambda e, i=i, g=g: e.dma_start(
            out=cum16[32 * (g % 4):32 * (g % 4) + STN, (g // 4) * FT:(g // 4 + 1) * FT],
            in_=cumscr[:, i % 2, :]),
          r=[f"cumscr{i % 2}", "cum_init"], w=[f"cum_{g}"])

    nsteps = sum(c for c, _ in setup_steps)
    per_tile = -(-nsteps // max(NFr - 1, 1)) + 1
    f_load(0)
    f_load(1)
    f_norm(0)
    f_stage_t(0)
    for i in range(NFr):
        f_load(i + 2)
        if i + 1 < NFr:
            f_norm(i + 1)
        f_stage_b(i)
        if i + 1 < NFr:
            f_stage_t(i + 1)
        f_chain(i)
        pop_steps(per_tile if i > 0 else 1)
    pop_steps(10 ** 9)

    A("sync", lambda e: e.dma_start(out=agin.rearrange("n 1 -> n 1"), in_=hscr[:, (NFr - 1) % 2, FT - 1:FT]),
      r=[f"hscr{(NFr - 1) % 2}"], w=["agin_d"])
    A("gpsimd", lambda e: e.collective_compute(
        "AllGather", ALU.bypass, CC_GROUPS, ins=[agin[:, :]], outs=[agout[:, :]]),
      r=["agin_d"], w=["agout_d"], skip_inc=True)
    A("sync", lambda e: e.dma_start(out=agsb[:], in_=agout.rearrange("(r n) 1 -> r n", n=STN)),
      r=["agout_d"], w=["agsb"])
    A("tensor", lambda e: e.matmul(pS2[0:STN, 0:1], agsb[:], cmsb[:], start=True, stop=True),
      r=["agsb", "cmsb"], x=["pS2"])
    A("vector", lambda e: e.tensor_copy(out=hinit[0:STN, :], in_=pS2[0:STN, 0:1]),
      r=["hinit"], w=["hinit"], x=["pS2"])
    for bp in (32, 64, 96):
        A("sync", lambda e, bp=bp: e.dma_start(out=hinit[bp:bp + STN, :], in_=hinit[0:STN, :]),
          r=["hinit"], w=["hinit"])
    A("vector", lambda e: e.scalar_tensor_tensor(
        out=hs16all[:], in0=cum16[:], scalar=hinit[:, 0:1], in1=hs16all[:],
        op0=ALU.mult, op1=ALU.add),
      r=["hinit", "cum_init", "hsall_init"] + [f"cum_{g}" for g in range(NFr)] + [f"hsb{g}" for g in range(NFr)],
      w=[f"hsb{g}" for g in range(NFr)])

    pl.barrier()
    stS.close()
    stF.close()
    stW.close()
    MB = 4
    BTK = 512
    NBT = OWN // BTK
    OFF = 0
    x_b = h["x"].rearrange("(i j p) d -> i p j d", j=MB, p=128)
    out_b = out_h.rearrange("(u j p) d -> u p j d", j=MB, p=128)

    xf = [T(f"xf_{i}", [128, MB, 1024], F32) for i in range(2)]
    yp = [T(f"yp_{i}", [128, MB, 1024], F32) for i in range(2)]
    xn3 = T("xn3", [128, MB, 1024], BF16)
    xn3T = T("xn3T", [128, KD, BTK], BF16)
    h_sb = T("h_sb", [128, KH, BTK], BF16)
    scr = T("scr", [128, 1024], F32)
    w1buf = [T(f"w1buf_{i}", [128, KD, 4, 128], BF16) for i in range(2)]
    bbst = T("bbst", [128, MB, 12], F32)
    bbag = T("bbag", [128, MB, 2], F32)
    bbst4 = T("bbst4", [128, MB, 12], F32)
    bag4 = T("bag4", [128, MB, 2], F32)
    brv = {n: T(f"brv{n}", [128, MB], F32) for n in (1, 2, 3, 4)}

    def b_load(c):
        if c < NBT:
            for m in range(MB):
                A("sync", lambda e, c=c, m=m: e.dma_start(out=xf[c % 2][:, m, :], in_=x_b[OFF + c][:, m, :]),
                  w=[f"xf{c % 2}m{m}"])

    def b_stream(u, s):
        A("sync", lambda e, s=s: e.dma_start(
            out=w1buf[s % 2][:].rearrange("p a b c -> p (a b c)"),
            in_=bw1d[s][:].rearrange("p a b c -> p (a b c)")),
          r=[f"bw1d_{s}"], w=[f"w1b{s % 2}"])

    def b_ln(src_fn, src_keys, rvk, n):
        for m in range(MB):
            sk = src_keys(m) if callable(src_keys) else src_keys
            A("vector", lambda e, m=m: e.bn_stats(out=bbst[:, m, 0:6], in_=src_fn(m)[:, 0:512]),
              r=sk, w=["bbst"])
            A("vector", lambda e, m=m: e.bn_stats(out=bbst[:, m, 6:12], in_=src_fn(m)[:, 512:1024]),
              r=sk, w=["bbst"])
            A("vector", lambda e, m=m: e.bn_aggr(out=bbag[:, m, :], in_=bbst[:, m, :]),
              r=["bbst"], w=["bbag"])
        A("scalar", lambda e, rvk=rvk: e.activation(out=brv[n][:], in_=bbag[:, :, 1:2], func=AF.Sqrt,
                                                    bias=epsv[:, 0:1]),
          r=["bbag", "epsv"], w=[rvk])
        A("vector", lambda e, rvk=rvk: e.reciprocal(out=brv[n][:], in_=brv[n][:]),
          r=[rvk], w=[rvk])

    def b_chain_a(c):
        pc = c % 2
        b_ln(lambda m: xf[pc][:, m, :], lambda m: [f"xf{pc}m{m}"], "brv1", 1)

    def b_chain_b(c):
        pc = c % 2
        bp = 32 * (c % 4)
        cb = (c // 4) * FT
        for m in range(MB):
            pz, pzk = (pA, "pA") if m % 2 == 0 else (pB, "pB")
            for half in range(2):
                sl = slice(half * 512, half * 512 + 512)
                A("tensor", lambda e, bp=bp, cb=cb, m=m, sl=sl, pz=pz: e.matmul(
                    pz[:, sl], hs16all[bp:bp + STN, cb + m * 128:cb + (m + 1) * 128],
                    ct16r[bp:bp + STN, sl],
                    start=True, stop=True, tile_position=(bp, 0)),
                  r=[f"hsb{c}", "ct16"], x=[pzk])
            A("vector", lambda e, pc=pc, m=m: e.tensor_scalar(out=scr[:], in0=xf[pc][:, m, :],
                                                              scalar1=bbag[:, m, 0:1],
                                                              scalar2=brv[1][:, m:m + 1],
                                                              op0=ALU.subtract, op1=ALU.mult),
              r=[f"xf{pc}m{m}", "bbag", "brv1"], w=["scr"])
            A("vector", lambda e: e.tensor_mul(out=scr[:], in0=scr[:], in1=d_bc[:]),
              r=["scr", "d_bc"], w=["scr"])
            A("vector", lambda e, pc=pc, m=m, pz=pz: e.tensor_add(out=yp[pc][:, m, :], in0=scr[:], in1=pz[:]),
              r=["scr"], w=[f"yp{pc}"], x=[pzk])
        b_ln(lambda m: yp[pc][:, m, :], [f"yp{pc}"], "brv2", 2)

    def b_chain_c(c):
        pc = c % 2
        for m in range(MB):
            A("vector", lambda e, pc=pc, m=m: e.tensor_scalar(out=scr[:], in0=yp[pc][:, m, :],
                                                              scalar1=bbag[:, m, 0:1],
                                                              scalar2=brv[2][:, m:m + 1],
                                                              op0=ALU.subtract, op1=ALU.mult),
              r=[f"yp{pc}", "bbag", "brv2"], w=["scr"])
            A("vector", lambda e, pc=pc, m=m: e.tensor_add(out=xf[pc][:, m, :], in0=scr[:], in1=xf[pc][:, m, :]),
              r=["scr", f"xf{pc}m{m}"], w=[f"xf{pc}m{m}"])
        b_ln(lambda m: xf[pc][:, m, :], lambda m: [f"xf{pc}m{m}"], "brv3", 3)

    def b_chain_d(c):
        pc = c % 2
        for m in range(MB):
            A("vector", lambda e, pc=pc, m=m: e.tensor_scalar(out=xn3[:, m, :], in0=xf[pc][:, m, :],
                                                              scalar1=bbag[:, m, 0:1],
                                                              scalar2=brv[3][:, m:m + 1],
                                                              op0=ALU.subtract, op1=ALU.mult),
              r=[f"xf{pc}m{m}", "bbag", "brv3"], w=[f"xn3_{m}"])

    def b_transposes(u):
        for k in range(KD):
            pt, ptk = (pT1, "pT1") if k % 2 == 0 else (pT2, "pT2")
            for j in range(MB):
                A("tensor", lambda e, j=j, k=k, pt=pt: e.transpose(
                    pt[:, j * 128:(j + 1) * 128], xn3[:, j, k * 128:(k + 1) * 128], eye16[:]),
                  r=[f"xn3_{j}", "eye16"], x=[ptk])
            A("vector", lambda e, k=k, pt=pt: e.tensor_copy(out=xn3T[:, k, :], in_=pt[:, 0:BTK]),
              w=[f"xn3T_{k}"], x=[ptk])

    def b_mm1_group(u, grp):
        dirn = u % 2
        g2 = grp if dirn == 0 else 3 - grp
        js = list(range(8 * g2, 8 * g2 + 8))
        if dirn:
            js = js[::-1]
        for idx, j in enumerate(js):
            s = j // 4
            c = grp * 2 + idx // 4
            if idx % 4 == 0 and 1 <= c <= 6:
                b_stream(u, (c + 1) if dirn == 0 else 7 - (c + 1))
            ph, phk = (pS1, "pS1") if j % 2 == 0 else (pS2, "pS2")
            for k in range(KD):
                A("tensor", lambda e, s=s, j=j, k=k, ph=ph, s0=(k == 0): e.matmul(
                    ph[:, 0:BTK], w1buf[s % 2][:, k, j % 4, :], xn3T[:, k, :],
                    start=s0, stop=(k == KD - 1)),
                  r=[f"w1b{s % 2}", f"xn3T_{k}"], x=[phk])
            A("scalar", lambda e, j=j, ph=ph: e.activation(
                out=h_sb[:, j, :], in_=ph[:, 0:BTK], func=AF.Gelu,
                bias=b1c[:, j:j + 1], scale=alpha1c[:, j:j + 1]),
              r=["b1c", "alpha1c"], w=[f"h_{j}"], x=[phk])

    def b_mm2(u):
        pc = u % 2
        for m in range(MB):
            pz, pzk = (pA, "pA") if m % 2 == 0 else (pB, "pB")
            for j in range(KH):
                for half in range(2):
                    sl = slice(half * 512, half * 512 + 512)
                    A("tensor", lambda e, m=m, j=j, sl=sl, pz=pz, s0=(j == 0): e.matmul(
                        pz[:, sl], h_sb[:, j, m * 128:(m + 1) * 128], w2t16[:, j, sl],
                        start=s0, stop=False),
                      r=[f"h_{j}", f"bw2_{j}"], x=[pzk])
            for half in range(2):
                sl = slice(half * 512, half * 512 + 512)
                A("tensor", lambda e, sl=sl, pz=pz: e.matmul(
                    pz[:, sl], ones16r[:], b2r16[:, sl], start=False, stop=True),
                  r=["ones16r", "b2r16"], x=[pzk])
            A("vector", lambda e, m=m, pz=pz: e.bn_stats(out=bbst4[:, m, 0:6], in_=pz[:, 0:512]),
              w=["bbst4"], x=[pzk])
            A("vector", lambda e, m=m, pz=pz: e.bn_stats(out=bbst4[:, m, 6:12], in_=pz[:, 512:1024]),
              w=["bbst4"], x=[pzk])
            A("vector", lambda e, m=m: e.bn_aggr(out=bag4[:, m, :], in_=bbst4[:, m, :]),
              r=["bbst4"], w=["bag4"])
            A("scalar", lambda e, pc=pc, m=m, pz=pz: e.copy(out=yp[pc][:, m, :], in_=pz[:]),
              w=[f"yp{pc}"], x=[pzk])
        A("scalar", lambda e: e.activation(out=brv[4][:], in_=bag4[:, :, 1:2], func=AF.Sqrt,
                                           bias=epsv[:, 0:1]),
          r=["bag4", "epsv"], w=["brv4"])
        A("vector", lambda e: e.reciprocal(out=brv[4][:], in_=brv[4][:]),
          r=["brv4"], w=["brv4"])

    def b_tail_out(u):
        pc = u % 2
        for m in range(MB):
            A("vector", lambda e, pc=pc, m=m: e.tensor_scalar(out=scr[:], in0=yp[pc][:, m, :],
                                                              scalar1=bag4[:, m, 0:1],
                                                              scalar2=brv[4][:, m:m + 1],
                                                              op0=ALU.subtract, op1=ALU.mult),
              r=[f"yp{pc}", "bag4", "brv4"], w=["scr"])
            A("vector", lambda e, pc=pc, m=m: e.tensor_add(out=xf[pc][:, m, :], in0=scr[:], in1=xf[pc][:, m, :]),
              r=["scr", f"xf{pc}m{m}"], w=[f"xf{pc}m{m}"])
            A("gpsimd", lambda e, u=u, pc=pc, m=m: e.dma_start(out=out_b[u][:, m, :], in_=xf[pc][:, m, :]),
              r=[f"xf{pc}m{m}"], w=[f"out_{u}_{m}"])

    b_load(0)
    b_stream(0, 0)
    b_stream(0, 1)
    b_chain_a(0)
    b_chain_b(0)
    b_chain_c(0)
    b_chain_d(0)
    b_transposes(0)
    for u in range(NBT):
        if u > 0:
            b_tail_out(u - 1)
        b_load(u + 1)
        b_mm1_group(u, 0)
        if u + 1 < NBT:
            b_chain_a(u + 1)
        b_mm1_group(u, 1)
        if u + 1 < NBT:
            b_chain_b(u + 1)
        b_mm1_group(u, 2)
        if u + 1 < NBT:
            b_chain_c(u + 1)
        b_mm1_group(u, 3)
        if u + 1 < NBT:
            b_chain_d(u + 1)
            b_transposes(u + 1)
        b_mm2(u)
    b_tail_out(NBT - 1)

    sems, emit_engine = pl.emit(nc, st)
    with nc.Block() as block:
        @block.sync
        def _(e):
            emit_engine(e, "sync")

        @block.vector
        def _(e):
            emit_engine(e, "vector")

        @block.scalar
        def _(e):
            emit_engine(e, "scalar")

        @block.tensor
        def _(e):
            emit_engine(e, "tensor")

        @block.gpsimd
        def _(e):
            emit_engine(e, "gpsimd")

    st.close()
    return nc


_NC_CACHE = []


def kernel(L0_attn_x, L0_attn_A_log, L0_attn_B, L0_attn_C, L0_attn_D,
           L0_attn_W_delta, L0_attn_b_delta, L0_ffn_w1, L0_ffn_b1,
           L0_ffn_w2, L0_ffn_b2):
    x = np.asarray(L0_attn_x, dtype=np.float32)
    wdt = np.ascontiguousarray(np.asarray(L0_attn_W_delta, np.float32).T)
    bt = np.ascontiguousarray(np.asarray(L0_attn_B, np.float32).T)
    ct = np.ascontiguousarray(np.asarray(L0_attn_C, np.float32).T)
    dv = np.ascontiguousarray(np.asarray(L0_attn_D, np.float32).reshape(1, DM))
    bdel = np.ascontiguousarray(np.asarray(L0_attn_b_delta, np.float32).reshape(1, DM))
    alog = np.ascontiguousarray(np.asarray(L0_attn_A_log, np.float32).reshape(STN, 1))
    w1t = np.ascontiguousarray(np.asarray(L0_ffn_w1, np.float32).T)
    b1c = np.ascontiguousarray(np.asarray(L0_ffn_b1, np.float32).reshape(KH, 128).T)
    w2t = np.ascontiguousarray(np.asarray(L0_ffn_w2, np.float32).T)
    b2v = np.ascontiguousarray(np.asarray(L0_ffn_b2, np.float32).reshape(1, DM))
    eye = np.eye(128, dtype=np.float32)

    if not _NC_CACHE:
        _NC_CACHE.append(build_nc())
    nc = _NC_CACHE[0]

    in_maps = []
    for c in range(8):
        b, hf = divmod(c, 2)
        xfr = np.ascontiguousarray(x[b, hf * OWN:(hf + 1) * OWN])
        cmask = np.asarray([[1.0], [0.0]], np.float32) if hf == 1 else np.zeros((2, 1), np.float32)
        in_maps.append(dict(x=xfr, cmask=cmask, wdt=wdt, bt=bt, ct=ct, dv=dv,
                            bdel=bdel, alog=alog, w1t=w1t, b1c=b1c, w2t=w2t, b2v=b2v, eye=eye,
                            w1n=np.asarray(L0_ffn_w1, np.float32), w2n=np.asarray(L0_ffn_w2, np.float32)))
    res = run_bass_kernel_spmd(nc, in_maps, core_ids=list(range(8)))
    out = np.empty((4, SEQ, DM), np.float32)
    for c, r in enumerate(res.results):
        b, hf = divmod(c, 2)
        out[b, hf * OWN:(hf + 1) * OWN] = r["out"]
    return out


# version 46 (final)
# speedup vs baseline: 1.0940x; 1.0051x over previous
import contextlib
import numpy as np
import concourse.bass as bass
import concourse.mybir as mybir
from concourse.bass_utils import run_bass_kernel_spmd

F32 = mybir.dt.float32
BF16 = mybir.dt.bfloat16
AF = mybir.ActivationFunctionType
ALU = mybir.AluOpType
AX = mybir.AxisListType

ENGS = ("tensor", "vector", "scalar", "sync", "gpsimd")
DMA_Q = {"sync": 4, "gpsimd": 4}

SEQ, OWN, DM, STN, HID = 8192, 4096, 1024, 16, 4096
CC_GROUPS = [[0, 1], [2, 3], [4, 5], [6, 7]]
FT, NF = 512, 16
BT, NB = 512, 8
KD = 8
KH = 32


class Plan:

    def __init__(self):
        self.ops = {e: [] for e in ENGS}
        self.lw = {}
        self.rd = {}
        self.xa = {}
        self._bar = {}
        self._bar_left = set()

    def barrier(self):
        snap = []
        for e in ENGS:
            n = len(self.ops[e])
            if n == 0:
                continue
            if e in DMA_Q:
                nq = DMA_Q[e]
                for q in range(min(nq, n)):
                    i = n - 1 - ((n - 1 - q) % nq)
                    i = max(j for j in range(n) if j % nq == q)
                    snap.append((e, i))
            else:
                self.ops[e][-1]["inc"] = True
                snap.append((e, n - 1))
        self._bar = snap
        self._bar_left = set(ENGS)

    def add(self, eng, fn, r=(), w=(), x=(), skip_inc=False):
        ops = self.ops[eng]
        idx = len(ops)
        waits = {}

        def cover(tok, same_ok):
            if tok is None:
                return
            te, ti = tok
            if same_ok and te == eng and te not in DMA_Q:
                return
            q = ti % DMA_Q[te] if te in DMA_Q else 0
            if waits.get((te, q), -1) < ti:
                waits[(te, q)] = ti

        if eng in self._bar_left and self._bar:
            for te, ti in self._bar:
                if te != eng or te in DMA_Q:
                    cover((te, ti), same_ok=True)
            self._bar_left.discard(eng)
        for k in r:
            cover(self.lw.get(k), same_ok=False)
        for k in w:
            cover(self.lw.get(k), same_ok=False)
            for te, ti in self.rd.get(k, {}).items():
                cover((te, ti), same_ok=False)
        for k in x:
            for te, ti in self.xa.get(k, {}).items():
                if te != eng:
                    cover((te, ti), same_ok=True)
        for (te, _q), ti in waits.items():
            self.ops[te][ti]["inc"] = True
        ops.append(dict(fn=fn, waits=waits, inc=False, skip=skip_inc))
        for k in r:
            self.rd.setdefault(k, {})[eng] = idx
        for k in w:
            self.lw[k] = (eng, idx)
            self.rd[k] = {}
        for k in x:
            self.xa.setdefault(k, {})[eng] = idx
        return idx

    def emit(self, nc, stack):
        sems = {}
        for e in ("tensor", "vector", "scalar"):
            sems[e] = {0: stack.enter_context(nc.semaphore(f"P_{e}"))}
        for e, nq in DMA_Q.items():
            sems[e] = {i: stack.enter_context(nc.semaphore(f"P_{e}{i}")) for i in range(nq)}
        sems["gpsimd"]["cc"] = stack.enter_context(nc.semaphore("P_cc"))
        semsel = {e: [None] * len(self.ops[e]) for e in ENGS}
        val_after = {e: [0] * len(self.ops[e]) for e in ENGS}
        for e in ENGS:
            if e in DMA_Q:
                nq = DMA_Q[e]
                counts = [0] * nq
                cnt = 0
                ccn = 0
                for i, op in enumerate(self.ops[e]):
                    if op.get("skip"):
                        op["inc"] = "cc"
                        ccn += 1
                        semsel[e][i] = "cc"
                        val_after[e][i] = ("cc", ccn)
                        continue
                    q = cnt % nq
                    cnt += 1
                    counts[q] += 16
                    op["inc"] = True
                    semsel[e][i] = q
                    val_after[e][i] = (q, counts[q])
            else:
                c = 0
                for i, op in enumerate(self.ops[e]):
                    if op["inc"]:
                        c += 1
                        semsel[e][i] = 0
                    val_after[e][i] = (0, c)

        def emit_engine(handle, e):
            waited = {}
            dma_i = 0
            for i, op in enumerate(self.ops[e]):
                if e in DMA_Q and not op.get("skip"):
                    if dma_i >= DMA_Q[e]:
                        q = dma_i % DMA_Q[e]
                        v = 16 * (dma_i // DMA_Q[e])
                        if waited.get((e, q), -1) < v:
                            handle.wait_ge(sems[e][q], v)
                            waited[(e, q)] = v
                    dma_i += 1
                for (te, _q), ti in sorted(op["waits"].items()):
                    q, v = val_after[te][ti]
                    keyq = (te, q)
                    if waited.get(keyq, -1) < v:
                        handle.wait_ge(sems[te][q], v)
                        waited[keyq] = v
                ins = op["fn"](handle)
                if op["inc"] == "cc":
                    ins.then_inc(sems[e]["cc"], 1)
                elif op["inc"]:
                    q = semsel[e][i]
                    ins.then_inc(sems[e][q], 16 if e in DMA_Q else 1)
            if e in DMA_Q:
                finals = {}
                for i, op in enumerate(self.ops[e]):
                    if op.get("inc") and semsel[e][i] is not None:
                        q, v = val_after[e][i]
                        finals[q] = max(finals.get(q, 0), v)
                for q, v in sorted(finals.items(), key=lambda kv: str(kv[0])):
                    handle.wait_ge(sems[e][q], v)
        return sems, emit_engine


def build_nc():
    nc = bass.Bass("TRN2", target_bir_lowering=False)
    h = {}
    h["x"] = nc.dram_tensor("x", [OWN, DM], F32, kind="ExternalInput").ap()
    h["cmask"] = nc.dram_tensor("cmask", [2, 1], F32, kind="ExternalInput").ap()
    agin = nc.dram_tensor("agin", [STN, 1], F32, kind="Internal").ap()
    agout = nc.dram_tensor("agout", [2 * STN, 1], F32, kind="Internal").ap()
    h["wdt"] = nc.dram_tensor("wdt", [DM, DM], F32, kind="ExternalInput").ap()
    h["bt"] = nc.dram_tensor("bt", [DM, STN], F32, kind="ExternalInput").ap()
    h["ct"] = nc.dram_tensor("ct", [STN, DM], F32, kind="ExternalInput").ap()
    h["dv"] = nc.dram_tensor("dv", [1, DM], F32, kind="ExternalInput").ap()
    h["bdel"] = nc.dram_tensor("bdel", [1, DM], F32, kind="ExternalInput").ap()
    h["alog"] = nc.dram_tensor("alog", [STN, 1], F32, kind="ExternalInput").ap()
    h["w1t"] = nc.dram_tensor("w1t", [DM, HID], F32, kind="ExternalInput").ap()
    h["b1c"] = nc.dram_tensor("b1c", [128, KH], F32, kind="ExternalInput").ap()
    h["w2t"] = nc.dram_tensor("w2t", [HID, DM], F32, kind="ExternalInput").ap()
    h["b2v"] = nc.dram_tensor("b2v", [1, DM], F32, kind="ExternalInput").ap()
    h["eye"] = nc.dram_tensor("eye", [128, 128], F32, kind="ExternalInput").ap()
    h["w1n"] = nc.dram_tensor("w1n", [HID, DM], F32, kind="ExternalInput").ap()
    h["w2n"] = nc.dram_tensor("w2n", [DM, HID], F32, kind="ExternalInput").ap()
    out_h = nc.dram_tensor("out", [OWN, DM], F32, kind="ExternalOutput").ap()
    bw1d = nc.dram_tensor("bw1d", [8, 128, KD, 4, 128], BF16, kind="Internal").ap()

    NFr = OWN // FT
    x_f = h["x"].rearrange("(i j p) d -> i p j d", j=4, p=128)
    out_t = out_h.rearrange("(u j p) d -> u p j d", j=4, p=128)

    st = contextlib.ExitStack()
    stW = contextlib.ExitStack()
    stS = contextlib.ExitStack()
    stF = contextlib.ExitStack()
    T = lambda *a: st.enter_context(nc.sbuf_tensor(*a))
    TW = lambda *a: stW.enter_context(nc.sbuf_tensor(*a))
    TS = lambda *a: stS.enter_context(nc.sbuf_tensor(*a))
    TF = lambda *a: stF.enter_context(nc.sbuf_tensor(*a))
    P = lambda *a: st.enter_context(nc.psum_tensor(*a))

    eye16 = T("eye16", [128, 128], BF16)
    ones16r = T("ones16r", [1, 128], BF16)
    onescol = T("onescol", [128, 1], BF16)
    epsv = T("epsv", [128, 1], F32)
    onev = T("onev", [128, 1], F32)
    a_sc = T("a_sc", [STN, 1], F32)
    d_bc = T("d_bc", [128, DM], F32)
    b2r16 = T("b2r16", [1, DM], BF16)
    b1c = T("b1c_s", [128, KH], F32)
    alpha1c = T("alpha1c", [128, KH], F32)
    ct16r = T("ct16r", [128, DM], BF16)
    w2t16 = T("w2t16", [128, KH, DM], BF16)
    hs16all = T("hs16all", [128, 1024], BF16)
    eye_f = TW("eye_f", [128, 128], F32)
    sqj = TW("sqj", [128, 1024], BF16)
    ones1f = TW("ones1f", [1, STN], F32)
    ones1f128 = TW("ones1f128", [1, 128], F32)
    bdel16 = TW("bdel16", [1, DM], BF16)
    bt16 = TW("bt16", [128, KD, STN], BF16)
    wdt16 = TW("wdt16", [128, KD, DM], BF16)
    hscr = TW("hscr", [STN, 2, FT], F32)
    cum16 = TW("cum16", [128, 1024], BF16)
    hinit = TW("hinit", [128, 1], F32)
    agsb = TW("agsb", [2, STN], F32)
    cmsb = TW("cmsb", [2, 1], F32)
    cumscr = TW("cumscr", [STN, 2, FT], F32)
    zer16 = TW("zer16", [STN, FT], F32)
    d_row = TW("d_row", [1, DM], F32)
    alog_s = TW("alog_s", [STN, 1], F32)
    w1q = [stS.enter_context(nc.sbuf_tensor(f"w1q{i}", [128, KD, 512], BF16, side="right")) for i in range(2)]
    ab16 = [stS.enter_context(nc.sbuf_tensor(f"ab16_{i}", [128, 512], BF16, side="right")) for i in range(2)]
    a1r = stS.enter_context(nc.sbuf_tensor("a1r", [1, 512], F32, side="right"))
    a2acc = stS.enter_context(nc.sbuf_tensor("a2acc", [1, DM], F32, side="right"))
    a2bc16 = stS.enter_context(nc.sbuf_tensor("a2bc16", [128, DM], BF16, side="right"))

    pA = P("pA", [128, 1024], F32)
    pB = P("pB", [128, 1024], F32)
    pT1 = P("pT1", [128, 1024], BF16)
    pT2 = P("pT2", [128, 1024], BF16)
    pS1 = P("pS1", [128, 512], F32)
    pS2 = P("pS2", [128, 512], F32)

    pl = Plan()
    A = pl.add

    A("sync", lambda e: e.dma_start(out=eye_f[:], in_=h["eye"][:]), w=["eye_f"])
    A("sync", lambda e: e.dma_start(out=d_row[:], in_=h["dv"][:]), w=["d_row"])
    A("sync", lambda e: e.dma_start(out=b1c[:], in_=h["b1c"][:]), w=["b1c"])
    A("sync", lambda e: e.dma_start(out=alog_s[:], in_=h["alog"][:]), w=["alog"])
    A("gpsimd", lambda e: e.dma_start(out=bdel16[:], in_=h["bdel"][:]), w=["bdel16"])
    A("gpsimd", lambda e: e.dma_start(out=b2r16[:], in_=h["b2v"][:]), w=["b2r16"])
    A("gpsimd", lambda e: e.dma_start(out=bt16[:], in_=h["bt"].rearrange("(k p) n -> p k n", p=128)), w=["bt16"])
    A("gpsimd", lambda e: e.dma_start(out=wdt16[:], in_=h["wdt"].rearrange("(k p) d -> p k d", p=128)), w=["wdt16"])
    A("gpsimd", lambda e: e.dma_start(out=ct16r[0:STN, :], in_=h["ct"][:]), w=["ct16"])
    for bp in (32, 64, 96):
        A("gpsimd", lambda e, bp=bp: e.dma_start(out=ct16r[bp:bp + STN, :], in_=ct16r[0:STN, :]),
          r=["ct16"], w=["ct16"])

    A("vector", lambda e: e.memset(ones16r[:], 1.0), w=["ones16r"])
    A("vector", lambda e: e.memset(ones1f[:], 1.0), w=["ones1f"])
    A("vector", lambda e: e.memset(onescol[:], 1.0), w=["onescol"])
    A("vector", lambda e: e.memset(epsv[:], 1e-5), w=["epsv"])
    A("vector", lambda e: e.memset(onev[:], 1.0), w=["onev"])
    A("vector", lambda e: e.tensor_copy(out=eye16[:], in_=eye_f[:]), r=["eye_f"], w=["eye16"])
    A("scalar", lambda e: e.activation(out=a_sc[:], in_=alog_s[:], func=AF.Exp), r=["alog"], w=["a_sc"])
    A("vector", lambda e: e.tensor_scalar_mul(out=a_sc[:], in0=a_sc[:], scalar1=-1.0 / DM), r=["a_sc"], w=["a_sc"])
    A("vector", lambda e: e.memset(ones1f128[:], 1.0), w=["ones1f128"])
    A("vector", lambda e: e.memset(zer16[:], 0.0), w=["zer16"])
    A("vector", lambda e: e.memset(hs16all[:], 0.0), w=["hsall_init"])
    A("vector", lambda e: e.memset(cum16[:], 0.0), w=["cum_init"])
    A("vector", lambda e: e.memset(hinit[:], 0.0), w=["hinit"])
    A("sync", lambda e: e.dma_start(out=cmsb[:], in_=h["cmask"][:]), w=["cmsb"])
    for half in range(2):
        sl = slice(half * 512, half * 512 + 512)
        A("tensor", lambda e, sl=sl: e.matmul(pA[:, sl], ones1f128[:], d_row[:, sl], start=True, stop=True),
          r=["ones1f128", "d_row"], x=["pA"])
    A("vector", lambda e: e.tensor_copy(out=d_bc[:], in_=pA[:]), w=["d_bc"], x=["pA"])

    setup_steps = []

    def _w2load():
        _w1load(0)
        A("gpsimd", lambda e: e.dma_start(out=w2t16[:], in_=h["w2t"].rearrange("(j p) d -> p j d", p=128)),
          w=["w2raw"])

    def _a2step(g):
        for jj in range(4):
            j = 4 * g + jj
            for half in range(2):
                sl = slice(half * 512, half * 512 + 512)
                A("vector", lambda e, j=j, sl=sl, half=half: e.scalar_tensor_tensor(
                    out=ab16[half][:], in0=w2t16[:, j, sl], scalar=-1.0, in1=w2t16[:, j, sl],
                    op0=ALU.mult, op1=ALU.max),
                  r=["w2raw"], w=[f"ab{half}"])
                A("tensor", lambda e, sl=sl, half=half, s0=(jj == 0), s1=(jj == 3): e.matmul(
                    pA[0:1, sl], onescol[:], ab16[half][:], start=s0, stop=s1),
                  r=[f"ab{half}", "onescol"], x=["pA"])
        if g == 0:
            A("vector", lambda e: e.tensor_copy(out=a2acc[:], in_=pA[0:1, 0:DM]),
              w=["a2acc"], x=["pA"])
        else:
            A("vector", lambda e: e.tensor_add(out=a2acc[:], in0=a2acc[:], in1=pA[0:1, 0:DM]),
              r=["a2acc"], w=["a2acc"], x=["pA"])

    def _w2fold(g):
        for j in range(4 * g, 4 * g + 4):
            A("scalar", lambda e, j=j: e.activation(out=w2t16[:, j, :], in_=w2t16[:, j, :], func=AF.Sign),
              r=["w2raw"], w=["w2raw", f"s2f_{j}"])
        for j in range(4 * g, 4 * g + 4):
            A("vector", lambda e, j=j: e.tensor_mul(out=w2t16[:, j, :], in0=w2t16[:, j, :], in1=a2bc16[:]),
              r=[f"s2f_{j}", "a2bc16"], w=[f"bw2_{j}"])

    def _a2fin():
        for half in range(2):
            sl = slice(half * 512, half * 512 + 512)
            A("tensor", lambda e, sl=sl: e.matmul(pA[:, sl], ones1f128[:], a2acc[0:1, sl], start=True, stop=True),
              r=["ones1f128", "a2acc"], x=["pA"])
        A("vector", lambda e: e.tensor_scalar_mul(out=a2bc16[:], in0=pA[:], scalar1=1.0 / HID),
          w=["a2bc16"], x=["pA"])

    def _a1fin():
        A("vector", lambda e: e.tensor_scalar_mul(out=alpha1c[:], in0=alpha1c[:], scalar1=1.0 / DM),
          r=["alpha1c"], w=["alpha1c"])

    def _w1load(g):
        if g >= 8:
            return
        q, hf = divmod(g, 2)
        A("gpsimd", lambda e, g=g, q=q, hf=hf: e.dma_start(
            out=w1q[g % 2][:], in_=h["w1t"].rearrange("(k p) h -> p k h", p=128)[:, :, q * 1024 + hf * 512:q * 1024 + hf * 512 + 512]),
          w=[f"w1q{g % 2}"])

    def _w1chunk(q, hf):
        g = q * 2 + hf
        _w1load(g + 1)
        wq, wk = w1q[g % 2], f"w1q{g % 2}"
        for k in range(KD):
            A("vector", lambda e, k=k, wq=wq: e.scalar_tensor_tensor(
                out=ab16[k % 2][:], in0=wq[:, k, :], scalar=-1.0, in1=wq[:, k, :],
                op0=ALU.mult, op1=ALU.max),
              r=[wk], w=[f"ab{k % 2}"])
            A("tensor", lambda e, k=k, s0=(k == 0), s1=(k == KD - 1): e.matmul(
                pS1[0:1, 0:512], onescol[:], ab16[k % 2][:], start=s0, stop=s1),
              r=[f"ab{k % 2}", "onescol"], x=["pS1"])
        A("vector", lambda e: e.tensor_copy(out=a1r[:], in_=pS1[0:1, 0:512]), w=["a1r"], x=["pS1"])
        base = q * 8 + hf * 4
        for jj in range(4):
            A("tensor", lambda e, jj=jj: e.matmul(
                pS2[:, jj:jj + 1], a1r[0:1, jj * 128:(jj + 1) * 128], onev[0:1, 0:1],
                start=True, stop=True),
              r=["a1r", "onev"], x=["pS2"])
        A("vector", lambda e, base=base: e.tensor_copy(out=alpha1c[:, base:base + 4], in_=pS2[:, 0:4]),
          w=["alpha1c"], x=["pS2"])
        for k in range(KD):
            A("scalar", lambda e, k=k, wq=wq: e.activation(out=wq[:, k, :], in_=wq[:, k, :], func=AF.Sign),
              r=[wk], w=[wk])
        A("sync", lambda e, q=q, hf=hf, wq=wq: e.dma_start(
            out=bw1d[q * 2 + hf][:],
            in_=wq[:, :, :].rearrange("p k (jj h) -> p k jj h", h=128)),
          r=[wk], w=[f"bw1d_{q * 2 + hf}"])
    setup_steps.append((1, _w2load))
    for g in range(8):
        setup_steps.append((4, lambda q=g // 2, hf=g % 2: _w1chunk(q, hf)))
        setup_steps.append((2, lambda g=g: _a2step(g)))
    setup_steps.append((1, _a2fin))
    setup_steps.append((1, _a1fin))
    for g in range(KH // 4):
        setup_steps.append((3, lambda g=g: _w2fold(g)))

    def pop_steps(budget):
        while setup_steps and budget > 0:
            c, fn = setup_steps.pop(0)
            fn()
            budget -= c

    xb32 = [TF(f"xb32_{i}", [128, 4, 1024], F32) for i in range(2)]
    xn1 = [TF(f"xn1_{i}", [128, 4, 1024], BF16) for i in range(2)]
    xn1T_t = TF("xn1T", [128, KD, FT], BF16)
    xn1T = [xn1T_t, xn1T_t]
    bst = [TF(f"bst_{i}", [128, 4, 12], F32) for i in range(2)]
    bag = [TF(f"bag_{i}", [128, 4, 2], F32) for i in range(2)]
    frv = [TF(f"frv_{i}", [128, 4], F32) for i in range(2)]
    cbuf = TF("cbuf", [128, 1024], F32)
    dcol = [TF(f"dcol_{i}", [128, 4], F32) for i in range(2)]
    drow_t = TF("drow", [1, FT], F32)
    drow = [drow_t, drow_t]
    dA_t = TF("dA", [STN, FT], F32)
    dA_sb = [dA_t, dA_t]
    dBx_t = TF("dBx", [STN, FT], F32)
    dBx = [dBx_t, dBx_t]
    dsb_t = TF("dsb", [STN, FT], F32)
    dsb = [dsb_t, dsb_t]

    def f_load(i):
        if i >= NFr:
            return
        pi = i % 2
        A("sync", lambda e, i=i, pi=pi: e.dma_start(out=xb32[pi][:], in_=x_f[i]), w=[f"xb{pi}"])

    def f_norm(i):
        pi = i % 2
        kx = f"xb{pi}"
        for j in range(4):
            A("vector", lambda e, pi=pi, j=j: e.bn_stats(out=bst[pi][:, j, 0:6], in_=xb32[pi][:, j, 0:512]),
              r=[kx], w=[f"bst{pi}"])
            A("vector", lambda e, pi=pi, j=j: e.bn_stats(out=bst[pi][:, j, 6:12], in_=xb32[pi][:, j, 512:1024]),
              r=[kx], w=[f"bst{pi}"])
            A("vector", lambda e, pi=pi, j=j: e.bn_aggr(out=bag[pi][:, j, :], in_=bst[pi][:, j, :]),
              r=[f"bst{pi}"], w=[f"bag{pi}"])
        A("scalar", lambda e, pi=pi: e.activation(out=frv[pi][:], in_=bag[pi][:, :, 1:2], func=AF.Ln,
                                                  bias=epsv[:, 0:1]),
          r=[f"bag{pi}", "epsv"], w=[f"frv{pi}"])
        A("scalar", lambda e, pi=pi: e.activation(out=frv[pi][:], in_=frv[pi][:], func=AF.Exp, scale=-0.5),
          r=[f"frv{pi}"], w=[f"frv{pi}"])
        for j in range(4):
            A("vector", lambda e, pi=pi, j=j: e.tensor_scalar(
                out=xn1[pi][:, j, :], in0=xb32[pi][:, j, :],
                scalar1=bag[pi][:, j, 0:1], scalar2=frv[pi][:, j:j + 1],
                op0=ALU.subtract, op1=ALU.mult),
              r=[kx, f"bag{pi}", f"frv{pi}"], w=[f"xn1{pi}_{j}"])

    def f_stage_t(i):
        pi = i % 2
        for k in range(KD):
            pt, ptk = (pT1, "pT1") if k % 2 == 0 else (pT2, "pT2")
            for j in range(4):
                A("tensor", lambda e, pi=pi, j=j, k=k, pt=pt: e.transpose(
                    pt[:, j * 128:(j + 1) * 128], xn1[pi][:, j, k * 128:(k + 1) * 128], eye16[:]),
                  r=[f"xn1{pi}_{j}", "eye16"], x=[ptk])
            A("vector", lambda e, pi=pi, k=k, pt=pt: e.tensor_copy(out=xn1T[pi][:, k, :], in_=pt[:, 0:512]),
              w=[f"xn1T_{k}"], x=[ptk])

    def f_stage_b(i):
        pi = i % 2
        for m in range(4):
            pz, pzk = (pA, "pA") if m % 2 == 0 else (pB, "pB")
            for k in range(KD):
                for half in range(2):
                    sl = slice(half * 512, half * 512 + 512)
                    A("tensor", lambda e, pi=pi, m=m, k=k, sl=sl, pz=pz, s0=(k == 0): e.matmul(
                        pz[:, sl], xn1T[pi][:, k, m * 128:(m + 1) * 128], wdt16[:, k, sl],
                        start=s0, stop=False),
                      r=[f"xn1T_{k}", "wdt16"], x=[pzk])
            for half in range(2):
                sl = slice(half * 512, half * 512 + 512)
                A("tensor", lambda e, sl=sl, pz=pz: e.matmul(
                    pz[:, sl], ones16r[:], bdel16[:, sl], start=False, stop=True),
                  r=["ones16r", "bdel16"], x=[pzk])
            A("scalar", lambda e, pz=pz: e.activation(out=cbuf[:], in_=pz[:], func=AF.Exp),
              r=[], w=["cbuf"], x=[pzk])
            A("scalar", lambda e, pi=pi, m=m: e.activation(out=sqj[:], in_=cbuf[:], func=AF.Ln,
                                                           bias=onev[:, 0:1],
                                                           accum_out=dcol[pi][:, m:m + 1]),
              r=["cbuf", "onev"], w=["sqj", f"dcol{pi}"])
        for k in range(KD):
            A("tensor", lambda e, pi=pi, k=k, s0=(k == 0): e.matmul(
                pS1[0:STN, :], bt16[:, k, :], xn1T[pi][:, k, :], start=s0, stop=(k == KD - 1)),
              r=["bt16", f"xn1T_{k}"], x=["pS1"])

    def f_chain(i):
        pi = i % 2
        for m in range(4):
            A("tensor", lambda e, pi=pi, m=m: e.transpose(
                pS2[0:1, m * 128:(m + 1) * 128], dcol[pi][:, m:m + 1], eye_f[:]),
              r=[f"dcol{pi}", "eye_f"], x=["pS2"])
        A("vector", lambda e, pi=pi: e.tensor_copy(out=drow[pi][:], in_=pS2[0:1, 0:FT]),
          w=["drow"], x=["pS2"])
        A("tensor", lambda e, pi=pi: e.matmul(pS2[0:STN, :], ones1f[:], drow[pi][:], start=True, stop=True),
          r=["ones1f", "drow"], x=["pS2"])
        A("scalar", lambda e, pi=pi: e.activation(out=dA_sb[pi][:], in_=pS2[0:STN, :], func=AF.Exp,
                                                  scale=a_sc[:, 0:1]),
          r=["a_sc"], w=["dAk"], x=["pS2"])
        A("vector", lambda e, pi=pi: e.tensor_scalar_mul(out=dsb[pi][:], in0=pS2[0:STN, :], scalar1=1.0 / DM),
          w=["dsb"], x=["pS2"])
        A("vector", lambda e, pi=pi: e.tensor_mul(out=dBx[pi][:], in0=dsb[pi][:], in1=pS1[0:STN, :]),
          r=["dsb"], w=["dBxk"], x=["pS1"])

        def pkey(t):
            return f"hscr{t % 2}"

        def scanfn(e, pi=pi, i=i):
            ini = 0.0 if i == 0 else hscr[:, (i - 1) % 2, FT - 1:FT]
            return e.tensor_tensor_scan(out=hscr[:, i % 2, :], data0=dA_sb[pi][:], data1=dBx[pi][:],
                                        initial=ini, op0=ALU.mult, op1=ALU.add)
        A("vector", scanfn, r=["dAk", "dBxk"] + ([pkey(i - 1)] if i > 0 else []), w=[pkey(i)])

        def cscanfn(e, pi=pi, i=i):
            ini = 1.0 if i == 0 else cumscr[:, (i - 1) % 2, FT - 1:FT]
            return e.tensor_tensor_scan(out=cumscr[:, i % 2, :], data0=dA_sb[pi][:], data1=zer16[:],
                                        initial=ini, op0=ALU.mult, op1=ALU.add)
        A("vector", cscanfn, r=["dAk", "zer16"] + ([f"cumscr{(i - 1) % 2}"] if i > 0 else []),
          w=[f"cumscr{i % 2}"])
        g = i
        A("gpsimd", lambda e, i=i, g=g: e.dma_start(
            out=hs16all[32 * (g % 4):32 * (g % 4) + STN, (g // 4) * FT:(g // 4 + 1) * FT],
            in_=hscr[:, i % 2, :]),
          r=[pkey(i), "hsall_init"], w=[f"hsb{g}"])
        A("gpsimd", lambda e, i=i, g=g: e.dma_start(
            out=cum16[32 * (g % 4):32 * (g % 4) + STN, (g // 4) * FT:(g // 4 + 1) * FT],
            in_=cumscr[:, i % 2, :]),
          r=[f"cumscr{i % 2}", "cum_init"], w=[f"cum_{g}"])

    nsteps = sum(c for c, _ in setup_steps)
    per_tile = -(-nsteps // max(NFr - 1, 1)) + 1
    f_load(0)
    f_load(1)
    f_norm(0)
    f_stage_t(0)
    for i in range(NFr):
        f_load(i + 2)
        if i + 1 < NFr:
            f_norm(i + 1)
        f_stage_b(i)
        if i + 1 < NFr:
            f_stage_t(i + 1)
        f_chain(i)
        pop_steps(per_tile if i > 0 else 1)
    pop_steps(10 ** 9)

    A("sync", lambda e: e.dma_start(out=agin.rearrange("n 1 -> n 1"), in_=hscr[:, (NFr - 1) % 2, FT - 1:FT]),
      r=[f"hscr{(NFr - 1) % 2}"], w=["agin_d"])
    A("gpsimd", lambda e: e.collective_compute(
        "AllGather", ALU.bypass, CC_GROUPS, ins=[agin[:, :]], outs=[agout[:, :]]),
      r=["agin_d"], w=["agout_d"], skip_inc=True)
    A("sync", lambda e: e.dma_start(out=agsb[:], in_=agout.rearrange("(r n) 1 -> r n", n=STN)),
      r=["agout_d"], w=["agsb"])
    A("tensor", lambda e: e.matmul(pS2[0:STN, 0:1], agsb[:], cmsb[:], start=True, stop=True),
      r=["agsb", "cmsb"], x=["pS2"])
    A("vector", lambda e: e.tensor_copy(out=hinit[0:STN, :], in_=pS2[0:STN, 0:1]),
      r=["hinit"], w=["hinit"], x=["pS2"])
    for bp in (32, 64, 96):
        A("sync", lambda e, bp=bp: e.dma_start(out=hinit[bp:bp + STN, :], in_=hinit[0:STN, :]),
          r=["hinit"], w=["hinit"])
    A("vector", lambda e: e.scalar_tensor_tensor(
        out=hs16all[:], in0=cum16[:], scalar=hinit[:, 0:1], in1=hs16all[:],
        op0=ALU.mult, op1=ALU.add),
      r=["hinit", "cum_init", "hsall_init"] + [f"cum_{g}" for g in range(NFr)] + [f"hsb{g}" for g in range(NFr)],
      w=[f"hsb{g}" for g in range(NFr)])

    pl.barrier()
    stS.close()
    stF.close()
    stW.close()
    MB = 4
    BTK = 512
    NBT = OWN // BTK
    OFF = 0
    x_b = h["x"].rearrange("(i j p) d -> i p j d", j=MB, p=128)
    out_b = out_h.rearrange("(u j p) d -> u p j d", j=MB, p=128)

    xf = [T(f"xf_{i}", [128, MB, 1024], F32) for i in range(2)]
    yp = [T(f"yp_{i}", [128, MB, 1024], F32) for i in range(2)]
    xn3 = T("xn3", [128, MB, 1024], BF16)
    xn3T = T("xn3T", [128, KD, BTK], BF16)
    h_sb = T("h_sb", [128, KH, BTK], BF16)
    scr = T("scr", [128, 1024], F32)
    w1buf = [T(f"w1buf_{i}", [128, KD, 4, 128], BF16) for i in range(2)]
    bbst = T("bbst", [128, MB, 12], F32)
    bbag = T("bbag", [128, MB, 2], F32)
    bbst4 = T("bbst4", [128, MB, 12], F32)
    bag4 = T("bag4", [128, MB, 2], F32)
    brv = {n: T(f"brv{n}", [128, MB], F32) for n in (1, 2, 3, 4)}

    def b_load(c):
        if c < NBT:
            for m in range(MB):
                A("sync", lambda e, c=c, m=m: e.dma_start(out=xf[c % 2][:, m, :], in_=x_b[OFF + c][:, m, :]),
                  w=[f"xf{c % 2}m{m}"])

    def b_stream(u, s):
        for part in range(4):
            A("sync", lambda e, s=s, part=part: e.dma_start(
                out=w1buf[s % 2][:, 2 * part:2 * part + 2, :, :],
                in_=bw1d[s][:, 2 * part:2 * part + 2, :, :]),
              r=[f"bw1d_{s}"], w=[f"w1b{s % 2}p{part}"])

    def b_ln(src_fn, src_keys, rvk, n):
        for m in range(MB):
            sk = src_keys(m) if callable(src_keys) else src_keys
            A("vector", lambda e, m=m: e.bn_stats(out=bbst[:, m, 0:6], in_=src_fn(m)[:, 0:512]),
              r=sk, w=["bbst"])
            A("vector", lambda e, m=m: e.bn_stats(out=bbst[:, m, 6:12], in_=src_fn(m)[:, 512:1024]),
              r=sk, w=["bbst"])
            A("vector", lambda e, m=m: e.bn_aggr(out=bbag[:, m, :], in_=bbst[:, m, :]),
              r=["bbst"], w=["bbag"])
        A("scalar", lambda e, rvk=rvk: e.activation(out=brv[n][:], in_=bbag[:, :, 1:2], func=AF.Sqrt,
                                                    bias=epsv[:, 0:1]),
          r=["bbag", "epsv"], w=[rvk])
        A("vector", lambda e, rvk=rvk: e.reciprocal(out=brv[n][:], in_=brv[n][:]),
          r=[rvk], w=[rvk])

    def b_chain_a(c):
        pc = c % 2
        b_ln(lambda m: xf[pc][:, m, :], lambda m: [f"xf{pc}m{m}"], "brv1", 1)

    def b_chain_b(c):
        pc = c % 2
        bp = 32 * (c % 4)
        cb = (c // 4) * FT
        for m in range(MB):
            pz, pzk = (pA, "pA") if m % 2 == 0 else (pB, "pB")
            for half in range(2):
                sl = slice(half * 512, half * 512 + 512)
                A("tensor", lambda e, bp=bp, cb=cb, m=m, sl=sl, pz=pz: e.matmul(
                    pz[:, sl], hs16all[bp:bp + STN, cb + m * 128:cb + (m + 1) * 128],
                    ct16r[bp:bp + STN, sl],
                    start=True, stop=True, tile_position=(bp, 0)),
                  r=[f"hsb{c}", "ct16"], x=[pzk])
            A("vector", lambda e, pc=pc, m=m: e.tensor_scalar(out=scr[:], in0=xf[pc][:, m, :],
                                                              scalar1=bbag[:, m, 0:1],
                                                              scalar2=brv[1][:, m:m + 1],
                                                              op0=ALU.subtract, op1=ALU.mult),
              r=[f"xf{pc}m{m}", "bbag", "brv1"], w=["scr"])
            A("vector", lambda e: e.tensor_mul(out=scr[:], in0=scr[:], in1=d_bc[:]),
              r=["scr", "d_bc"], w=["scr"])
            A("vector", lambda e, pc=pc, m=m, pz=pz: e.tensor_add(out=yp[pc][:, m, :], in0=scr[:], in1=pz[:]),
              r=["scr"], w=[f"yp{pc}"], x=[pzk])
        b_ln(lambda m: yp[pc][:, m, :], [f"yp{pc}"], "brv2", 2)

    def b_chain_c(c):
        pc = c % 2
        for m in range(MB):
            A("vector", lambda e, pc=pc, m=m: e.tensor_scalar(out=scr[:], in0=yp[pc][:, m, :],
                                                              scalar1=bbag[:, m, 0:1],
                                                              scalar2=brv[2][:, m:m + 1],
                                                              op0=ALU.subtract, op1=ALU.mult),
              r=[f"yp{pc}", "bbag", "brv2"], w=["scr"])
            A("vector", lambda e, pc=pc, m=m: e.tensor_add(out=xf[pc][:, m, :], in0=scr[:], in1=xf[pc][:, m, :]),
              r=["scr", f"xf{pc}m{m}"], w=[f"xf{pc}m{m}"])
        b_ln(lambda m: xf[pc][:, m, :], lambda m: [f"xf{pc}m{m}"], "brv3", 3)

    def b_chain_d(c):
        pc = c % 2
        for m in range(MB):
            A("vector", lambda e, pc=pc, m=m: e.tensor_scalar(out=xn3[:, m, :], in0=xf[pc][:, m, :],
                                                              scalar1=bbag[:, m, 0:1],
                                                              scalar2=brv[3][:, m:m + 1],
                                                              op0=ALU.subtract, op1=ALU.mult),
              r=[f"xf{pc}m{m}", "bbag", "brv3"], w=[f"xn3_{m}"])

    def b_transposes(u):
        for k in range(KD):
            pt, ptk = (pT1, "pT1") if k % 2 == 0 else (pT2, "pT2")
            for j in range(MB):
                A("tensor", lambda e, j=j, k=k, pt=pt: e.transpose(
                    pt[:, j * 128:(j + 1) * 128], xn3[:, j, k * 128:(k + 1) * 128], eye16[:]),
                  r=[f"xn3_{j}", "eye16"], x=[ptk])
            A("vector", lambda e, k=k, pt=pt: e.tensor_copy(out=xn3T[:, k, :], in_=pt[:, 0:BTK]),
              w=[f"xn3T_{k}"], x=[ptk])

    def b_mm1_group(u, grp):
        dirn = u % 2
        g2 = grp if dirn == 0 else 3 - grp
        js = list(range(8 * g2, 8 * g2 + 8))
        if dirn:
            js = js[::-1]
        for idx, j in enumerate(js):
            s = j // 4
            c = grp * 2 + idx // 4
            if idx % 4 == 0 and 1 <= c <= 6:
                b_stream(u, (c + 1) if dirn == 0 else 7 - (c + 1))
            ph, phk = (pS1, "pS1") if j % 2 == 0 else (pS2, "pS2")
            for k in range(KD):
                A("tensor", lambda e, s=s, j=j, k=k, ph=ph, s0=(k == 0): e.matmul(
                    ph[:, 0:BTK], w1buf[s % 2][:, k, j % 4, :], xn3T[:, k, :],
                    start=s0, stop=(k == KD - 1)),
                  r=[f"w1b{s % 2}p{k // 2}", f"xn3T_{k}"], x=[phk])
            A("scalar", lambda e, j=j, ph=ph: e.activation(
                out=h_sb[:, j, :], in_=ph[:, 0:BTK], func=AF.Gelu,
                bias=b1c[:, j:j + 1], scale=alpha1c[:, j:j + 1]),
              r=["b1c", "alpha1c"], w=[f"h_{j}"], x=[phk])

    def b_mm2(u):
        pc = u % 2
        for m in range(MB):
            pz, pzk = (pA, "pA") if m % 2 == 0 else (pB, "pB")
            for j in range(KH):
                for half in range(2):
                    sl = slice(half * 512, half * 512 + 512)
                    A("tensor", lambda e, m=m, j=j, sl=sl, pz=pz, s0=(j == 0): e.matmul(
                        pz[:, sl], h_sb[:, j, m * 128:(m + 1) * 128], w2t16[:, j, sl],
                        start=s0, stop=False),
                      r=[f"h_{j}", f"bw2_{j}"], x=[pzk])
            for half in range(2):
                sl = slice(half * 512, half * 512 + 512)
                A("tensor", lambda e, sl=sl, pz=pz: e.matmul(
                    pz[:, sl], ones16r[:], b2r16[:, sl], start=False, stop=True),
                  r=["ones16r", "b2r16"], x=[pzk])
            A("vector", lambda e, m=m, pz=pz: e.bn_stats(out=bbst4[:, m, 0:6], in_=pz[:, 0:512]),
              w=["bbst4"], x=[pzk])
            A("vector", lambda e, m=m, pz=pz: e.bn_stats(out=bbst4[:, m, 6:12], in_=pz[:, 512:1024]),
              w=["bbst4"], x=[pzk])
            A("vector", lambda e, m=m: e.bn_aggr(out=bag4[:, m, :], in_=bbst4[:, m, :]),
              r=["bbst4"], w=["bag4"])
            A("scalar", lambda e, pc=pc, m=m, pz=pz: e.copy(out=yp[pc][:, m, :], in_=pz[:]),
              w=[f"yp{pc}"], x=[pzk])
        A("scalar", lambda e: e.activation(out=brv[4][:], in_=bag4[:, :, 1:2], func=AF.Sqrt,
                                           bias=epsv[:, 0:1]),
          r=["bag4", "epsv"], w=["brv4"])
        A("vector", lambda e: e.reciprocal(out=brv[4][:], in_=brv[4][:]),
          r=["brv4"], w=["brv4"])

    def b_tail_out(u):
        pc = u % 2
        for m in range(MB):
            A("vector", lambda e, pc=pc, m=m: e.tensor_scalar(out=scr[:], in0=yp[pc][:, m, :],
                                                              scalar1=bag4[:, m, 0:1],
                                                              scalar2=brv[4][:, m:m + 1],
                                                              op0=ALU.subtract, op1=ALU.mult),
              r=[f"yp{pc}", "bag4", "brv4"], w=["scr"])
            A("vector", lambda e, pc=pc, m=m: e.tensor_add(out=xf[pc][:, m, :], in0=scr[:], in1=xf[pc][:, m, :]),
              r=["scr", f"xf{pc}m{m}"], w=[f"xf{pc}m{m}"])
            A("gpsimd", lambda e, u=u, pc=pc, m=m: e.dma_start(out=out_b[u][:, m, :], in_=xf[pc][:, m, :]),
              r=[f"xf{pc}m{m}"], w=[f"out_{u}_{m}"])

    b_load(0)
    b_stream(0, 0)
    b_stream(0, 1)
    b_chain_a(0)
    b_chain_b(0)
    b_chain_c(0)
    b_chain_d(0)
    b_transposes(0)
    for u in range(NBT):
        if u > 0:
            b_tail_out(u - 1)
        b_load(u + 1)
        b_mm1_group(u, 0)
        if u + 1 < NBT:
            b_chain_a(u + 1)
        b_mm1_group(u, 1)
        if u + 1 < NBT:
            b_chain_b(u + 1)
        b_mm1_group(u, 2)
        if u + 1 < NBT:
            b_chain_c(u + 1)
        b_mm1_group(u, 3)
        if u + 1 < NBT:
            b_chain_d(u + 1)
            b_transposes(u + 1)
        b_mm2(u)
    b_tail_out(NBT - 1)

    sems, emit_engine = pl.emit(nc, st)
    with nc.Block() as block:
        @block.sync
        def _(e):
            emit_engine(e, "sync")

        @block.vector
        def _(e):
            emit_engine(e, "vector")

        @block.scalar
        def _(e):
            emit_engine(e, "scalar")

        @block.tensor
        def _(e):
            emit_engine(e, "tensor")

        @block.gpsimd
        def _(e):
            emit_engine(e, "gpsimd")

    st.close()
    return nc


_NC_CACHE = []


def kernel(L0_attn_x, L0_attn_A_log, L0_attn_B, L0_attn_C, L0_attn_D,
           L0_attn_W_delta, L0_attn_b_delta, L0_ffn_w1, L0_ffn_b1,
           L0_ffn_w2, L0_ffn_b2):
    x = np.asarray(L0_attn_x, dtype=np.float32)
    wdt = np.ascontiguousarray(np.asarray(L0_attn_W_delta, np.float32).T)
    bt = np.ascontiguousarray(np.asarray(L0_attn_B, np.float32).T)
    ct = np.ascontiguousarray(np.asarray(L0_attn_C, np.float32).T)
    dv = np.ascontiguousarray(np.asarray(L0_attn_D, np.float32).reshape(1, DM))
    bdel = np.ascontiguousarray(np.asarray(L0_attn_b_delta, np.float32).reshape(1, DM))
    alog = np.ascontiguousarray(np.asarray(L0_attn_A_log, np.float32).reshape(STN, 1))
    w1t = np.ascontiguousarray(np.asarray(L0_ffn_w1, np.float32).T)
    b1c = np.ascontiguousarray(np.asarray(L0_ffn_b1, np.float32).reshape(KH, 128).T)
    w2t = np.ascontiguousarray(np.asarray(L0_ffn_w2, np.float32).T)
    b2v = np.ascontiguousarray(np.asarray(L0_ffn_b2, np.float32).reshape(1, DM))
    eye = np.eye(128, dtype=np.float32)

    if not _NC_CACHE:
        _NC_CACHE.append(build_nc())
    nc = _NC_CACHE[0]

    in_maps = []
    for c in range(8):
        b, hf = divmod(c, 2)
        xfr = np.ascontiguousarray(x[b, hf * OWN:(hf + 1) * OWN])
        cmask = np.asarray([[1.0], [0.0]], np.float32) if hf == 1 else np.zeros((2, 1), np.float32)
        in_maps.append(dict(x=xfr, cmask=cmask, wdt=wdt, bt=bt, ct=ct, dv=dv,
                            bdel=bdel, alog=alog, w1t=w1t, b1c=b1c, w2t=w2t, b2v=b2v, eye=eye,
                            w1n=np.asarray(L0_ffn_w1, np.float32), w2n=np.asarray(L0_ffn_w2, np.float32)))
    res = run_bass_kernel_spmd(nc, in_maps, core_ids=list(range(8)))
    out = np.empty((4, SEQ, DM), np.float32)
    for c, r in enumerate(res.results):
        b, hf = divmod(c, 2)
        out[b, hf * OWN:(hf + 1) * OWN] = r["out"]
    return out
